# version 16
# baseline (speedup 1.0000x reference)
import numpy as np
import ml_dtypes
import concourse.bass as bass
import concourse.mybir as mybir
from concourse.bass_utils import run_bass_kernel_spmd
from concourse.bass_core_rust import IndirectOffsetOnAxis

F32 = mybir.dt.float32
BF16 = mybir.dt.bfloat16
I32 = mybir.dt.int32
AF = mybir.ActivationFunctionType
ALU = mybir.AluOpType
AXX = mybir.AxisListType.X

NC = 8
B = 2
S = 8192
D = 1024
SEG = 512
NSEG = 16
TOK = 2048
DCH = 8
WARM = 64


def core_insts(i):
    return [(0, i), (0, 15 - i), (1, i), (1, 15 - i)]


def owner_of(b, sg):
    i = min(sg, 15 - sg)
    slot = (0 if sg <= 7 else 1) + 2 * b
    return i, slot


def seg_loc(b, sg):
    i, slot = owner_of(b, sg)
    return i, slot * SEG


ENGS = ("t", "v", "s", "g", "y")


class Ctx:

    def __init__(self, nc, block):
        self.nc = nc
        self.block = block
        self.q = {e: [] for e in ENGS}
        self.sem = {}
        self.n = {}
        for name in ("t", "v", "s", "g", "y"):
            self._mksem(name)
        self.dstreams = []
        self.ncc = 0
        self.pb = [nc.psum_tensor(f"pb{k}", [128, 512], F32).__enter__()
                   for k in range(8)]
        self.pguard = [("v", 0)] * 8

    def _mksem(self, name):
        h = self.nc.semaphore(name).__enter__()
        self.sem[name] = h
        self.n[name] = 0

    def ccsem(self):
        name = f"cc{self.ncc}"
        self.ncc += 1
        self._mksem(name)
        return name

    def collective(self, kind, op, in_t, out_t, waits=()):
        ccs = self.ccsem()
        w = tuple(waits)
        if getattr(self, "last_cc", None) is not None:
            w = w + (self.last_cc,)
        self.op("g", P(lambda g, i=in_t, o=out_t, k=kind, a=op:
                       g.collective_compute(k, a,
                                            replica_groups=[list(range(NC))],
                                            ins=[i.ap().opt()],
                                            outs=[o.ap().opt()])),
                w, sem=ccs, by=1)
        self.last_cc = (ccs, 1)
        return (ccs, 1)

    def op(self, eng, fn, waits=(), sem=None, by=1):
        if sem is None:
            sem = eng
        self.n[sem] += by
        self.q[eng].append((tuple(waits), fn, sem, by))
        return self.n[sem]

    def dma(self, eng, fn, waits=(), stream="misc"):
        nm = "sd_" + stream
        if nm not in self.sem:
            self._mksem(nm)
            self.dstreams.append(nm)
        cnt = self.op(eng, fn, waits, sem=nm, by=16)
        return (nm, cnt)

    def dma_all(self):
        return tuple((nm, self.n[nm]) for nm in self.dstreams
                     if self.n[nm] > 0)

    def now(self, name):
        return self.n[name]

    def allv(self):
        return ("v", self.n["v"])

    def alls(self):
        return ("s", self.n["s"])

    def allt(self):
        return ("t", self.n["t"])

    def finalize(self):
        secmap = {"t": self.block.tensor, "v": self.block.vector,
                  "s": self.block.scalar, "g": self.block.gpsimd,
                  "y": self.block.sync}

        for eng in ENGS:
            ops = self.q[eng]
            if not ops:
                continue

            def run(e, ops=ops):
                last = {}
                for waits, fn, semname, by in ops:
                    for snm, val in waits:
                        if last.get(snm, -1) < val:
                            e.wait_ge(self.sem[snm], val)
                            last[snm] = val
                    fn(e).then_inc(self.sem[semname], by)
            secmap[eng](run)


def P(fn, **kw):
    return lambda e, fn=fn, kw=dict(kw): fn(e, **kw)


class Scope:

    def __init__(self, nc):
        self.nc = nc
        self.cms = []

    def t(self, name, shape, dt):
        cm = self.nc.sbuf_tensor(name, shape, dt)
        h = cm.__enter__()
        self.cms.append(cm)
        return h

    def close(self):
        for cm in reversed(self.cms):
            cm.__exit__(None, None, None)
        self.cms = []


def fence(c):
    tgts = [(nm, c.n[nm]) for nm in ("t", "v", "s", "g", "y") if c.n[nm] > 0]
    tgts += list(c.dma_all())
    for eng in ENGS:
        c.q[eng].append((tuple(tgts), None, None, 0))


_LN_SCRATCH = 0


def alloc_ln_pack(sc, with_ffn=False):
    sb = {}
    global _LN_SCRATCH
    u = _LN_SCRATCH
    _LN_SCRATCH += 1
    sb["sq"] = sc.t(f"sqt{u}", [128, TOK], BF16)
    sb["ssum"] = sc.t(f"ssum{u}", [1, TOK], F32)
    sb["ssq"] = sc.t(f"ssq{u}", [1, TOK], F32)
    sb["sv1"] = sc.t(f"sv1_{u}", [1, TOK], F32)
    sb["sv2"] = sc.t(f"sv2_{u}", [1, TOK], F32)
    sb["mu_bc"] = sc.t(f"mu_bc{u}", [128, TOK], F32)
    sb["sc_bc"] = sc.t(f"sc_bc{u}", [128, TOK], F32)
    sb["tmp"] = sc.t(f"tmpf{u}", [128, TOK], F32)
    return sb


def emit_ln(c, sb, variant):
    eps2 = {"plain": None, "l0a": 1e-6, "l0ffn": 1e-3}[variant]
    ssum, ssq = sb["ssum"], sb["ssq"]
    sv1, sv2 = sb["sv1"], sb["sv2"]
    for ch in range(DCH):
        wv = (("t", c.now("t")),) if ch > 0 else ()
        vc = c.op("v", P(lambda v, o=sb["xn"][ch], i=sb["x"][ch]:
                         v.tensor_scalar(o[:], i[:], 1.0, None, ALU.mult)), wv)
        sq = c.op("v", P(lambda v, o=sb["sq"], i=sb["xn"][ch]:
                         v.tensor_tensor(o[:], i[:], i[:], ALU.mult)),
                  (("v", vc),))
        for piece in range(4):
            w = [("v", vc)]
            if ch == 0:
                w.append(c.pguard[piece])
            c.op("t", P(lambda t, pb=c.pb[piece], lh=sb["ones_bf"],
                        r=sb["xn"][ch], n0=piece * 512, st0=(ch == 0),
                        sp=(ch == DCH - 1):
                        t.matmul(pb[0:1, :], lh[:, 0:1], r[:, n0:n0 + 512],
                                 start=st0, stop=sp)), w)
        for piece in range(4):
            w = [("v", sq)]
            if ch == 0:
                w.append(c.pguard[4 + piece])
            c.op("t", P(lambda t, pb=c.pb[4 + piece], lh=sb["ones_bf"],
                        r=sb["sq"], n0=piece * 512, st0=(ch == 0),
                        sp=(ch == DCH - 1):
                        t.matmul(pb[0:1, :], lh[:, 0:1], r[:, n0:n0 + 512],
                                 start=st0, stop=sp)), w)
    tmm = c.now("t")
    for piece in range(4):
        ev = c.op("v", P(lambda v, o=ssum, pb=c.pb[piece], n0=piece * 512:
                         v.tensor_scalar(o[0:1, n0:n0 + 512], pb[0:1, :],
                                         1.0 / D, None, ALU.mult)),
                  (("t", tmm),))
        c.pguard[piece] = ("v", ev)
        ev = c.op("v", P(lambda v, o=ssq, pb=c.pb[4 + piece], n0=piece * 512:
                         v.tensor_scalar(o[0:1, n0:n0 + 512], pb[0:1, :],
                                         1.0 / D, None, ALU.mult)),
                  (("t", tmm),))
        c.pguard[4 + piece] = ("v", ev)
    def V(fn, *w):
        return c.op("v", fn, tuple(("v", x) if isinstance(x, int) else x
                                   for x in w))

    def SS(fn, *w):
        return c.op("s", fn, tuple(("v", x) if isinstance(x, int) else x
                                   for x in w))
    p = c.now("v")
    p = V(P(lambda v, o=sv1, m=ssum: v.tensor_tensor(
        o[0:1, :], m[0:1, :], m[0:1, :], ALU.mult)), p)
    p = V(P(lambda v, o=sv1, m=ssq: v.tensor_tensor(
        o[0:1, :], m[0:1, :], o[0:1, :], ALU.subtract)), p)
    p = V(P(lambda v, o=sv2, i=sv1: v.tensor_scalar(
        o[0:1, :], i[0:1, :], 1e-5, None, ALU.add)), p)
    if eps2 is None:
        ps = SS(P(lambda s, o=sv2, i=sv2: s.activation(
            o[0:1, :], i[0:1, :], AF.Sqrt)), p)
        p = V(P(lambda v, o=sv2, i=sv2: v.reciprocal(
            o[0:1, :], i[0:1, :])), ("s", ps))
    else:
        p1 = V(P(lambda v, o=ssq, i=sv2: v.reciprocal(
            o[0:1, :], i[0:1, :])), p)
        ps = SS(P(lambda s, o=sv2, i=sv2: s.activation(
            o[0:1, :], i[0:1, :], AF.Sqrt)), p1)
        p = V(P(lambda v, o=sv2, i=sv2: v.reciprocal(
            o[0:1, :], i[0:1, :])), ("s", ps))
        p = V(P(lambda v, o=sv1, a=sv1, b=ssq: v.tensor_tensor(
            o[0:1, :], a[0:1, :], b[0:1, :], ALU.mult)), p)
        p = V(P(lambda v, o=sv1, i=sv1, e=eps2: v.tensor_scalar(
            o[0:1, :], i[0:1, :], e, None, ALU.add)), p)
        ps = SS(P(lambda s, o=sv1, i=sv1: s.activation(
            o[0:1, :], i[0:1, :], AF.Sqrt)), p)
        p = V(P(lambda v, o=sv1, i=sv1: v.reciprocal(
            o[0:1, :], i[0:1, :])), ("s", ps))
        p = V(P(lambda v, o=sv2, a=sv2, b=sv1: v.tensor_tensor(
            o[0:1, :], a[0:1, :], b[0:1, :], ALU.mult)), p)
    scv = p
    for piece in range(4):
        kmu, ksc = piece, 4 + piece
        w = [("v", scv), c.pguard[kmu], c.pguard[ksc]]
        c.op("t", P(lambda t, pb=c.pb[kmu], lh=sb["ones1_f"], r=ssum,
                    n0=piece * 512:
                    t.matmul(pb[:, :], lh[0:1, :], r[0:1, n0:n0 + 512],
                             start=True, stop=True)), w)
        tb = c.op("t", P(lambda t, pb=c.pb[ksc], lh=sb["ones1_f"], r=sv2,
                         n0=piece * 512:
                         t.matmul(pb[:, :], lh[0:1, :], r[0:1, n0:n0 + 512],
                                  start=True, stop=True)), w)
        e1 = c.op("v", P(lambda v, o=sb["mu_bc"], pb=c.pb[kmu], n0=piece * 512:
                         v.tensor_scalar(o[:, n0:n0 + 512], pb[:, :], 1.0,
                                         None, ALU.mult)), (("t", tb),))
        e2 = c.op("v", P(lambda v, o=sb["sc_bc"], pb=c.pb[ksc], n0=piece * 512:
                         v.tensor_scalar(o[:, n0:n0 + 512], pb[:, :], 1.0,
                                         None, ALU.mult)), (("t", tb),))
        c.pguard[kmu] = ("v", e1)
        c.pguard[ksc] = ("v", e2)
    bc_done = c.now("v")
    for ch in range(DCH):
        t1 = c.op("v", P(lambda v, o=sb["tmp"], x=sb["x"][ch], m=sb["mu_bc"]:
                         v.tensor_tensor(o[:], x[:], m[:], ALU.subtract)),
                  (("v", max(bc_done, c.now("v"))),))
        if variant == "plain":
            c.op("v", P(lambda v, o=sb["xn"][ch], a=sb["tmp"], b=sb["sc_bc"]:
                        v.tensor_tensor(o[:], a[:], b[:], ALU.mult)),
                 (("v", t1),))
        elif variant == "l0a":
            c.op("v", P(lambda v, o=sb["xn"][ch], a=sb["tmp"], b=sb["sc_bc"],
                        w=sb["normw"], ch=ch:
                        v.scalar_tensor_tensor(o[:], a[:], w[:, ch:ch + 1],
                                               b[:], ALU.mult, ALU.mult)),
                 (("v", t1),))
        else:
            t2 = c.op("v", P(lambda v, o=sb["tmp"], a=sb["tmp"], b=sb["sc_bc"]:
                             v.tensor_tensor(o[:], a[:], b[:], ALU.mult)),
                      (("v", t1),))
            xnv = c.op("v", P(lambda v, o=sb["xn"][ch], i=sb["tmp"],
                              g=sb["lngam"], bt=sb["lnbet"], ch=ch:
                              v.tensor_scalar(o[:], i[:], g[:, ch:ch + 1],
                                              bt[:, ch:ch + 1], ALU.mult,
                                              ALU.add)), (("v", t2),))
            c.op("v", P(lambda v, o=sb["amax"], i=sb["xn"][ch], ch=ch:
                        v.reduce_max(o[:, ch:ch + 1], i[:], AXX,
                                     apply_absolute_value=True)),
                 (("v", xnv),))
    return c.now("v")


def bcast_col(c, sb, src, dst_col, k, waits):
    tb = c.op("t", P(lambda t, pb=c.pb[k], lh=sb["ones1_f"], r=src:
                     t.matmul(pb[:, 0:1], lh[0:1, :], r, start=True,
                              stop=True)), list(waits) + [c.pguard[k]])
    ev = c.op("v", P(lambda v, o=dst_col, pb=c.pb[k]:
                     v.tensor_scalar(o, pb[:, 0:1], 1.0, None, ALU.mult)),
              (("t", tb),))
    c.pguard[k] = ("v", ev)
    return ev


def emit_proj(c, w_tiles, w_ready, rhs, rhs_ready, dout, tokw, evac_cb,
              wslice=None):
    npieces = (tokw + 511) // 512
    bi = 0
    for oc in range(dout // 128):
        for piece in range(npieces):
            n0 = piece * 512
            nn = min(512, tokw - n0)
            k = bi % 8
            bi += 1
            pbk = c.pb[k]
            td = 0
            for kc in range(DCH):
                w = [w_ready, rhs_ready]
                if kc == 0:
                    w.append(c.pguard[k])
                td = c.op("t", P(lambda t, pb=pbk, lh=w_tiles[kc],
                                 o0=oc * 128, r=rhs[kc], n0=n0, nn=nn,
                                 st0=(kc == 0), sp=(kc == DCH - 1):
                                 t.matmul(pb[:, 0:nn], lh[:, o0:o0 + 128],
                                          r[:, n0:n0 + nn], start=st0,
                                          stop=sp)), w)
            c.pguard[k] = evac_cb(oc, n0, nn, pbk, ("t", td))


def emit_l0_ffn(c, sbp, io):
    sc = Scope(c.nc)
    u = io["uid"]
    sb = dict(sbp)
    sb["xn"] = [sc.t(f"ffxn{u}_{i}", [128, TOK], BF16) for i in range(DCH)]
    sb["amax"] = sc.t(f"amax{u}", [128, 8], F32)
    sb["lngam"] = sc.t(f"lngam{u}", [128, 8], F32)
    sb["lnbet"] = sc.t(f"lnbet{u}", [128, 8], F32)
    lnp = Scope(c.nc)
    sb.update(alloc_ln_pack(lnp))
    gb1 = c.dma("y", P(lambda y, o=sb["lngam"], i=io["lngb"]:
                       y.dma_start(out=o[:], in_=i[:, 0:8])), stream="gb")
    gb2 = c.dma("y", P(lambda y, o=sb["lnbet"], i=io["lngb"]:
                       y.dma_start(out=o[:], in_=i[:, 8:16])), stream="gb")
    c.q["v"].append(((gb2,), None, None, 0))
    ln_done = emit_ln(c, sb, "l0ffn")
    ln_fence = (("v", c.now("v")), ("t", c.now("t")), ("s", c.now("s")))
    lnp.close()
    sb["wf"] = sc.t(f"wf{u}", [128, D], F32)
    sb["wq"] = [sc.t(f"wqz{u}_{i}", [128, D], BF16) for i in range(DCH)]
    sb["wabs"] = sc.t(f"wabs{u}", [128, D], F32)
    sb["wsg"] = sc.t(f"wsg{u}", [128, D], BF16)
    sb["col"] = sc.t(f"colw{u}", [128, 8], F32)
    sca = sb["sca"]
    rs = 0
    for ch in range(DCH):
        ld = c.dma("y", P(lambda y, o=sb["wf"], hp=io["w_hbm"], ch=ch:
                          y.dma_start(out=o[:], in_=hp[ch * 128:(ch + 1) * 128, :])),
                   (("v", rs),) + (ln_fence if ch == 0 else ()), stream="wf")
        rs = c.op("v", P(lambda v, o=sb["col"], i=sb["wf"], ch=ch:
                         v.reduce_sum(o[:, ch:ch + 1], i[:], AXX,
                                      apply_absolute_value=True)),
                  (ld,))
    vsum = c.now("v")
    kB = 0
    tb = c.op("t", P(lambda t, pb=c.pb[kB], lh=sb["onesc_f"], r=sb["col"]:
                     t.matmul(pb[0:1, 0:8], lh[:, 0:1], r[:, 0:8],
                              start=True, stop=True)),
              [("v", vsum), c.pguard[kB]])
    e1 = c.op("v", P(lambda v, o=sca, pb=c.pb[kB]:
                     v.tensor_scalar(o[0:1, 0:8], pb[0:1, 0:8], 1.0, None,
                                     ALU.mult)), (("t", tb),))
    c.pguard[kB] = ("v", e1)
    c.op("v", P(lambda v, o=sca: v.reduce_sum(o[0:1, 8:9], o[0:1, 0:8], AXX)),
         (("v", e1),))
    bet = c.op("v", P(lambda v, o=sca: v.tensor_scalar(
        o[0:1, 9:10], o[0:1, 8:9], 1.0 / (D * D), None, ALU.mult)),
        (("v", c.now("v")),))
    thr = c.op("v", P(lambda v, o=sca: v.tensor_scalar(
        o[0:1, 10:11], o[0:1, 9:10], 1e-5, 0.5, ALU.add, ALU.mult)),
        (("v", bet),))
    thr_bc = bcast_col(c, sb, sca[0:1, 10:11], sb["thr_col"][:, 0:1], 1,
                       [("v", thr)])
    wq_done = 0
    sgd = 0
    for ch in range(DCH):
        ld = c.dma("y", P(lambda y, o=sb["wf"], hp=io["w_hbm"], ch=ch:
                          y.dma_start(out=o[:], in_=hp[ch * 128:(ch + 1) * 128, :])),
                   (("s", sgd), ("v", rs)), stream="wf")
        ab = c.op("s", P(lambda s, o=sb["wabs"], i=sb["wf"]:
                         s.activation(o[:], i[:], AF.Abs)),
                  (ld, ("v", wq_done)))
        cp = c.op("v", P(lambda v, o=sb["wq"][ch], i=sb["wabs"],
                         tc=sb["thr_col"]:
                         v.tensor_scalar(o[:], i[:], tc[:, 0:1], None,
                                         ALU.is_gt)),
                  (("s", ab), ("v", thr_bc)))
        sgd = c.op("s", P(lambda s, o=sb["wsg"], i=sb["wf"]:
                          s.activation(o[:], i[:], AF.Sign)), (("s", ab),))
        wq_done = c.op("v", P(lambda v, o=sb["wq"][ch], b=sb["wsg"]:
                              v.tensor_tensor(o[:], o[:], b[:], ALU.mult)),
                       (("s", sgd), ("v", cp)))
    kT = 2
    t1 = c.op("t", P(lambda t, pb=c.pb[kT], i=sb["amax"], idn=sb["ident_f"]:
                     t.transpose(pb[0:8, 0:128], i[:, 0:8], idn[:])),
              [("v", ln_done), c.pguard[kT]])
    m1 = c.op("v", P(lambda v, o=sb["gcol"], pb=c.pb[kT]:
                     v.reduce_max(o[0:8, 0:1], pb[0:8, 0:128], AXX)),
              (("t", t1),))
    c.pguard[kT] = ("v", m1)
    t2 = c.op("t", P(lambda t, pb=c.pb[kT], i=sb["gcol"], idn=sb["ident_f"]:
                     t.transpose(pb[0:1, 0:8], i[0:8, 0:1], idn[0:8, 0:8])),
              [("v", m1), c.pguard[kT]])
    m2 = c.op("v", P(lambda v, o=sb["grow"], pb=c.pb[kT]:
                     v.reduce_max(o[0:1, 0:1], pb[0:1, 0:8], AXX)),
              (("t", t2),))
    c.pguard[kT] = ("v", m2)
    st_g = c.dma("y", P(lambda y, o=io["gam_in"].ap(), i=sb["grow"]:
                        y.dma_start(out=o, in_=i[0:1, 0:16])), (("v", m2),),
                 stream="gam")
    ccp = c.collective("AllReduce", ALU.max, io["gam_in"], io["gam_out"],
                       (st_g,))
    ldg = c.dma("y", P(lambda y, o=sb["grow"], i=io["gam_out"]:
                       y.dma_start(out=o[0:1, 16:32], in_=i.ap())),
                (ccp,), stream="gam")
    rg = c.op("v", P(lambda v, o=sca, i=sb["grow"]:
                     v.reciprocal(o[0:1, 11:12], i[0:1, 16:17])),
              (ldg,))
    scq = c.op("v", P(lambda v, o=sca: v.tensor_scalar(
        o[0:1, 12:13], o[0:1, 11:12], 128.0, None, ALU.mult)), (("v", rg),))
    g1 = c.op("v", P(lambda v, o=sca, g=sb["grow"]: v.tensor_tensor(
        o[0:1, 13:14], g[0:1, 16:17], o[0:1, 9:10], ALU.mult)), (("v", scq),))
    sco = c.op("v", P(lambda v, o=sca: v.tensor_scalar(
        o[0:1, 14:15], o[0:1, 13:14], 1.0 / 128.0, None, ALU.mult)),
        (("v", g1),))
    b1 = bcast_col(c, sb, sca[0:1, 12:13], sb["thr_col"][:, 1:2], 3,
                   [("v", scq)])
    b2 = bcast_col(c, sb, sca[0:1, 14:15], sb["thr_col"][:, 2:3], 4,
                   [("v", sco)])
    xq_done = 0
    for ch in range(DCH):
        q1 = c.op("v", P(lambda v, o=sb["xn"][ch], tc=sb["thr_col"]:
                         v.tensor_scalar(o[:], o[:], tc[:, 1:2],
                                         128.0 - 1e-5, ALU.mult, ALU.min)),
                  (("v", b1), ("v", ln_done)))
        xq_done = c.op("v", P(lambda v, o=sb["xn"][ch]:
                              v.tensor_scalar(o[:], o[:], -(128.0 - 1e-5),
                                              None, ALU.max)), (("v", q1),))
    def evac(oc, n0, nn, pbk, tg):
        ev = c.op("v", P(lambda v, o=sb["x"][oc], pb=pbk, tc=sb["thr_col"],
                         n0=n0, nn=nn:
                         v.scalar_tensor_tensor(o[:, n0:n0 + nn],
                                                pb[:, 0:nn], tc[:, 2:3],
                                                o[:, n0:n0 + nn],
                                                ALU.mult, ALU.add)),
                  [tg, ("v", b2)])
        return ("v", ev)

    emit_proj(c, sb["wq"], ("v", wq_done), sb["xn"], ("v", xq_done),
              D, TOK, evac)
    fence(c)
    sc.close()


def alloc_common(nc, stack):
    def sb_(name, shape, dt):
        t = nc.sbuf_tensor(name, shape, dt).__enter__()
        stack.append(t)
        return t
    sb = {}
    sb["x"] = [sb_(f"x{i}", [128, TOK], F32) for i in range(DCH)]
    sb["ones_bf"] = sb_("ones_bf", [128, 1], BF16)
    sb["onesc_f"] = sb_("onesc_f", [128, 1], F32)
    sb["ones1_f"] = sb_("ones1_f", [1, 128], F32)
    sb["ident_f"] = sb_("ident_f", [128, 128], F32)
    sb["ident_bf"] = sb_("ident_bf", [128, 128], BF16)
    sb["sca"] = sb_("sca", [1, 32], F32)
    sb["thr_col"] = sb_("thr_col", [128, 4], F32)
    sb["grow"] = sb_("grow", [1, 32], F32)
    sb["gcol"] = sb_("gcol", [8, 1], F32)
    return sb


def emit_init_consts(c, sb, io):
    c.op("v", P(lambda v, o=sb["ones_bf"]: v.memset(o[:], 1.0)))
    c.op("v", P(lambda v, o=sb["onesc_f"]: v.memset(o[:], 1.0)))
    c.op("v", P(lambda v, o=sb["ones1_f"]: v.memset(o[:], 1.0)))
    c.op("v", P(lambda v, o=sb["grow"]: v.memset(o[:], 0.0)))
    c.dma("y", P(lambda y, o=sb["ident_f"], i=io["ident"]:
                 y.dma_start(out=o[:], in_=i)), stream="init")
    c.dma("g", P(lambda g, o=sb["ident_bf"], i=io["ident"]:
                 g.dma_start(out=o[:], in_=i)), stream="initg")


def emit_load_x(c, sb, x_hbm):
    prs = []
    for ch in range(DCH):
        prs.append(c.dma("y", P(lambda y, o=sb["x"][ch], hp=x_hbm, ch=ch:
                   y.dma_start(out=o[:], in_=hp[ch * 128:(ch + 1) * 128, :])),
                   stream="ldx"))
    return tuple(prs)


def emit_store_x(c, sb, x_hbm, waits):
    prs = []
    for ch in range(DCH):
        prs.append(c.dma("y", P(lambda y, i=sb["x"][ch], hp=x_hbm, ch=ch:
                   y.dma_start(out=hp[ch * 128:(ch + 1) * 128, :], in_=i[:])),
                   waits if ch == 0 else (), stream="stx"))
    c.q["y"].append(((prs[-1],), None, None, 0))


def _final_wait_patch(ctx):
    pass


def _finalize(self):
    secmap = {"t": self.block.tensor, "v": self.block.vector,
              "s": self.block.scalar, "g": self.block.gpsimd,
              "y": self.block.sync}
    for eng in ENGS:
        ops = self.q[eng]
        if not ops:
            continue

        def run(e, ops=ops):
            last = {}
            for waits, fn, semname, by in ops:
                for snm, val in waits:
                    if last.get(snm, -1) < val:
                        e.wait_ge(self.sem[snm], val)
                        last[snm] = val
                if fn is not None:
                    fn(e).then_inc(self.sem[semname], by)
        secmap[eng](run)


Ctx.finalize = _finalize


PHASES = ("l0a", "l0f1", "l0b", "l0f2", "l1a", "l1f1", "l1b", "l1f2")


def build_program(phases):
    nc = bass.Bass("TRN2", target_bir_lowering=False, debug=False)
    io = {}
    io["x_in"] = nc.dram_tensor("x_in", [D, TOK], F32, kind="ExternalInput").ap()
    io["x_out"] = nc.dram_tensor("x_out", [D, TOK], F32, kind="ExternalOutput").ap()
    io["ident"] = nc.dram_tensor("ident", [128, 128], F32, kind="ExternalInput").ap()
    needs_ffn = any(p in ("l0f1", "l0f2", "lnonly") for p in phases)
    if any(p in ("l1f1", "l1f2") for p in phases):
        io["w1T_sh"] = nc.dram_tensor("w1T_sh", [D, 512], F32, kind="ExternalInput").ap()
        io["w2T_sh"] = nc.dram_tensor("w2T_sh", [4096, 128], F32, kind="ExternalInput").ap()
        io["b1c"] = nc.dram_tensor("b1c", [128, 32], F32, kind="ExternalInput").ap()
        io["b2c"] = nc.dram_tensor("b2c", [128, 8], F32, kind="ExternalInput").ap()
        io["ag1_in"] = nc.dram_tensor("ag1_in", [D, 512], BF16)
        io["ag1_out"] = nc.dram_tensor("ag1_out", [8 * D, 512], BF16)
        io["ag2_in"] = nc.dram_tensor("ag2_in", [4096, 128], BF16)
        io["ag2_out"] = nc.dram_tensor("ag2_out", [8 * 4096, 128], BF16)
    if "l0b" in phases:
        io["b_wqT"] = nc.dram_tensor("b_wqT", [D, D], F32, kind="ExternalInput").ap()
        io["b_wkT"] = nc.dram_tensor("b_wkT", [D, D], F32, kind="ExternalInput").ap()
        io["b_wvT"] = nc.dram_tensor("b_wvT", [D, D], F32, kind="ExternalInput").ap()
        io["b_wffT"] = nc.dram_tensor("b_wffT", [D, D], F32, kind="ExternalInput").ap()
        io["b_bff"] = nc.dram_tensor("b_bff", [128, 8], F32, kind="ExternalInput").ap()
        io["agk_in"] = nc.dram_tensor("agk_in", [D, TOK], BF16)
        io["agk_out"] = nc.dram_tensor("agk_out", [8 * D, TOK], BF16)
        io["agv_in"] = nc.dram_tensor("agv_in", [TOK, 16 * 65], BF16)
        io["agv_out"] = nc.dram_tensor("agv_out", [8 * TOK, 16 * 65], BF16)
    if "l1b" in phases:
        io["m_wdq"] = nc.dram_tensor("m_wdq", [D, 256], F32, kind="ExternalInput").ap()
        io["m_wdkv"] = nc.dram_tensor("m_wdkv", [D, 256], F32, kind="ExternalInput").ap()
        io["m_wuq"] = nc.dram_tensor("m_wuq", [256, D], F32, kind="ExternalInput").ap()
        io["m_wqr"] = nc.dram_tensor("m_wqr", [256, 512], F32, kind="ExternalInput").ap()
        io["m_wuk"] = nc.dram_tensor("m_wuk", [256, D], F32, kind="ExternalInput").ap()
        io["m_wuv"] = nc.dram_tensor("m_wuv", [256, D], F32, kind="ExternalInput").ap()
        io["m_wkr"] = nc.dram_tensor("m_wkr", [D, 32], F32, kind="ExternalInput").ap()
        io["m_wo"] = nc.dram_tensor("m_wo", [D, D], F32, kind="ExternalInput").ap()
        io["m_cos"] = nc.dram_tensor("m_cos", [128, TOK], F32, kind="ExternalInput").ap()
        io["m_sin"] = nc.dram_tensor("m_sin", [128, TOK], F32, kind="ExternalInput").ap()
        io["mgate"] = nc.dram_tensor("mgate", [128, 128], F32, kind="ExternalInput").ap()
        if "rotT" not in io:
            io["rotT"] = nc.dram_tensor("rotT", [128, 128], F32, kind="ExternalInput").ap()
        if "amask" not in io:
            io["amask"] = nc.dram_tensor("amask", [512, 512], F32, kind="ExternalInput").ap()
        io["agkc_in"] = nc.dram_tensor("agkc_in", [D, TOK], BF16)
        io["agkc_out"] = nc.dram_tensor("agkc_out", [8 * D, TOK], BF16)
        io["agkr_in"] = nc.dram_tensor("agkr_in", [32, TOK], BF16)
        io["agkr_out"] = nc.dram_tensor("agkr_out", [256, TOK], BF16)
        io["agv2_in"] = nc.dram_tensor("agv2_in", [TOK, 16 * 65], BF16)
        io["agv2_out"] = nc.dram_tensor("agv2_out", [8 * TOK, 16 * 65], BF16)
    if "l1a" in phases:
        for nm2, shp in (("la_wrT", [D, D]), ("la_wkT", [D, D]),
                         ("la_wvT", [D, D]), ("la_wtdT", [D, D]),
                         ("la_gd", [128, 8]), ("la_mix", [128, 8]),
                         ("la_sel", [128, 64])):
            io[nm2] = nc.dram_tensor(nm2, shp, F32, kind="ExternalInput").ap()
        io["la_sel8"] = nc.dram_tensor("la_sel8", [128, 512], I32, kind="ExternalInput").ap()
        io["la_bidx"] = nc.dram_tensor("la_bidx", [128, 4], I32, kind="ExternalInput").ap()
        io["la_stidx"] = nc.dram_tensor("la_stidx", [128, 32], I32, kind="ExternalInput").ap()
        for nm2 in ("scr_z", "scr_ek", "scr_v", "scr_r", "scr_xin"):
            io[nm2] = nc.dram_tensor(nm2, [D, TOK], F32)
        io["agbx_in"] = nc.dram_tensor("agbx_in", [512, 8], BF16)
        io["agbx_out"] = nc.dram_tensor("agbx_out", [4096, 8], BF16)
        io["agsum_in"] = nc.dram_tensor("agsum_in", [128, 96], F32)
        io["agsum_out"] = nc.dram_tensor("agsum_out", [1024, 96], F32)
        io["agst_in"] = nc.dram_tensor("agst_in", [4096, 65], F32)
        io["agst_out"] = nc.dram_tensor("agst_out", [32768, 65], F32)
    if "l0a" in phases:
        io["a_wqkv"] = nc.dram_tensor("a_wqkv", [D, 3072], F32, kind="ExternalInput").ap()
        io["a_wout"] = nc.dram_tensor("a_wout", [D, D], F32, kind="ExternalInput").ap()
        io["a_bout"] = nc.dram_tensor("a_bout", [128, 8], F32, kind="ExternalInput").ap()
        io["a_normw"] = nc.dram_tensor("a_normw", [128, 8], F32, kind="ExternalInput").ap()
        io["a_pmkT"] = nc.dram_tensor("a_pmkT", [128, 256], F32, kind="ExternalInput").ap()
        io["a_pmv"] = nc.dram_tensor("a_pmv", [256, 64], F32, kind="ExternalInput").ap()
        io["a_cos"] = nc.dram_tensor("a_cos", [128, TOK], F32, kind="ExternalInput").ap()
        io["a_sin"] = nc.dram_tensor("a_sin", [128, TOK], F32, kind="ExternalInput").ap()
        if "rotT" not in io:
            io["rotT"] = nc.dram_tensor("rotT", [128, 128], F32, kind="ExternalInput").ap()
        if "amask" not in io:
            io["amask"] = nc.dram_tensor("amask", [512, 512], F32, kind="ExternalInput").ap()
    if needs_ffn:
        io["w_hbm"] = nc.dram_tensor("l0f_w", [D, D], F32, kind="ExternalInput").ap()
        io["lngb"] = nc.dram_tensor("l0f_gb", [128, 16], F32, kind="ExternalInput").ap()
        io["gam_in"] = nc.dram_tensor("gam_in", [1, 16], F32)
        io["gam_out"] = nc.dram_tensor("gam_out", [1, 16], F32)
        io["gam_in2"] = nc.dram_tensor("gam_in2", [1, 16], F32)
        io["gam_out2"] = nc.dram_tensor("gam_out2", [1, 16], F32)

    stack = []
    with nc.Block() as block:
        sb = alloc_common(nc, stack)
        c = Ctx(nc, block)
        emit_init_consts(c, sb, io)
        emit_load_x(c, sb, io["x_in"])
        for eng in ("v", "t", "s"):
            c.q[eng].append((c.dma_all(), None, None, 0))
        first_ffn = True
        for ph in phases:
            if ph == "lnonly":
                io2 = dict(io, uid=0)
                sc = Scope(nc)
                sb2 = dict(sb)
                sb2["xn"] = [sc.t(f"lnxn{i}", [128, TOK], BF16)
                             for i in range(DCH)]
                sb2["amax"] = sc.t("amax_l", [128, 8], F32)
                sb2["lngam"] = sc.t("lngam_l", [128, 8], F32)
                sb2["lnbet"] = sc.t("lnbet_l", [128, 8], F32)
                sb2.update(alloc_ln_pack(Scope(nc)))
                c.dma("y", P(lambda y, o=sb2["lngam"], i=io["lngb"]:
                             y.dma_start(out=o[:], in_=i[:, 0:8])), stream="gb")
                gbl = c.dma("y", P(lambda y, o=sb2["lnbet"], i=io["lngb"]:
                                   y.dma_start(out=o[:], in_=i[:, 8:16])),
                            stream="gb")
                c.q["v"].append(((gbl,), None, None, 0))
                nv = emit_ln(c, sb2, "l0ffn")
                for ch in range(DCH):
                    nv = c.op("v", P(lambda v, o=sb2["x"][ch], i=sb2["xn"][ch]:
                                     v.tensor_scalar(o[:], i[:], 1.0, None,
                                                     ALU.mult)), (("v", nv),))
            elif ph == "l0a":
                emit_l0_attn_a(c, sb, io)
            elif ph == "l1a":
                emit_l1_attn_a(c, sb, io)
            elif ph == "l1b":
                emit_l1_attn_b(c, sb, io)
            elif ph == "l0b":
                emit_l0_attn_b(c, sb, io)
            elif ph in ("l1f1", "l1f2"):
                iof = io.setdefault("_l1f_io", dict(io, uid=2))
                first_l1f = "ag_sems" not in iof
                iof2 = dict(iof, uid=2 if first_l1f else 3)
                if not first_l1f:
                    iof2["ag_sems"] = iof["ag_sems"]
                emit_l1_ffn(c, sb, iof2, first=first_l1f)
                if first_l1f:
                    iof["ag_sems"] = iof2["ag_sems"]
            elif ph in ("l0f1", "l0f2"):
                io2 = dict(io, uid=(0 if first_ffn else 1))
                if not first_ffn:
                    io2["gam_in"] = io["gam_in2"]
                    io2["gam_out"] = io["gam_out2"]
                first_ffn = False
                emit_l0_ffn(c, sb, io2)
            else:
                raise NotImplementedError(ph)
        emit_store_x(c, sb, io["x_out"],
                     ((c.allv()), (c.allt()), (c.alls()), ("g", c.now("g")))
                     + c.dma_all())
        c.finalize()
    return nc


def shard_x(x):
    shards = []
    for i in range(NC):
        blocks = [x[b, sg * SEG:(sg + 1) * SEG, :] for (b, sg) in core_insts(i)]
        xc = np.concatenate(blocks, axis=0)
        shards.append(np.ascontiguousarray(xc.T))
    return shards


def unshard_x(shards):
    x = np.zeros((B, S, D), np.float32)
    for i in range(NC):
        xc = shards[i].T
        for j, (b, sg) in enumerate(core_insts(i)):
            x[b, sg * SEG:(sg + 1) * SEG, :] = xc[j * SEG:(j + 1) * SEG]
    return x


_IDENT = np.eye(128, dtype=np.float32)


def _build_rott():
    r = np.zeros((128, 128), np.float32)
    for j in range(64):
        r[2 * j + 1, 2 * j] = -1.0
        r[2 * j, 2 * j + 1] = 1.0
    return r


_ROTT = _build_rott()


def _build_amask():
    kp = np.arange(512)[:, None]
    q = np.arange(512)[None, :]
    return (kp <= q).astype(np.float32)


_AMASK = _build_amask()


def core_positions(i):
    pos = np.zeros(TOK, np.int64)
    for j, (b, sg) in enumerate(core_insts(i)):
        pos[j * SEG:(j + 1) * SEG] = np.arange(sg * SEG, (sg + 1) * SEG)
    return pos


def _rope_tables_core(i, dh):
    pos = core_positions(i).astype(np.float64)
    j = (np.arange(128) % dh) // 2
    inv = 1.0 / 10000.0 ** ((2 * j) / dh)
    ang = inv[:, None] * pos[None, :]
    return (np.cos(ang).astype(np.float32),
            np.sin(ang).astype(np.float32))


def run_phase(phases, shards, inputs, extra=None):
    nc = build_program(tuple(phases))
    in_maps = []
    for i in range(NC):
        m = {"x_in": shards[i], "ident": _IDENT}
        if any(p in ("l0f1", "l0f2", "lnonly") for p in phases):
            m["l0f_w"] = inputs["L0_ffn_weight"]
            gb = np.zeros((128, 16), np.float32)
            gb[:, 0:8] = inputs["L0_ffn_ln_gamma"].reshape(8, 128).T
            gb[:, 8:16] = inputs["L0_ffn_ln_beta"].reshape(8, 128).T
            m["l0f_gb"] = np.ascontiguousarray(gb)
        if any(p in ("l1f1", "l1f2") for p in phases):
            w1t = np.ascontiguousarray(inputs["L1_ffn_w1"].T)
            m["w1T_sh"] = np.ascontiguousarray(w1t[:, i * 512:(i + 1) * 512])
            w2t = np.ascontiguousarray(inputs["L1_ffn_w2"].T)
            m["w2T_sh"] = np.ascontiguousarray(w2t[:, i * 128:(i + 1) * 128])
            m["b1c"] = np.ascontiguousarray(
                inputs["L1_ffn_b1"].reshape(32, 128).T)
            m["b2c"] = np.ascontiguousarray(
                inputs["L1_ffn_b2"].reshape(8, 128).T)
        if "l0b" in phases:
            m["b_wqT"] = np.ascontiguousarray(inputs["L0_attn_b_Wq"].T)
            m["b_wkT"] = np.ascontiguousarray(inputs["L0_attn_b_Wk"].T)
            m["b_wvT"] = np.ascontiguousarray(inputs["L0_attn_b_Wv"].T)
            m["b_wffT"] = np.ascontiguousarray(inputs["L0_attn_b_Wff"].T)
            m["b_bff"] = np.ascontiguousarray(
                inputs["L0_attn_b_bff"].reshape(8, 128).T)
        if "l1a" in phases:
            m["la_wrT"] = np.ascontiguousarray(inputs["L1_attn_a_W_R"].T)
            m["la_wkT"] = np.ascontiguousarray(inputs["L1_attn_a_W_K"].T)
            m["la_wvT"] = np.ascontiguousarray(inputs["L1_attn_a_W_V"].T)
            m["la_wtdT"] = np.ascontiguousarray(inputs["L1_attn_a_W_td"].T)
            m["la_gd"] = np.ascontiguousarray(
                inputs["L1_attn_a_global_decay"].reshape(8, 128).T)
            m["la_mix"] = np.ascontiguousarray(
                inputs["L1_attn_a_mix"].reshape(8, 128).T)
            sel = np.zeros((128, 64), np.float32)
            bidx = np.full((128, 4), 10 ** 7, np.int32)
            stidx = np.full((128, 32), 10 ** 7, np.int32)
            for j, (b, sg) in enumerate(core_insts(i)):
                sel[:, sg * 4 + j] = 1.0
                if sg > 0:
                    po, ps = owner_of(b, sg - 1)
                    bidx[:, j] = po * 512 + ps * 128 + np.arange(128)
                    for oc in range(8):
                        stidx[:, j * 8 + oc] = (po * 4096 + (ps * 8 + oc) * 128
                                                + np.arange(128))
            m["la_sel"] = sel
            m["la_sel8"] = np.ascontiguousarray(
                np.repeat(sel.astype(np.int32), 8, axis=1))
            m["la_bidx"] = bidx
            m["la_stidx"] = stidx
        if "l1b" in phases:
            m["m_wdq"] = inputs["L1_attn_b_W_DQ"]
            m["m_wdkv"] = inputs["L1_attn_b_W_DKV"]
            m["m_wuq"] = inputs["L1_attn_b_W_UQ"]
            m["m_wqr"] = inputs["L1_attn_b_W_QR"]
            m["m_wuk"] = inputs["L1_attn_b_W_UK"]
            m["m_wuv"] = inputs["L1_attn_b_W_UV"]
            m["m_wkr"] = inputs["L1_attn_b_W_KR"]
            m["m_wo"] = inputs["L1_attn_b_W_O"]
            m["m_cos"], m["m_sin"] = _rope_tables_core(i, 32)
            m["rotT"] = _ROTT
            m["amask"] = _AMASK
            mg = np.zeros((128, 128), np.float32)
            for sl in range(2):
                myseg = i if sl == 0 else 15 - i
                for r in range(8):
                    for j in range(8):
                        gseg = r if j < 4 else 15 - r
                        if sl == 0:
                            if j < 4:
                                mg[:, r * 4 + j] = 1.0 if gseg < myseg else 0.0
                        else:
                            mg[:, 64 + r * 8 + j] = 1.0 if gseg < myseg else 0.0
            m["mgate"] = mg
        if "l0a" in phases:
            m["a_wqkv"] = inputs["L0_attn_a_w_qkv"]
            m["a_wout"] = inputs["L0_attn_a_w_out"]
            m["a_bout"] = np.ascontiguousarray(
                inputs["L0_attn_a_b_out"].reshape(8, 128).T)
            m["a_normw"] = np.ascontiguousarray(
                inputs["L0_attn_a_norm_w"].reshape(8, 128).T)
            pmkt = np.concatenate([inputs["L0_attn_a_pm_k"][h].T
                                   for h in range(16)], axis=1)
            m["a_pmkT"] = np.ascontiguousarray(
                np.concatenate([pmkt, pmkt], axis=0))
            m["a_pmv"] = np.ascontiguousarray(
                inputs["L0_attn_a_pm_v"].reshape(256, 64))
            m["a_cos"], m["a_sin"] = _rope_tables_core(i, 64)
            m["rotT"] = _ROTT
            m["amask"] = _AMASK
        if extra:
            m.update(extra[i] if isinstance(extra, list) else extra)
        in_maps.append(m)
    import os
    if os.environ.get("KSIM"):
        import concourse.bass_interp as bass_interp
        sim = bass_interp.MultiCoreSim(nc, NC, require_finite=False,
                                       require_nnan=False,
                                       ignore_data_errors=True)
        for i in range(NC):
            for k, v in in_maps[i].items():
                sim.cores[i].tensor(k)[:] = v
        sim.simulate()
        return [np.array(sim.cores[i].mem_tensor("x_out")) for i in range(NC)]
    res = run_bass_kernel_spmd(nc, in_maps, core_ids=list(range(NC)))
    return [res.results[i]["x_out"] for i in range(NC)]


ALL_PHASES = ("l0a", "l0f1", "l0b", "l0f2", "l1a", "l1f1", "l1b", "l1f2")


def kernel(**inputs):
    import os
    x = np.asarray(inputs["L0_attn_a_seq"], np.float32)
    shards = shard_x(x)
    if os.environ.get("KUNFUSED"):
        for ph in ALL_PHASES:
            shards = run_phase([ph], shards, inputs)
    else:
        shards = run_phase(list(ALL_PHASES), shards, inputs)
    return unshard_x(shards)


def emit_l0_attn_a(c, sbp, io):
    sc = Scope(c.nc)
    sb = dict(sbp)
    sb["xn"] = [sc.t(f"axn{i}", [128, TOK], BF16) for i in range(DCH)]
    sb["normw"] = sc.t("normw", [128, 8], F32)
    lnp = Scope(c.nc)
    sb.update(alloc_ln_pack(lnp))
    nw = c.dma("y", P(lambda y, o=sb["normw"], i=io["a_normw"]:
                      y.dma_start(out=o[:], in_=i)), stream="gb")
    c.q["v"].append(((nw,), None, None, 0))
    ln_done = emit_ln(c, sb, "l0a")
    ln_fence = (("v", c.now("v")), ("t", c.now("t")), ("s", c.now("s")))
    lnp.close()

    wqkv = [sc.t(f"awq{i}", [128, 3072], BF16) for i in range(DCH)]
    wo2 = [sc.t(f"awo{i}", [128, D], BF16) for i in range(2)]
    bout = sc.t("about", [128, 8], F32)
    cosx = sc.t("acos", [128, TOK], BF16)
    sinx = sc.t("asin", [128, TOK], BF16)
    rt = sc.t("art", [128, 128], BF16)
    pmk = sc.t("apmk", [128, 256], BF16)
    pmv = sc.t("apmv", [16, 16 * 65], BF16)
    masks = sc.t("amask_sb", [128, 4 * 512], BF16)
    for ch in range(DCH):
        c.dma("g", P(lambda g, o=wqkv[ch], i=io["a_wqkv"], ch=ch:
                     g.dma_start(out=o[:], in_=i[ch * 128:(ch + 1) * 128, :])),
              ln_fence if ch == 0 else (), stream="wa")
    c.dma("y", P(lambda y, o=bout, i=io["a_bout"]:
                 y.dma_start(out=o[:], in_=i)), ln_fence, stream="wa2")
    c.dma("g", P(lambda g, o=cosx, i=io["a_cos"]:
                 g.dma_start(out=o[:], in_=i)), stream="wa")
    c.dma("g", P(lambda g, o=sinx, i=io["a_sin"]:
                 g.dma_start(out=o[:], in_=i)), stream="wa")
    c.dma("g", P(lambda g, o=rt, i=io["rotT"]:
                 g.dma_start(out=o[:], in_=i)), stream="wa")
    c.dma("g", P(lambda g, o=pmk, i=io["a_pmkT"]:
                 g.dma_start(out=o[:], in_=i)), stream="wa")
    pm_set = c.op("v", P(lambda v, o=pmv: v.memset(o[:], 1.0)))
    for h in range(16):
        c.dma("g", P(lambda g, o=pmv, i=io["a_pmv"], h=h:
                     g.dma_start(out=o[:, h * 65:h * 65 + 64],
                                 in_=i[h * 16:(h + 1) * 16, :])),
              (("v", pm_set),), stream="wa")
    c.dma("g", P(lambda g, o=masks, i=io["amask"]:
                 g.dma_start(out=o[:].rearrange("p (k q) -> p k q", k=4),
                             in_=i.rearrange("(k p) q -> p k q", p=128))),
          stream="wa")
    wready = c.dma_all()

    qs = [sc.t(f"aqs{i}", [128, SEG], BF16) for i in range(8)]
    ks = [sc.t(f"aks{i}", [128, SEG], BF16) for i in range(8)]
    vfm = sc.t("avfm", [128, SEG], BF16)
    vaug = [sc.t(f"avg{i}", [128, 16 * 65], BF16) for i in range(4)]
    rtmp1 = sc.t("artmp1", [128, SEG], F32)
    pts = [sc.t(f"apts{i}", [128, SEG], BF16) for i in range(4)]
    oun = sc.t("aoun", [128, SEG], F32)
    rtmp2 = oun
    rcp = sc.t("arcp", [1, SEG], F32)
    attn = [sc.t(f"aat{i}", [128, SEG], BF16) for i in range(8)]
    vg_set = c.op("v", P(lambda v: None) if False else
                  P(lambda v, o=vaug[0]: v.memset(o[:], 1.0)))
    for tb in range(1, 4):
        vg_set = c.op("v", P(lambda v, o=vaug[tb]: v.memset(o[:], 1.0)))

    bki = [0]

    def bank():
        k = bki[0] % 8
        bki[0] += 1
        return k

    wo_prev = [("t", 0), ("t", 0)]
    for sg in range(4):
        s0 = sg * SEG
        segw = [("v", c.now("v")), ("t", c.now("t")), ("s", c.now("s"))]
        for oc in range(24):
            k = bank()
            pbk = c.pb[k]
            td = 0
            for kc in range(DCH):
                w = [("v", ln_done)] + list(wready)
                if kc == 0:
                    w.append(c.pguard[k])
                    w += segw
                td = c.op("t", P(lambda t, pb=pbk, lh=wqkv[kc], o0=oc * 128,
                                 r=sb["xn"][kc], s0=s0, st0=(kc == 0),
                                 sp=(kc == DCH - 1):
                                 t.matmul(pb[:, :], lh[:, o0:o0 + 128],
                                          r[:, s0:s0 + SEG], start=st0,
                                          stop=sp)), w)
            if oc < 16:
                dst = qs[oc] if oc < 8 else ks[oc - 8]
                ev = c.op("s", P(lambda s, o=dst, pb=pbk:
                                 s.activation(o[:], pb[:, :], AF.Identity)),
                          (("t", td),))
                k2 = bank()
                pb2 = c.pb[k2]
                tr = c.op("t", P(lambda t, pb=pb2, lh=rt, r=dst:
                                 t.matmul(pb[:, :], lh[:, :], r[:, :],
                                          start=True, stop=True)),
                          [("s", ev), c.pguard[k2]])
                v1 = c.op("v", P(lambda v, o=rtmp1, a=dst, cx=cosx, s0=s0:
                                 v.tensor_tensor(o[:], a[:],
                                                 cx[:, s0:s0 + SEG],
                                                 ALU.mult)),
                          [("s", ev), ("v", c.now("v"))])
                v2 = c.op("v", P(lambda v, o=rtmp2, pb=pb2, sx=sinx, s0=s0:
                                 v.tensor_tensor(o[:], pb[:, :],
                                                 sx[:, s0:s0 + SEG],
                                                 ALU.mult)), (("t", tr),))
                v3 = c.op("v", P(lambda v, o=dst, a=rtmp1, b=rtmp2:
                                 v.tensor_tensor(o[:], a[:], b[:], ALU.add)),
                          (("v", v1), ("v", v2)))
                c.pguard[k] = ("v", v1)
                c.pguard[k2] = ("v", v2)
            else:
                voc = oc - 16
                ev = c.op("s", P(lambda s, o=vfm, pb=pbk:
                                 s.activation(o[:], pb[:, :], AF.Identity)),
                          [("t", td), ("t", c.now("t"))])
                c.pguard[k] = ("s", ev)
                for tb in range(4):
                    k3 = bank()
                    pb3 = c.pb[k3]
                    tt = c.op("t", P(lambda t, pb=pb3, i=vfm,
                                     idn=sb["ident_bf"], tb=tb:
                                     t.transpose(pb[:, :].bitcast(BF16)[:, 0:128],
                                                 i[:, tb * 128:(tb + 1) * 128],
                                                 idn[:])),
                              [("s", ev), c.pguard[k3]])
                    hA, hB = 2 * voc, 2 * voc + 1
                    e1 = c.op("v", P(lambda v, o=vaug[tb], pb=pb3, hA=hA:
                                     v.tensor_scalar(o[:, hA * 65:hA * 65 + 64],
                                                     pb[:, :].bitcast(BF16)[:, 0:64],
                                                     1.0, None, ALU.mult)),
                              [("t", tt), ("v", vg_set)])
                    e2 = c.op("v", P(lambda v, o=vaug[tb], pb=pb3, hB=hB:
                                     v.tensor_scalar(o[:, hB * 65:hB * 65 + 64],
                                                     pb[:, :].bitcast(BF16)[:, 64:128],
                                                     1.0, None, ALU.mult)),
                              (("t", tt),))
                    c.pguard[k3] = ("v", e2)
        proj_done = [("v", c.now("v")), ("s", c.now("s"))]
        state = {"mu_prev": ("v", 0), "tb2_prev": ("t", 0)}
        ko_of = {}

        def emit_norm(h, av_cnt):
            oc_h, row = h // 2, (h % 2) * 64
            ko = ko_of.pop(h)
            pbo = c.pb[ko]
            ev = c.op("s", P(lambda s, o=oun, pb=pbo:
                             s.activation(o[0:65, :],
                                          pb[0:65, :], AF.Identity)),
                      (("t", av_cnt), state["mu_prev"]))
            c.pguard[ko] = ("s", ev)
            rc = c.op("v", P(lambda v, o=rcp, i=oun:
                             v.reciprocal(o[0:1, :], i[64:65, :])),
                      (("s", ev), state["tb2_prev"]))
            kb = bank()
            while kb in ko_of.values():
                kb = bank()
            pbb = c.pb[kb]
            tb2 = c.op("t", P(lambda t, pb=pbb, lh=sb["ones1_f"],
                              r=rcp:
                              t.matmul(pb[0:64, :], lh[0:1, 0:64],
                                       r[0:1, :],
                                       start=True, stop=True)),
                       [("v", rc), c.pguard[kb]])
            mu = c.op("v", P(lambda v, o=attn[oc_h], row=row, a=oun,
                             pb=pbb:
                             v.tensor_tensor(o[row:row + 64, :],
                                             a[0:64, :],
                                             pb[0:64, :], ALU.mult)),
                      (("t", tb2),))
            c.pguard[kb] = ("v", mu)
            state["mu_prev"] = ("v", mu)
            state["tb2_prev"] = ("t", tb2)

        av_init = c.now("t")
        avh = {}
        DEFER = 2
        pendq = []
        steps = [(h, kc) for h in range(16) for kc in range(5)]
        for si, (h, kc) in enumerate(steps):
            oc_h, row = h // 2, (h % 2) * 64
            if kc == 0:
                ko_of[h] = bank()
                while len([1 for k2 in ko_of.values()
                           if k2 == ko_of[h]]) > 1:
                    ko_of[h] = bank()
            ko = ko_of[h]
            ksc = bank()
            while ksc in ko_of.values():
                ksc = bank()
            pbs = c.pb[ksc]
            pbo = c.pb[ko]
            if kc == 0:
                tmm = c.op("t", P(lambda t, pb=pbs, lh=pmk, h=h,
                                  q=qs[oc_h], row=row:
                                  t.matmul(pb[0:16, :],
                                           lh[row:row + 64,
                                              h * 16:(h + 1) * 16],
                                           q[row:row + 64, :],
                                           start=True, stop=True)),
                           proj_done + [c.pguard[ksc]])
                npart = 16
            else:
                tmm = c.op("t", P(lambda t, pb=pbs, kk=ks[oc_h], row=row,
                                  kc=kc, q=qs[oc_h]:
                                  t.matmul(pb[:, :],
                                           kk[row:row + 64,
                                              (kc - 1) * 128:kc * 128],
                                           q[row:row + 64, :],
                                           start=True, stop=True)),
                           proj_done + [c.pguard[ksc]])
                npart = 128
            ptb = si % 4
            ex = c.op("s", P(lambda s, o=pts[ptb], pb=pbs, np_=npart:
                             s.activation(o[0:np_, :], pb[0:np_, :],
                                          AF.Exp, scale=0.125)),
                      [("t", tmm), ("t", avh.get(si - 4, av_init))])
            c.pguard[ksc] = ("s", ex)
            pdep = ("s", ex)
            if kc > 0:
                mk = c.op("v", P(lambda v, o=pts[ptb], m=masks, kc=kc:
                                 v.tensor_tensor(o[:], o[:],
                                                 m[:, (kc - 1) * 512:
                                                   kc * 512],
                                                 ALU.mult)), (("s", ex),))
                pdep = ("v", mk)
            pendq.append((ptb, pdep, h, kc, si))
            if len(pendq) > DEFER:
                p_ptb, p_dep, p_h, p_kc, p_si = pendq.pop(0)
                w = [p_dep]
                if p_kc == 0:
                    w.append(c.pguard[ko_of[p_h]])
                lhs = (P(lambda t, pb=c.pb[ko_of[p_h]], lh=pmv, h=p_h,
                         r=pts[p_ptb]:
                         t.matmul(pb[0:65, :],
                                  lh[0:16, h * 65:h * 65 + 65],
                                  r[0:16, :], start=True, stop=False))
                       if p_kc == 0 else
                       P(lambda t, pb=c.pb[ko_of[p_h]], lh=vaug[p_kc - 1],
                         h=p_h, r=pts[p_ptb], sp=(p_kc == 4):
                         t.matmul(pb[0:65, :],
                                  lh[:, h * 65:h * 65 + 65],
                                  r[:, :], start=False, stop=sp)))
                avh[p_si] = c.op("t", lhs, w)
                if p_kc == 4:
                    emit_norm(p_h, avh[p_si])
        for p_ptb, p_dep, p_h, p_kc, p_si in pendq:
            w = [p_dep]
            if p_kc == 0:
                w.append(c.pguard[ko_of[p_h]])
            lhs = (P(lambda t, pb=c.pb[ko_of[p_h]], lh=pmv, h=p_h,
                     r=pts[p_ptb]:
                     t.matmul(pb[0:65, :],
                              lh[0:16, h * 65:h * 65 + 65],
                              r[0:16, :], start=True, stop=False))
                   if p_kc == 0 else
                   P(lambda t, pb=c.pb[ko_of[p_h]], lh=vaug[p_kc - 1],
                     h=p_h, r=pts[p_ptb], sp=(p_kc == 4):
                     t.matmul(pb[0:65, :],
                              lh[:, h * 65:h * 65 + 65],
                              r[:, :], start=False, stop=sp)))
            avh[p_si] = c.op("t", lhs, w)
            if p_kc == 4:
                emit_norm(p_h, avh[p_si])
        pendq = []
        attn_done = ("v", c.now("v"))
        td = 0
        for kc in range(DCH):
            wb = kc % 2
            ld = c.dma("g", P(lambda g, o=wo2[wb], i=io["a_wout"], kc=kc:
                              g.dma_start(out=o[:],
                                          in_=i[kc * 128:(kc + 1) * 128, :])),
                       (wo_prev[wb],), stream=f"wo{wb}")
            for oc in range(DCH):
                w = [attn_done, ld]
                if kc == 0:
                    w.append(c.pguard[oc])
                td = c.op("t", P(lambda t, pb=c.pb[oc], lh=wo2[wb],
                                 o0=oc * 128, r=attn[kc], st0=(kc == 0),
                                 sp=(kc == DCH - 1):
                                 t.matmul(pb[:, :], lh[:, o0:o0 + 128],
                                          r[:, :], start=st0, stop=sp)), w)
            wo_prev[wb] = ("t", td)
        for oc in range(DCH):
            ev = c.op("v", P(lambda v, o=sb["x"][oc], pb=c.pb[oc], bc=bout,
                             oc=oc, s0=s0:
                             v.scalar_tensor_tensor(o[:, s0:s0 + SEG],
                                                    pb[:, :],
                                                    bc[:, oc:oc + 1],
                                                    o[:, s0:s0 + SEG],
                                                    ALU.add, ALU.add)),
                      (("t", td),))
            c.pguard[oc] = ("v", ev)
        bki[0] = 0
    fence(c)
    sc.close()


def emit_l1_ffn(c, sbp, io, first):
    sc = Scope(c.nc)
    sb = dict(sbp)
    sb["xn"] = [sc.t(f"fxn{io['uid']}_{i}", [128, TOK], BF16)
                for i in range(DCH)]
    lnp = Scope(c.nc)
    sb.update(alloc_ln_pack(lnp))
    ln_done = emit_ln(c, sb, "plain")
    ln_fence = (("v", c.now("v")), ("t", c.now("t")), ("s", c.now("s")))
    lnp.close()
    u = io["uid"]

    if first:
        bsc = Scope(c.nc)
        wsl = bsc.t(f"bwsl{u}", [128, 512], F32)
        sgn = bsc.t(f"bsgn{u}", [128, 512], BF16)
        ab = bsc.t(f"bab{u}", [128, 512], BF16)
        arow = bsc.t(f"barow{u}", [1, 512], F32)
        bwsh = bsc.t(f"bwsh{u}", [128, 512], BF16)
        sb["tmpbc"] = bsc.t(f"btmpbc{u}", [128, 640], F32)

        def binarize(w_sh_hbm, nch, ncols, alpha_mm_bank):
            for chx in range(nch):
                ld = c.dma("y", P(lambda y, o=wsl, i=w_sh_hbm, chx=chx,
                                  nc_=ncols:
                                  y.dma_start(out=o[:, 0:nc_],
                                              in_=i[chx * 128:(chx + 1) * 128, :])),
                           (("s", c.now("s")),)
                           + (ln_fence if chx == 0 else ()), stream=f"bz{u}")
                aa = c.op("s", P(lambda s, o=ab, i=wsl, nc_=ncols:
                                 s.activation(o[:, 0:nc_], i[:, 0:nc_],
                                              AF.Abs)), (ld, ("t", c.now("t"))))
                w = [("s", aa)]
                if chx == 0:
                    w.append(c.pguard[alpha_mm_bank])
                c.op("t", P(lambda t, pb=c.pb[alpha_mm_bank], lh=sb["ones_bf"],
                            r=ab, st0=(chx == 0), sp=(chx == nch - 1),
                            nc_=ncols:
                            t.matmul(pb[0:1, 0:nc_], lh[:, 0:1], r[:, 0:nc_],
                                     start=st0, stop=sp)), w)

        binarize(io["w1T_sh"], 8, 512, 0)
        ev = c.op("v", P(lambda v, o=arow, pb=c.pb[0]:
                         v.tensor_scalar(o[0:1, 0:512], pb[0:1, 0:512],
                                         1.0 / D, None, ALU.mult)),
                  (("t", c.now("t")),))
        c.pguard[0] = ("v", ev)
        a1bc = c.op("t", P(lambda t, pb=c.pb[1], lh=sb["ones1_f"], r=arow:
                           t.matmul(pb[:, :], lh[0:1, :], r[0:1, 0:512],
                                    start=True, stop=True)),
                    [("v", ev), c.pguard[1]])
        a1e = c.op("v", P(lambda v, o=sb["tmpbc"], pb=c.pb[1]:
                          v.tensor_scalar(o[:, 0:512], pb[:, 0:512], 1.0,
                                          None, ALU.mult)), (("t", a1bc),))
        c.pguard[1] = ("v", a1e)
        pst = ("v", 0)
        for chx in range(8):
            ld = c.dma("y", P(lambda y, o=wsl, i=io["w1T_sh"], chx=chx:
                              y.dma_start(out=o[:],
                                          in_=i[chx * 128:(chx + 1) * 128, :])),
                       (("s", c.now("s")), pst), stream=f"bz{u}")
            sg = c.op("s", P(lambda s, o=sgn, i=wsl:
                             s.activation(o[:], i[:], AF.Sign)), (ld,))
            ml = c.op("v", P(lambda v, o=bwsh, a=sgn, b=sb["tmpbc"]:
                             v.tensor_tensor(o[:], a[:], b[:, 0:512],
                                             ALU.mult)),
                      (("s", sg), ("v", a1e), pst))
            pst = c.dma("y", P(lambda y, i=bwsh, o=io["ag1_in"], chx=chx:
                               y.dma_start(out=o.ap()[chx * 128:(chx + 1) * 128, :],
                                           in_=i[:])), (("v", ml),),
                        stream=f"bzs{u}")
        ag1s = c.collective("AllGather", ALU.bypass, io["ag1_in"],
                            io["ag1_out"], (pst,))
        binarize(io["w2T_sh"], 32, 128, 2)
        ev2 = c.op("v", P(lambda v, o=arow, pb=c.pb[2]:
                          v.tensor_scalar(o[0:1, 0:128], pb[0:1, 0:128],
                                          1.0 / 4096.0, None, ALU.mult)),
                   (("t", c.now("t")),))
        c.pguard[2] = ("v", ev2)
        a2bc = c.op("t", P(lambda t, pb=c.pb[3], lh=sb["ones1_f"], r=arow:
                           t.matmul(pb[:, 0:128], lh[0:1, :], r[0:1, 0:128],
                                    start=True, stop=True)),
                    [("v", ev2), c.pguard[3]])
        a2e = c.op("v", P(lambda v, o=sb["tmpbc"], pb=c.pb[3]:
                          v.tensor_scalar(o[:, 512:640], pb[:, 0:128], 1.0,
                                          None, ALU.mult)), (("t", a2bc),))
        c.pguard[3] = ("v", a2e)
        pst2 = pst
        for chx in range(32):
            ld = c.dma("y", P(lambda y, o=wsl, i=io["w2T_sh"], chx=chx:
                              y.dma_start(out=o[:, 0:128],
                                          in_=i[chx * 128:(chx + 1) * 128, :])),
                       (("s", c.now("s")), pst2), stream=f"bz{u}")
            sg = c.op("s", P(lambda s, o=sgn, i=wsl:
                             s.activation(o[:, 0:128], i[:, 0:128], AF.Sign)),
                      (ld,))
            ml = c.op("v", P(lambda v, o=bwsh, a=sgn, b=sb["tmpbc"]:
                             v.tensor_tensor(o[:, 0:128], a[:, 0:128],
                                             b[:, 512:640], ALU.mult)),
                      (("s", sg), ("v", a2e), pst2))
            pst2 = c.dma("y", P(lambda y, i=bwsh, o=io["ag2_in"], chx=chx:
                                y.dma_start(out=o.ap()[chx * 128:(chx + 1) * 128, :],
                                            in_=i[:, 0:128])), (("v", ml),),
                         stream=f"bzs{u}")
        ag2s = c.collective("AllGather", ALU.bypass, io["ag2_in"],
                            io["ag2_out"], (pst2,))
        io["bz_fence"] = (pst2, ("s", c.now("s")), ("t", c.now("t")),
                          ("v", c.now("v")))
        bsc.close()
        io["ag_sems"] = (ag1s, ag2s)
    ag1s, ag2s = io["ag_sems"]
    bz_fence = io.get("bz_fence", ())

    bw1 = [sc.t(f"fbw1_{u}_{i}", [128, 4096], BF16) for i in range(DCH)]
    b1c = sc.t(f"fb1c{u}", [128, 32], F32)
    b2c = sc.t(f"fb2c{u}", [128, 8], F32)
    hbf = [sc.t(f"fh{u}_{i}", [128, 512], BF16) for i in range(32)]
    bw2 = [sc.t(f"fbw2_{u}_{i}", [128, 1024], BF16) for i in range(2)]
    ag1r = io["ag1_out"].ap().rearrange("(r k p) h -> k p r h", r=8, k=8,
                                        p=128)
    ag2r = io["ag2_out"].ap().rearrange("(r h p) q -> h p r q", r=8, h=32,
                                        p=128)
    for kc in range(DCH):
        c.dma("y", P(lambda y, o=bw1[kc], i=ag1r, kc=kc:
                     y.dma_start(out=o[:].rearrange("p (r h) -> p r h", r=8),
                                 in_=i[kc])),
              (ag1s,) + (tuple(bz_fence) + ln_fence if kc == 0 else ()),
              stream=f"fw{u}")
    c.dma("y", P(lambda y, o=b1c, i=io["b1c"]: y.dma_start(out=o[:], in_=i)),
          stream=f"fw{u}")
    c.dma("y", P(lambda y, o=b2c, i=io["b2c"]: y.dma_start(out=o[:], in_=i)),
          stream=f"fw{u}")
    wready = c.dma_all()

    bw2_prev = [("t", 0), ("t", 0)]
    for q in range(4):
        q0 = q * 512
        qw = [("v", c.now("v")), ("s", c.now("s")), ("t", c.now("t"))]
        for hc in range(32):
            k = hc % 8
            pbk = c.pb[k]
            td = 0
            for kc in range(DCH):
                w = [("v", ln_done)] + list(wready)
                if kc == 0:
                    w.append(c.pguard[k])
                    w += qw
                td = c.op("t", P(lambda t, pb=pbk, lh=bw1[kc], hc=hc,
                                 r=sb["xn"][kc], q0=q0, st0=(kc == 0),
                                 sp=(kc == DCH - 1):
                                 t.matmul(pb[:, :], lh[:, hc * 128:(hc + 1) * 128],
                                          r[:, q0:q0 + 512], start=st0,
                                          stop=sp)), w)
            ge = c.op("s", P(lambda s, o=hbf[hc], pb=pbk, b=b1c, hc=hc:
                             s.activation(o[:], pb[:, :], AF.Gelu,
                                          bias=b[:, hc:hc + 1])),
                      (("t", td),))
            c.pguard[k] = ("s", ge)
        h_done = ("s", c.now("s"))
        td = 0
        for hc in range(32):
            wb = hc % 2
            ld = c.dma("g", P(lambda g, o=bw2[wb], i=ag2r, hc=hc:
                              g.dma_start(out=o[:].rearrange(
                                  "p (r q) -> p r q", r=8), in_=i[hc])),
                       (bw2_prev[wb], ag2s)
                       + (tuple(bz_fence) + ln_fence if hc < 2 else ()),
                       stream=f"f2w{u}{wb}")
            for oc in range(DCH):
                w = [h_done, ld]
                if hc == 0:
                    w.append(c.pguard[oc])
                td = c.op("t", P(lambda t, pb=c.pb[oc], lh=bw2[wb],
                                 o0=oc * 128, r=hbf[hc], st0=(hc == 0),
                                 sp=(hc == 31):
                                 t.matmul(pb[:, :], lh[:, o0:o0 + 128],
                                          r[:, :], start=st0, stop=sp)), w)
            bw2_prev[wb] = ("t", td)
        for oc in range(DCH):
            ev = c.op("v", P(lambda v, o=sb["x"][oc], pb=c.pb[oc], bc=b2c,
                             oc=oc, q0=q0:
                             v.scalar_tensor_tensor(o[:, q0:q0 + 512],
                                                    pb[:, :],
                                                    bc[:, oc:oc + 1],
                                                    o[:, q0:q0 + 512],
                                                    ALU.add, ALU.add)),
                      (("t", td),))
            c.pguard[oc] = ("v", ev)
    fence(c)
    sc.close()


def emit_l0_attn_b(c, sbp, io):
    sc = Scope(c.nc)
    sb = dict(sbp)
    sb["xn"] = [sc.t(f"bxn{i}", [128, TOK], BF16) for i in range(DCH)]
    lnp = Scope(c.nc)
    sb.update(alloc_ln_pack(lnp))
    ln_done = emit_ln(c, sb, "plain")
    ln_fence = (("v", c.now("v")), ("t", c.now("t")), ("s", c.now("s")))
    lnp.close()

    qf = [sc.t(f"bqf{i}", [128, TOK], BF16) for i in range(DCH)]
    bffc = sc.t("bbffc", [128, 8], F32)
    psc = Scope(c.nc)
    bffl = None
    wres = [psc.t(f"bw{i}", [128, D], BF16) for i in range(DCH)]
    kstage = psc.t("bkst", [128, TOK], BF16)
    vaug = psc.t("bvaug", [128, 16 * 65], BF16)
    bffl = c.dma("y", P(lambda y, o=bffc, i=io["b_bff"]:
                        y.dma_start(out=o[:], in_=i)), ln_fence, stream="bb")

    bki = [0]

    def bank():
        k = bki[0] % 8
        bki[0] += 1
        return k

    def load_w(w_hbm, prev_done):
        lds = []
        for kc in range(DCH):
            lds.append(c.dma("g", P(lambda g, o=wres[kc], i=w_hbm, kc=kc:
                             g.dma_start(out=o[:],
                                         in_=i[kc * 128:(kc + 1) * 128, :])),
                             (prev_done if kc == 0 else ()), stream="bwl"))
        return lds[-1]

    ldq = load_w(io["b_wqT"], ln_fence)
    for oc in range(DCH):
        for piece in range(4):
            n0 = piece * 512
            k = bank()
            td = 0
            for kc in range(DCH):
                w = [("v", ln_done), ldq]
                if kc == 0:
                    w.append(c.pguard[k])
                td = c.op("t", P(lambda t, pb=c.pb[k], lh=wres[kc],
                                 o0=oc * 128, r=sb["xn"][kc], n0=n0,
                                 st0=(kc == 0), sp=(kc == DCH - 1):
                                 t.matmul(pb[:, :], lh[:, o0:o0 + 128],
                                          r[:, n0:n0 + 512], start=st0,
                                          stop=sp)), w)
            ev = c.op("v", P(lambda v, o=qf[oc], pb=c.pb[k], n0=n0:
                             v.tensor_scalar(o[:, n0:n0 + 512], pb[:, :],
                                             1.0, None, ALU.mult)),
                      (("t", td),))
            c.pguard[k] = ("v", ev)
    q_done = ("v", c.now("v"))
    ldk = load_w(io["b_wkT"], (("t", c.now("t")),))
    kst_prev = ("sd_bkg", 0)
    for oc in range(DCH):
        evs = []
        for piece in range(4):
            n0 = piece * 512
            k = bank()
            td = 0
            for kc in range(DCH):
                w = [("v", ln_done), ldk]
                if kc == 0:
                    w.append(c.pguard[k])
                td = c.op("t", P(lambda t, pb=c.pb[k], lh=wres[kc],
                                 o0=oc * 128, r=sb["xn"][kc], n0=n0,
                                 st0=(kc == 0), sp=(kc == DCH - 1):
                                 t.matmul(pb[:, :], lh[:, o0:o0 + 128],
                                          r[:, n0:n0 + 512], start=st0,
                                          stop=sp)), w)
            ev = c.op("v", P(lambda v, o=kstage, pb=c.pb[k], n0=n0:
                             v.tensor_scalar(o[:, n0:n0 + 512], pb[:, :],
                                             1.0, None, ALU.mult)),
                      (("t", td), kst_prev))
            evs.append(ev)
            c.pguard[k] = ("v", ev)
        kst_prev = c.dma("y", P(lambda y, i=kstage, o=io["agk_in"], oc=oc:
                                y.dma_start(out=o.ap()[oc * 128:(oc + 1) * 128, :],
                                            in_=i[:])),
                         (("v", evs[-1]),), stream="bkg")
    ldv = load_w(io["b_wvT"], (("t", c.now("t")),))
    vg_set = c.op("v", P(lambda v, o=vaug: v.memset(o[:], 1.0)),
                  (("v", c.now("v")),))
    vst_prev = ("sd_bvg", 0)
    for tc in range(16):
        bs = [bank(), bank()]
        tds = []
        for half in range(2):
            td = 0
            for kc in range(DCH):
                w = [("v", ln_done), ldv]
                if kc == 0:
                    w.append(c.pguard[bs[half]])
                td = c.op("t", P(lambda t, pb=c.pb[bs[half]], lh=sb["xn"][kc],
                                 tc=tc, r=wres[kc], half=half,
                                 st0=(kc == 0), sp=(kc == DCH - 1):
                                 t.matmul(pb[:, :],
                                          lh[:, tc * 128:(tc + 1) * 128],
                                          r[:, half * 512:half * 512 + 512],
                                          start=st0, stop=sp)), w)
            tds.append(td)
        evl = 0
        for h in range(16):
            half, col = h // 8, (h % 8) * 64
            evl = c.op("v", P(lambda v, o=vaug, pb=c.pb[bs[half]], h=h,
                              col=col:
                              v.tensor_scalar(o[:, h * 65:h * 65 + 64],
                                              pb[:, col:col + 64], 1.0,
                                              None, ALU.mult)),
                       [("t", tds[half]), ("v", vg_set), vst_prev])
        c.pguard[bs[0]] = ("v", evl)
        c.pguard[bs[1]] = ("v", evl)
        vst_prev = c.dma("y", P(lambda y, i=vaug, o=io["agv_in"], tc=tc:
                                y.dma_start(out=o.ap()[tc * 128:(tc + 1) * 128, :],
                                            in_=i[:])),
                         (("v", evl),), stream="bvg")
    agk = c.collective("AllGather", ALU.bypass, io["agk_in"], io["agk_out"],
                       (kst_prev,))
    agv = c.collective("AllGather", ALU.bypass, io["agv_in"], io["agv_out"],
                       (vst_prev,))
    proj_fence = (("v", c.now("v")), ("t", c.now("t")), ("s", c.now("s")),
                  kst_prev, vst_prev)
    psc.close()
    attn = sb["xn"]
    Kt = [sc.t(f"bK{i}", [128, S], BF16) for i in range(2)]
    Vt = [sc.t(f"bV{i}", [128, 64 * 65], BF16) for i in range(2)]
    qh = [sc.t(f"bqh{i}", [128, 1024], BF16) for i in range(2)]
    zms = None
    for tl in (Kt[0], Kt[1], qh[0], qh[1]):
        zms = c.op("v", P(lambda v, o=tl: v.memset(o[64:128, :], 0.0)),
                   () if zms is None else (("v", zms),))
    zms = ("v", zms)
    pts = [sc.t(f"bpts{i}", [128, 512], BF16) for i in range(4)]
    oun = sc.t("boun", [128, 1024], F32)
    rcp = sc.t("brcp", [1, 1024], F32)
    agvr = io["agv_out"].ap().rearrange("(r b j p) cc -> r b p j cc", r=8,
                                        b=2, j=8, p=128)
    kv_free = [("t", 0), ("t", 0)]
    mu_prev = ("v", 0)
    ev_prev = ("s", 0)
    bhs = [(b, h) for b in range(B) for h in range(16)]
    kvld = {}

    def load_kv_l0b(idx):
        b, h = bhs[idx]
        pr = idx % 2
        lk = lv = None
        for r in range(NC):
            lk = c.dma("y", P(lambda y, o=Kt[pr], i=io["agk_out"], r=r, h=h,
                              b=b:
                              y.dma_start(out=o[0:64,
                                               r * 1024:(r + 1) * 1024],
                                          in_=i.ap()[r * 1024 + (h // 2) * 128
                                                     + (h % 2) * 64:
                                                     r * 1024 + (h // 2) * 128
                                                     + (h % 2) * 64 + 64,
                                                     b * 1024:(b + 1) * 1024])),
                       [agk, kv_free[pr]] + (list(proj_fence) if idx == 0
                                             else []), stream=f"bkl{pr}")
            lv = c.dma("y", P(lambda y, o=Vt[pr], i=agvr, r=r, h=h, b=b:
                              y.dma_start(out=o[:, r * 520:(r + 1) * 520]
                                          .rearrange("p (j cc) -> p j cc",
                                                     j=8),
                                          in_=i[r, b][:, :,
                                                      h * 65:(h + 1) * 65])),
                       [agv, kv_free[pr]], stream=f"bvl{pr}")
        qc = c.op("v", P(lambda v, o=qh[pr], i=qf[h // 2], row=(h % 2) * 64,
                         b=b:
                         v.tensor_scalar(o[0:64, :],
                                         i[row:row + 64,
                                           b * 1024:(b + 1) * 1024],
                                         1.0, None, ALU.mult)),
                  [q_done, ("v", c.now("v")), kv_free[pr]])
        kvld[idx] = (lk, lv, ("v", qc))

    load_kv_l0b(0)
    for idx, (b, h) in enumerate(bhs):
        oc_h, row = h // 2, (h % 2) * 64
        pr = idx % 2
        lk, lv, qc = kvld[idx]
        if idx + 1 < len(bhs):
            load_kv_l0b(idx + 1)
        ko = [bank(), bank()]
        av_init = c.now("t")
        avh = {}
        steps = [(kc, piece) for piece in range(2) for kc in range(64)]
        DEFER = 2
        pendq = []
        for si, (kc, piece) in enumerate(steps):
            ksb = bank()
            while ksb in ko:
                ksb = bank()
            tmm = c.op("t", P(lambda t, pb=c.pb[ksb], K=Kt[pr], kc=kc,
                              q=qh[pr], n0=piece * 512:
                              t.matmul(pb[:, :],
                                       K[:, kc * 128:(kc + 1) * 128],
                                       q[:, n0:n0 + 512],
                                       start=True, stop=True)),
                       [lk, lv, qc, zms, c.pguard[ksb]])
            ptb = si % 4
            ex = c.op("s", P(lambda s, o=pts[ptb], pb=c.pb[ksb]:
                             s.activation(o[:], pb[:, :], AF.Exp,
                                          scale=0.03125)),
                      [("t", tmm), ("t", avh.get(si - 4, av_init))])
            c.pguard[ksb] = ("s", ex)
            pendq.append((ptb, ex, piece, kc, si))
            if len(pendq) > DEFER:
                p_ptb, p_ex, p_piece, p_kc, p_si = pendq.pop(0)
                w = [("s", p_ex)]
                if p_kc == 0:
                    w.append(c.pguard[ko[p_piece]])
                avh[p_si] = c.op("t", P(lambda t, pb=c.pb[ko[p_piece]],
                                        V=Vt[pr],
                                        kc=p_kc, r=pts[p_ptb],
                                        st0=(p_kc == 0),
                                        sp=(p_kc == 63):
                                        t.matmul(pb[0:65, :],
                                                 V[:, kc * 65:(kc + 1) * 65],
                                                 r[:, :], start=st0,
                                                 stop=sp)), w)
        for p_ptb, p_ex, p_piece, p_kc, p_si in pendq:
            av_last = c.op("t", P(lambda t, pb=c.pb[ko[p_piece]], V=Vt[pr],
                                  kc=p_kc, r=pts[p_ptb], st0=(p_kc == 0),
                                  sp=(p_kc == 63):
                                  t.matmul(pb[0:65, :],
                                           V[:, kc * 65:(kc + 1) * 65],
                                           r[:, :], start=st0, stop=sp)),
                           (("s", p_ex),))
        pendq = []
        kv_free[pr] = ("t", av_last)
        for piece in range(2):
            ev = c.op("v", P(lambda v, o=oun, pb=c.pb[ko[piece]],
                             n0=piece * 512:
                             v.tensor_scalar(o[0:65, n0:n0 + 512],
                                             pb[0:65, :], 1.0, None,
                                             ALU.mult)),
                      [("t", av_last), mu_prev])
            c.pguard[ko[piece]] = ("v", ev)
            ev_prev = ("v", ev)
        rc = c.op("v", P(lambda v, o=rcp, i=oun:
                         v.reciprocal(o[0:1, :], i[64:65, :])),
                  [ev_prev, ("v", c.now("v")), ("t", c.now("t"))])
        for piece in range(2):
            kb = bank()
            while kb in ko:
                kb = bank()
            tb2 = c.op("t", P(lambda t, pb=c.pb[kb], lh=sb["ones1_f"],
                              r=rcp, n0=piece * 512:
                              t.matmul(pb[0:64, :], lh[0:1, 0:64],
                                       r[0:1, n0:n0 + 512], start=True,
                                       stop=True)),
                       [("v", rc), c.pguard[kb]])
            mu = c.op("v", P(lambda v, o=attn[oc_h], row=row, a=oun,
                             pb=c.pb[kb], b=b, n0=piece * 512:
                             v.tensor_tensor(o[row:row + 64,
                                               b * 1024 + n0:
                                               b * 1024 + n0 + 512],
                                             a[0:64, n0:n0 + 512],
                                             pb[0:64, :], ALU.mult)),
                      (("t", tb2),))
            c.pguard[kb] = ("v", mu)
            mu_prev = ("v", mu)
    attn_done = ("v", c.now("v"))
    wo2 = [sc.t(f"bwo{i}", [128, D], BF16) for i in range(2)]
    wo_prev = [("t", 0), ("t", 0)]
    for piece in range(4):
        n0 = piece * 512
        td = 0
        for kc in range(DCH):
            wb = kc % 2
            ld = c.dma("g", P(lambda g, o=wo2[wb], i=io["b_wffT"], kc=kc:
                              g.dma_start(out=o[:],
                                          in_=i[kc * 128:(kc + 1) * 128, :])),
                       (wo_prev[wb],), stream=f"bwo{wb}")
            for oc in range(DCH):
                w = [attn_done, ld]
                if kc == 0:
                    w.append(c.pguard[oc])
                td = c.op("t", P(lambda t, pb=c.pb[oc], lh=wo2[wb],
                                 o0=oc * 128, r=attn[kc], n0=n0,
                                 st0=(kc == 0), sp=(kc == DCH - 1):
                                 t.matmul(pb[:, :], lh[:, o0:o0 + 128],
                                          r[:, n0:n0 + 512], start=st0,
                                          stop=sp)), w)
            wo_prev[wb] = ("t", td)
        for oc in range(DCH):
            ev = c.op("v", P(lambda v, o=sb["x"][oc], pb=c.pb[oc], bc=bffc,
                             oc=oc, n0=n0:
                             v.scalar_tensor_tensor(o[:, n0:n0 + 512],
                                                    pb[:, :],
                                                    bc[:, oc:oc + 1],
                                                    o[:, n0:n0 + 512],
                                                    ALU.add, ALU.add)),
                      (("t", td), bffl))
            c.pguard[oc] = ("v", ev)
    fence(c)
    sc.close()


def emit_l1_attn_b(c, sbp, io):
    sc = Scope(c.nc)
    sb = dict(sbp)
    masks2 = sc.t("mmask", [128, 4 * 512], BF16)
    sb["xn"] = [sc.t(f"mxn{i}", [128, TOK], BF16) for i in range(DCH)]
    lnp = Scope(c.nc)
    sb.update(alloc_ln_pack(lnp))
    ln_done = emit_ln(c, sb, "plain")
    ln_fence = (("v", c.now("v")), ("t", c.now("t")), ("s", c.now("s")))
    lnp.close()
    qcf = [sc.t(f"mqc{i}", [128, TOK], BF16) for i in range(DCH)]
    qrf = [sc.t(f"mqr{i}", [128, TOK], BF16) for i in range(4)]

    psc = Scope(c.nc)
    cos2 = psc.t("mcos", [128, TOK], BF16)
    sin2 = psc.t("msin", [128, TOK], BF16)
    rt2 = psc.t("mrt", [128, 128], BF16)
    cqf = [psc.t(f"mcq{i}", [128, TOK], BF16) for i in range(2)]
    ckf = [psc.t(f"mck{i}", [128, TOK], BF16) for i in range(2)]
    wdq = [psc.t(f"mwdq{i}", [128, 256], BF16) for i in range(DCH)]
    wdkv = [psc.t(f"mwdkv{i}", [128, 256], BF16) for i in range(DCH)]
    wu = [psc.t(f"mwu{i}", [128, D], BF16) for i in range(2)]
    wkr = [psc.t(f"mwkr{i}", [128, 32], BF16) for i in range(DCH)]
    kstage = psc.t("mkst", [128, TOK], BF16)
    krst = psc.t("mkrst", [32, TOK], BF16)
    vaug = psc.t("mvaug", [128, 16 * 65], BF16)
    rtmp1 = psc.t("mrt1", [128, 512], F32)
    rtmp2 = psc.t("mrt2", [128, 512], F32)
    c.dma("g", P(lambda g, o=cos2, i=io["m_cos"]:
                 g.dma_start(out=o[:], in_=i)), ln_fence, stream="mw")
    c.dma("g", P(lambda g, o=sin2, i=io["m_sin"]:
                 g.dma_start(out=o[:], in_=i)), stream="mw")
    c.dma("g", P(lambda g, o=rt2, i=io["rotT"]:
                 g.dma_start(out=o[:], in_=i)), stream="mw")
    c.dma("g", P(lambda g, o=masks2, i=io["amask"]:
                 g.dma_start(out=o[:].rearrange("p (k q) -> p k q", k=4),
                             in_=i.rearrange("(k p) q -> p k q", p=128))),
          stream="mw")
    for ch in range(DCH):
        c.dma("g", P(lambda g, o=wdq[ch], i=io["m_wdq"], ch=ch:
                     g.dma_start(out=o[:], in_=i[ch * 128:(ch + 1) * 128, :])),
              ln_fence if ch == 0 else (), stream="mw")
        c.dma("g", P(lambda g, o=wdkv[ch], i=io["m_wdkv"], ch=ch:
                     g.dma_start(out=o[:], in_=i[ch * 128:(ch + 1) * 128, :])),
              stream="mw")
        c.dma("g", P(lambda g, o=wkr[ch], i=io["m_wkr"], ch=ch:
                     g.dma_start(out=o[:], in_=i[ch * 128:(ch + 1) * 128, :])),
              stream="mw")
    wready = c.dma_all()

    bki = [0]

    def bank():
        k = bki[0] % 8
        bki[0] += 1
        return k

    for dst, wt in ((cqf, wdq), (ckf, wdkv)):
        for oc in range(2):
            for piece in range(4):
                n0 = piece * 512
                k = bank()
                td = 0
                for kc in range(DCH):
                    w = [("v", ln_done)] + list(wready)
                    if kc == 0:
                        w.append(c.pguard[k])
                    td = c.op("t", P(lambda t, pb=c.pb[k], lh=wt[kc],
                                     o0=oc * 128, r=sb["xn"][kc], n0=n0,
                                     st0=(kc == 0), sp=(kc == DCH - 1):
                                     t.matmul(pb[:, :], lh[:, o0:o0 + 128],
                                              r[:, n0:n0 + 512], start=st0,
                                              stop=sp)), w)
                ev = c.op("s", P(lambda s, o=dst[oc], pb=c.pb[k], n0=n0:
                                 s.activation(o[:, n0:n0 + 512], pb[:, :],
                                              AF.Identity)), (("t", td),))
                c.pguard[k] = ("s", ev)
    cq_done = ("s", c.now("s"))
    for piece in range(4):
        n0 = piece * 512
        k = bank()
        td = 0
        for kc in range(DCH):
            w = [("v", ln_done)] + list(wready)
            if kc == 0:
                w.append(c.pguard[k])
            td = c.op("t", P(lambda t, pb=c.pb[k], lh=wkr[kc],
                             r=sb["xn"][kc], n0=n0, st0=(kc == 0),
                             sp=(kc == DCH - 1):
                             t.matmul(pb[0:32, :], lh[:, :],
                                      r[:, n0:n0 + 512], start=st0,
                                      stop=sp)), w)
        ev = c.op("s", P(lambda s, o=krst, pb=c.pb[k], n0=n0:
                         s.activation(o[0:32, n0:n0 + 512], pb[0:32, :],
                                      AF.Identity)), (("t", td),))
        c.pguard[k] = ("s", ev)
    kr_ev = ("s", c.now("s"))
    rot_last = ("v", c.now("v"))
    for piece in range(4):
        n0 = piece * 512
        k2 = bank()
        trk = c.op("t", P(lambda t, pb=c.pb[k2], lh=rt2, r=krst, n0=n0:
                          t.matmul(pb[0:32, 0:512], lh[0:32, 0:32],
                                   r[0:32, n0:n0 + 512], start=True,
                                   stop=True)), [kr_ev, c.pguard[k2]])
        v1 = c.op("v", P(lambda v, o=rtmp1, a=krst, cx=cos2, n0=n0:
                         v.tensor_tensor(o[0:32, 0:512],
                                         a[0:32, n0:n0 + 512],
                                         cx[0:32, n0:n0 + 512], ALU.mult)),
                  [kr_ev, rot_last])
        v2 = c.op("v", P(lambda v, o=rtmp2, pb=c.pb[k2], sx=sin2, n0=n0:
                         v.tensor_tensor(o[0:32, 0:512], pb[0:32, 0:512],
                                         sx[0:32, n0:n0 + 512], ALU.mult)),
                  (("t", trk), rot_last))
        rot_last = ("v", c.op("v", P(lambda v, o=krst, a=rtmp1, b=rtmp2,
                                     n0=n0:
                                     v.tensor_tensor(o[0:32, n0:n0 + 512],
                                                     a[0:32, 0:512],
                                                     b[0:32, 0:512],
                                                     ALU.add)),
                              (("v", v1), ("v", v2))))
        c.pguard[k2] = ("v", v2)
    krg = c.dma("y", P(lambda y, i=krst, o=io["agkr_in"]:
                       y.dma_start(out=o.ap(), in_=i[0:32, :])),
                (rot_last,), stream="mkr")
    wu_prev = [("t", 0)]

    def load_wu(w_hbm, cols):
        lds = []
        for kc in range(2):
            lds.append(c.dma("g", P(lambda g, o=wu[kc], i=w_hbm, kc=kc,
                                    cc=cols:
                                    g.dma_start(out=o[:, 0:cc],
                                                in_=i[kc * 128:(kc + 1) * 128, :])),
                             (wu_prev[0],), stream="mwu"))
        return lds[-1]

    def proj256(src, dout, evac_cb, ld):
        for oc in range(dout // 128):
            for piece in range(4):
                n0 = piece * 512
                k = bank()
                td = 0
                for kc in range(2):
                    w = [cq_done, ld]
                    if kc == 0:
                        w.append(c.pguard[k])
                    td = c.op("t", P(lambda t, pb=c.pb[k], lh=wu[kc],
                                     o0=oc * 128, r=src[kc], n0=n0,
                                     st0=(kc == 0), sp=(kc == 1):
                                     t.matmul(pb[:, :], lh[:, o0:o0 + 128],
                                              r[:, n0:n0 + 512],
                                              start=st0, stop=sp)), w)
                c.pguard[k] = evac_cb(oc, n0, k, ("t", td))
        wu_prev[0] = ("t", c.now("t"))

    ld = load_wu(io["m_wuq"], D)
    proj256(cqf, D, lambda oc, n0, k, tg: ("s", c.op(
        "s", P(lambda s, o=qcf[oc], pb=c.pb[k], n0=n0:
               s.activation(o[:, n0:n0 + 512], pb[:, :], AF.Identity)),
        (tg,))), ld)
    ld = load_wu(io["m_wqr"], 512)
    proj256(cqf, 512, lambda oc, n0, k, tg: ("s", c.op(
        "s", P(lambda s, o=qrf[oc], pb=c.pb[k], n0=n0:
               s.activation(o[:, n0:n0 + 512], pb[:, :], AF.Identity)),
        (tg,))), ld)
    qr_ev = ("s", c.now("s"))
    rp_prev = ("v", c.now("v"))
    for oc in range(4):
        for piece in range(4):
            n0 = piece * 512
            k2 = bank()
            trq = c.op("t", P(lambda t, pb=c.pb[k2], lh=rt2, r=qrf[oc],
                              n0=n0:
                              t.matmul(pb[:, :], lh[:, :],
                                       r[:, n0:n0 + 512], start=True,
                                       stop=True)), [qr_ev, c.pguard[k2]])
            v1 = c.op("v", P(lambda v, o=rtmp1, a=qrf[oc], cx=cos2, n0=n0:
                             v.tensor_tensor(o[:, 0:512],
                                             a[:, n0:n0 + 512],
                                             cx[:, n0:n0 + 512], ALU.mult)),
                      [qr_ev, rp_prev])
            v2 = c.op("v", P(lambda v, o=rtmp2, pb=c.pb[k2], sx=sin2, n0=n0:
                             v.tensor_tensor(o[:, 0:512], pb[:, :],
                                             sx[:, n0:n0 + 512], ALU.mult)),
                      (("t", trq), rp_prev))
            ve = c.op("v", P(lambda v, o=qrf[oc], a=rtmp1, b=rtmp2, n0=n0:
                             v.tensor_tensor(o[:, n0:n0 + 512],
                                             a[:, 0:512],
                                             b[:, 0:512], ALU.add)),
                      (("v", v1), ("v", v2)))
            c.pguard[k2] = ("v", v2)
            rp_prev = ("v", ve)
    ld = load_wu(io["m_wuk"], D)
    kst_prev = [("sd_mkg", 0)]

    def kc_evac(oc, n0, k, tg):
        ev = c.op("s", P(lambda s, o=kstage, pb=c.pb[k], n0=n0:
                         s.activation(o[:, n0:n0 + 512], pb[:, :],
                                      AF.Identity)), (tg, kst_prev[0]))
        if n0 == 3 * 512:
            kst_prev[0] = c.dma("y", P(lambda y, i=kstage, o=io["agkc_in"],
                                       oc=oc:
                                       y.dma_start(out=o.ap()[oc * 128:
                                                              (oc + 1) * 128, :],
                                                   in_=i[:])),
                                (("s", ev),), stream="mkg")
        return ("s", ev)
    proj256(ckf, D, kc_evac, ld)
    ld = load_wu(io["m_wuv"], D)
    vg_set = c.op("v", P(lambda v, o=vaug: v.memset(o[:], 1.0)),
                  (("v", c.now("v")),))
    vst_prev = ("sd_mvg", 0)
    for tc in range(16):
        bs = [bank(), bank()]
        tds = []
        for half in range(2):
            td = 0
            for kc in range(2):
                w = [cq_done, ld]
                if kc == 0:
                    w.append(c.pguard[bs[half]])
                td = c.op("t", P(lambda t, pb=c.pb[bs[half]], lh=ckf[kc],
                                 tc=tc, r=wu[kc], half=half, st0=(kc == 0),
                                 sp=(kc == 1):
                                 t.matmul(pb[:, :],
                                          lh[:, tc * 128:(tc + 1) * 128],
                                          r[:, half * 512:half * 512 + 512],
                                          start=st0, stop=sp)), w)
            tds.append(td)
        evl = 0
        for h in range(16):
            half, col = h // 8, (h % 8) * 64
            evl = c.op("v", P(lambda v, o=vaug, pb=c.pb[bs[half]], h=h,
                              col=col:
                              v.tensor_scalar(o[:, h * 65:h * 65 + 64],
                                              pb[:, col:col + 64], 1.0,
                                              None, ALU.mult)),
                       [("t", tds[half]), ("v", vg_set), vst_prev])
        c.pguard[bs[0]] = ("v", evl)
        c.pguard[bs[1]] = ("v", evl)
        vst_prev = c.dma("y", P(lambda y, i=vaug, o=io["agv2_in"], tc=tc:
                                y.dma_start(out=o.ap()[tc * 128:(tc + 1) * 128, :],
                                            in_=i[:])),
                         (("v", evl),), stream="mvg")
    agkc = c.collective("AllGather", ALU.bypass, io["agkc_in"],
                        io["agkc_out"], (kst_prev[0],))
    agkr = c.collective("AllGather", ALU.bypass, io["agkr_in"],
                        io["agkr_out"], (krg,))
    agv2 = c.collective("AllGather", ALU.bypass, io["agv2_in"],
                        io["agv2_out"], (vst_prev,))
    proj_fence = (("v", c.now("v")), ("t", c.now("t")), ("s", c.now("s")),
                  kst_prev[0], vst_prev, krg)
    psc.close()
    attn = sb["xn"]
    Kt = sc.t("mK", [96, S], BF16)
    Kd = [sc.t(f"mKd{i}", [96, 512], BF16) for i in range(1)]
    Vt = [sc.t(f"mV{i}", [128, 64 * 65], BF16) for i in range(2)]
    Vd = [sc.t(f"mVd{i}", [128, 4 * 65], BF16) for i in range(1)]
    qh = [sc.t(f"mqh{i}", [96, 1024], BF16) for i in range(2)]
    pts = [sc.t(f"mpts{i}", [128, 512], BF16) for i in range(6)]
    oun = sc.t("moun", [128, 512], F32)
    rcp = sc.t("mrcp", [1, 512], F32)
    gates = sc.t("mgates", [128, 128], F32)
    gl = c.dma("y", P(lambda y, o=gates, i=io["mgate"]:
                      y.dma_start(out=o[:], in_=i)), proj_fence, stream="mgl")
    agvr = io["agv2_out"].ap().rearrange("(r b j p) cc -> r b p j cc", r=8,
                                         b=2, j=8, p=128)
    agvloc = io["agv2_in"].ap().rearrange("(sl j p) cc -> sl p j cc", sl=4,
                                          j=4, p=128)
    SC96 = 1.0 / float(np.sqrt(96.0))
    CLIP_HI = float(np.exp(np.float64(80.0) * np.float64(np.float32(SC96))))
    CLIP_LO = float(np.exp(np.float64(-80.0) * np.float64(np.float32(SC96))))
    kv_free = [("t", 0), ("t", 0)]
    d_free = [("t", 0)] * 1
    mu_prev = ("v", 0)
    bhs = [(b, h) for b in range(B) for h in range(16)]
    kvld = {}

    def load_k_l1b(idx, kwait):
        b, h = bhs[idx]
        lks = {}
        for r in range(NC):
            c.dma("y", P(lambda y, o=Kt, i=io["agkc_out"], r=r, h=h,
                         b=b:
                         y.dma_start(out=o[0:64,
                                          r * 1024:(r + 1) * 1024],
                                     in_=i.ap()[r * 1024 + (h // 2) * 128
                                                + (h % 2) * 64:
                                                r * 1024 + (h // 2) * 128
                                                + (h % 2) * 64 + 64,
                                                b * 1024:(b + 1) * 1024])),
                  [agkc, kwait] + (list(proj_fence) if idx == 0
                                   else []), stream=f"mkl{r}")
            lks[r] = c.dma("y", P(lambda y, o=Kt, i=io["agkr_out"], r=r, b=b:
                                  y.dma_start(out=o[64:96,
                                                   r * 1024:(r + 1) * 1024],
                                              in_=i.ap()[r * 32:r * 32 + 32,
                                                         b * 1024:(b + 1)
                                                         * 1024])),
                           [agkr], stream=f"mkl{r}")
        kvld_k[idx] = lks

    def load_kv_l1b(idx):
        b, h = bhs[idx]
        pr = idx % 2
        lv = None
        for r in range(NC):
            lv = c.dma("y", P(lambda y, o=Vt[pr], i=agvr, r=r, h=h, b=b:
                              y.dma_start(out=o[:, r * 520:(r + 1) * 520]
                                          .rearrange("p (j cc) -> p j cc",
                                                     j=8),
                                          in_=i[r, b][:, :,
                                                      h * 65:(h + 1) * 65])),
                       [agv2, kv_free[pr]] + (list(proj_fence) if idx == 0
                                              else []), stream=f"mvl{pr}")
        qc1 = c.op("v", P(lambda v, o=qh[pr], i=qcf[h // 2],
                          row=(h % 2) * 64, b=b:
                          v.tensor_scalar(o[0:64, :],
                                          i[row:row + 64,
                                            b * 1024:(b + 1) * 1024],
                                          1.0, None, ALU.mult)),
                   [("v", c.now("v")), kv_free[pr]])
        qc2 = c.op("v", P(lambda v, o=qh[pr], i=qrf[h // 4],
                          qr=(h % 4) * 32, b=b:
                          v.tensor_scalar(o[64:96, :],
                                          i[qr:qr + 32,
                                            b * 1024:(b + 1) * 1024],
                                          1.0, None, ALU.mult)),
                   (("v", qc1),))
        kvld[idx] = (lv, ("v", qc2))
    kvld_k = {}

    def load_diag(idx2, b, h, sl):
        slot = 2 * b + sl
        dp = 0
        lkd = c.dma("y", P(lambda y, o=Kd[dp], i=io["agkc_in"], h=h,
                           slot=slot:
                           y.dma_start(out=o[0:64, :],
                                       in_=i.ap()[(h // 2) * 128
                                                  + (h % 2) * 64:
                                                  (h // 2) * 128
                                                  + (h % 2) * 64 + 64,
                                                  slot * 512:
                                                  slot * 512 + 512])),
                    [d_free[dp]], stream=f"mdl{dp}")
        lkd = c.dma("y", P(lambda y, o=Kd[dp], i=io["agkr_in"], slot=slot:
                           y.dma_start(out=o[64:96, :],
                                       in_=i.ap()[0:32,
                                                  slot * 512:
                                                  slot * 512 + 512])),
                    (), stream=f"mdl{dp}")
        lvd = c.dma("y", P(lambda y, o=Vd[dp], i=agvloc, h=h, slot=slot:
                           y.dma_start(out=o[:].rearrange(
                               "p (j cc) -> p j cc", j=4),
                               in_=i[slot][:, :, h * 65:(h + 1) * 65])),
                    (), stream=f"mdl{dp}")
        return dp, lvd

    load_kv_l1b(0)
    load_k_l1b(0, ("t", 0))
    dql = {}
    idx2 = 0
    for b0_, h0_ in bhs[:1]:
        pass
    dql[(0, 0)] = load_diag(0, bhs[0][0], bhs[0][1], 0)
    nd = 1
    for idx, (b, h) in enumerate(bhs):
        oc_h, row = h // 2, (h % 2) * 64
        pr = idx % 2
        lv, qc2 = kvld[idx]
        lks = kvld_k[idx]
        if idx + 1 < len(bhs):
            load_kv_l1b(idx + 1)
        for sl in range(2):
            n0q = sl * 512
            nchunk = 32 if sl == 0 else 64
            ko = bank()
            pbo = c.pb[ko]
            dp, lvd = dql[(idx, sl)]
            av_init = c.now("t")
            avh = {}
            DEFER = 5
            pendq = []
            for kc in range(nchunk + 4):
                diag = kc >= nchunk
                ksb = bank()
                while ksb == ko:
                    ksb = bank()
                if not diag:
                    if sl == 0:
                        kcol = (kc // 4) * 1024 + (kc % 4) * 128
                        vcol = ((kc // 4) * 8 + (kc % 4)) * 65
                    else:
                        kcol = kc * 128
                        vcol = kc * 65
                    tmm = c.op("t", P(lambda t, pb=c.pb[ksb], K=Kt,
                                      kcol=kcol, q=qh[pr], n0=n0q:
                                      t.matmul(pb[:, :],
                                               K[:, kcol:kcol + 128],
                                               q[:, n0:n0 + 512],
                                               start=True, stop=True)),
                               [lks[kcol // 1024], lv, qc2,
                                c.pguard[ksb]])
                else:
                    j = kc - nchunk
                    vcol = None
                    tmm = c.op("t", P(lambda t, pb=c.pb[ksb], K=Kd[dp], j=j,
                                      q=qh[pr], n0=n0q:
                                      t.matmul(pb[:, :],
                                               K[:, j * 128:(j + 1) * 128],
                                               q[:, n0:n0 + 512],
                                               start=True, stop=True)),
                               [lvd, qc2, c.pguard[ksb]])
                ptb = kc % 6
                ex = c.op("s", P(lambda s, o=pts[ptb], pb=c.pb[ksb]:
                                 s.activation(o[:], pb[:, :], AF.Exp,
                                              scale=SC96)),
                          [("t", tmm), ("t", avh.get(kc - 6, av_init))])
                cl = c.op("v", P(lambda v, o=pts[ptb]:
                                 v.tensor_scalar(o[:], o[:], CLIP_HI,
                                                 CLIP_LO, ALU.min,
                                                 ALU.max)), (("s", ex),))
                if not diag:
                    gt = c.op("v", P(lambda v, o=pts[ptb], g=gates, sl=sl,
                                     kc=kc:
                                     v.tensor_scalar(o[:], o[:],
                                                     g[:, sl * 64 + kc:
                                                       sl * 64 + kc + 1],
                                                     None, ALU.mult)),
                              (("v", cl), gl))
                else:
                    j = kc - nchunk
                    gt = c.op("v", P(lambda v, o=pts[ptb], m=masks2, j=j:
                                     v.tensor_tensor(o[:], o[:],
                                                     m[:, j * 512:
                                                       (j + 1) * 512],
                                                     ALU.mult)),
                              (("v", cl),))
                c.pguard[ksb] = ("v", gt)
                pendq.append((ptb, gt, kc, diag, vcol))
                if len(pendq) > DEFER:
                    p_ptb, p_gt, p_kc, p_diag, p_vcol = pendq.pop(0)
                    w = [("v", p_gt)]
                    if p_kc == 0:
                        w.append(c.pguard[ko])
                    Vsrc = Vt[pr] if not p_diag else Vd[dp]
                    vc2 = p_vcol if not p_diag else (p_kc - nchunk) * 65
                    avh[p_kc] = c.op("t", P(lambda t, pb=pbo, V=Vsrc, vc=vc2,
                                            r=pts[p_ptb], st0=(p_kc == 0):
                                            t.matmul(pb[0:65, :],
                                                     V[:, vc:vc + 65],
                                                     r[:, :], start=st0,
                                                     stop=False)),
                                     w + ([lvd] if p_diag else [lv]))
            for fi, (p_ptb, p_gt, p_kc, p_diag, p_vcol) in enumerate(pendq):
                Vsrc = Vt[pr] if not p_diag else Vd[dp]
                vc2 = p_vcol if not p_diag else (p_kc - nchunk) * 65
                av_last = c.op("t", P(lambda t, pb=pbo, V=Vsrc, vc=vc2,
                                      r=pts[p_ptb], st0=(p_kc == 0),
                                      sp=(fi == len(pendq) - 1):
                                      t.matmul(pb[0:65, :], V[:, vc:vc + 65],
                                               r[:, :], start=st0,
                                               stop=sp)),
                               [("v", p_gt)] + ([lvd] if p_diag else [lv]))
            pendq = []
            d_free[dp] = ("t", av_last)
            nxt = (idx, 1) if sl == 0 else (idx + 1, 0)
            if nxt[0] < len(bhs):
                dql[nxt] = load_diag(nd, bhs[nxt[0]][0], bhs[nxt[0]][1],
                                     nxt[1])
                nd += 1
            ev = c.op("s", P(lambda s, o=oun, pb=pbo:
                             s.activation(o[0:65, :],
                                          pb[0:65, :], AF.Identity)),
                      [("t", av_last), mu_prev])
            c.pguard[ko] = ("s", ev)
            rc = c.op("v", P(lambda v, o=rcp, i=oun:
                             v.reciprocal(o[0:1, :], i[64:65, :])),
                      [("s", ev), ("v", c.now("v")), ("t", c.now("t"))])
            kb = bank()
            while kb == ko:
                kb = bank()
            tb2 = c.op("t", P(lambda t, pb=c.pb[kb], lh=sb["ones1_f"],
                              r=rcp:
                              t.matmul(pb[0:64, :], lh[0:1, 0:64],
                                       r[0:1, :], start=True,
                                       stop=True)),
                       [("v", rc), c.pguard[kb]])
            mu = c.op("v", P(lambda v, o=attn[oc_h], row=row, a=oun,
                             pb=c.pb[kb], b=b, n0=n0q:
                             v.tensor_tensor(o[row:row + 64,
                                               b * 1024 + n0:
                                               b * 1024 + n0 + 512],
                                             a[0:64, :],
                                             pb[0:64, :], ALU.mult)),
                      (("t", tb2),))
            c.pguard[kb] = ("v", mu)
            mu_prev = ("v", mu)
        kv_free[pr] = ("t", c.now("t"))
        if idx + 1 < len(bhs):
            load_k_l1b(idx + 1, ("t", c.now("t")))
    attn_done = ("v", c.now("v"))
    wo2 = [sc.t(f"mwo{i}", [128, D], BF16) for i in range(2)]
    wo_prev = [("t", 0), ("t", 0)]
    for piece in range(4):
        n0 = piece * 512
        td = 0
        for kc in range(DCH):
            wb = kc % 2
            ld = c.dma("g", P(lambda g, o=wo2[wb], i=io["m_wo"], kc=kc:
                              g.dma_start(out=o[:],
                                          in_=i[kc * 128:(kc + 1) * 128, :])),
                       (wo_prev[wb],), stream=f"mwo{wb}")
            for oc in range(DCH):
                w = [attn_done, ld]
                if kc == 0:
                    w.append(c.pguard[oc])
                td = c.op("t", P(lambda t, pb=c.pb[oc], lh=wo2[wb],
                                 o0=oc * 128, r=attn[kc], n0=n0,
                                 st0=(kc == 0), sp=(kc == DCH - 1):
                                 t.matmul(pb[:, :], lh[:, o0:o0 + 128],
                                          r[:, n0:n0 + 512], start=st0,
                                          stop=sp)), w)
            wo_prev[wb] = ("t", td)
        for oc in range(DCH):
            ev = c.op("v", P(lambda v, o=sb["x"][oc], pb=c.pb[oc], n0=n0:
                             v.tensor_tensor(o[:, n0:n0 + 512], pb[:, :],
                                             o[:, n0:n0 + 512], ALU.add)),
                      (("t", td),))
            c.pguard[oc] = ("v", ev)
    fence(c)
    sc.close()


def emit_l1_attn_a(c, sbp, io):
    sc = Scope(c.nc)
    sb = dict(sbp)
    gdc = sc.t("agdc", [128, 8], F32)
    mixc = sc.t("amixc", [128, 8], F32)
    omixc = sc.t("aomixc", [128, 8], F32)
    selt = sc.t("aselt", [128, 64], F32)
    sel8 = sc.t("asel8", [128, 512], I32)
    bidxt = sc.t("abidxt", [128, 4], I32)
    stidxt = sc.t("astidxt", [128, 32], I32)
    bxn = sc.t("abxn", [128, 32], BF16)
    sb["p3t"] = sc.t("lap3t", [128, 512], F32)
    sb["tmpx"] = sc.t("latmpx", [128, 512], F32)
    xmsc = Scope(c.nc)
    xm = [xmsc.t(f"axm{i}", [128, TOK], BF16) for i in range(DCH)]
    xnsc = Scope(c.nc)
    sb["xn"] = [xnsc.t(f"laxn{i}", [128, TOK], BF16) for i in range(DCH)]
    lnp = Scope(c.nc)
    sb.update(alloc_ln_pack(lnp))
    ln_done = emit_ln(c, sb, "plain")
    ln_fence = (("v", c.now("v")), ("t", c.now("t")), ("s", c.now("s")))
    lnp.close()
    c.op("v", P(lambda v, o=sb["p3t"]: v.memset(o[:], 0.3)))
    gml = c.dma("y", P(lambda y, o=gdc, i=io["la_gd"]:
                       y.dma_start(out=o[:], in_=i)), ln_fence, stream="lac")
    c.dma("y", P(lambda y, o=mixc, i=io["la_mix"]:
                 y.dma_start(out=o[:], in_=i)), stream="lac")
    sl_l = c.dma("y", P(lambda y, o=selt, i=io["la_sel"]:
                        y.dma_start(out=o[:], in_=i)), stream="lac")
    sl8 = c.dma("y", P(lambda y, o=sel8, i=io["la_sel8"]:
                       y.dma_start(out=o[:], in_=i)), stream="lac")
    bxl = c.dma("y", P(lambda y, o=bidxt, i=io["la_bidx"]:
                       y.dma_start(out=o[:], in_=i)), stream="laidx")
    stl = c.dma("y", P(lambda y, o=stidxt, i=io["la_stidx"]:
                       y.dma_start(out=o[:], in_=i)), stream="laidx")
    gde = c.op("s", P(lambda s, o=gdc, i=gdc:
                      s.activation(o[:], i[:], AF.Exp)), (sl8,))
    omx = c.op("v", P(lambda v, o=omixc, i=mixc:
                      v.tensor_scalar(o[:], i[:], -1.0, 1.0, ALU.mult,
                                      ALU.add)), (sl8,))
    bst = None
    for j in range(4):
        for ch in range(DCH):
            bst = c.dma("y", P(lambda y, i=sb["xn"][ch], o=io["agbx_in"],
                               j=j, ch=ch:
                               y.dma_start(out=o.ap()[j * 128:(j + 1) * 128,
                                                      ch:ch + 1],
                                           in_=i[:, j * 512 + 511:j * 512 + 512],
                                           allow_slow_non_contiguous=True)),
                        (("v", ln_done),) if (j, ch) == (0, 0) else (),
                        stream="labx")
    agbx = c.collective("AllGather", ALU.bypass, io["agbx_in"],
                        io["agbx_out"], (bst,))
    bxm = c.op("v", P(lambda v, o=bxn: v.memset(o[:], 0.0)))
    bxg = None
    for j in range(4):
        bxg = c.dma("g", P(lambda g, o=bxn, i=io["agbx_out"], ii=bidxt,
                           j=j:
                           g.indirect_dma_start(
                               out=o[:, j * 8:(j + 1) * 8], out_offset=None,
                               in_=i.ap(),
                               in_offset=IndirectOffsetOnAxis(
                                   ap=ii[:, j:j + 1], axis=0),
                               bounds_check=4095, oob_is_err=False)),
                    [agbx, ("v", bxm), stl], stream="labxg")
    xm_done = 0
    for ch in range(DCH):
        for j in range(4):
            t1 = c.op("v", P(lambda v, o=sb["tmpx"], i=sb["xn"][ch], mx=mixc,
                             ch=ch, j=j:
                             v.tensor_scalar(o[:, 0:512],
                                             i[:, j * 512:(j + 1) * 512],
                                             mx[:, ch:ch + 1], None,
                                             ALU.mult)),
                      [("v", ln_done), ("v", omx), ("v", xm_done)])
            x2 = c.op("v", P(lambda v, o=xm[ch], i=sb["xn"][ch], om=omixc,
                             t=sb["tmpx"], ch=ch, j=j:
                             v.scalar_tensor_tensor(
                                 o[:, j * 512 + 1:(j + 1) * 512],
                                 i[:, j * 512:(j + 1) * 512 - 1],
                                 om[:, ch:ch + 1], t[:, 1:512],
                                 ALU.mult, ALU.add)), (("v", t1),))
            xm_done = c.op("v", P(lambda v, o=xm[ch], bx=bxn, om=omixc,
                                  t=sb["tmpx"], ch=ch, j=j:
                                  v.scalar_tensor_tensor(
                                      o[:, j * 512:j * 512 + 1],
                                      bx[:, j * 8 + ch:j * 8 + ch + 1],
                                      om[:, ch:ch + 1], t[:, 0:1],
                                      ALU.mult, ALU.add)),
                           (("v", x2), bxg))
    xm_fence = (("v", c.now("v")),)
    xnsc.close()
    prsc = Scope(c.nc)
    wres = [prsc.t(f"law{i}", [128, D], BF16) for i in range(DCH)]
    ev1 = prsc.t("laev1", [128, 512], F32)
    mats = [("la_wtdT", "scr_z", "z"), ("la_wkT", "scr_ek", "ek"),
            ("la_wvT", "scr_v", "v"), ("la_wrT", "scr_r", "r")]
    bki = [0]

    def bank():
        k = bki[0] % 8
        bki[0] += 1
        return k

    st_prev = ("v", 0)
    for wkey, skey, kind in mats:
        lds = []
        for kc in range(DCH):
            lds.append(c.dma("g", P(lambda g, o=wres[kc], i=io[wkey], kc=kc:
                             g.dma_start(out=o[:],
                                         in_=i[kc * 128:(kc + 1) * 128, :])),
                             ((("t", c.now("t")),) + xm_fence) if kc == 0
                             else (), stream="law"))
        ld = lds[-1]
        sprev = ("s", 0)
        for oc in range(DCH):
            for piece in range(4):
                n0 = piece * 512
                k = bank()
                td = 0
                for kc in range(DCH):
                    w = [ld]
                    if kc == 0:
                        w.append(c.pguard[k])
                    td = c.op("t", P(lambda t, pb=c.pb[k], lh=wres[kc],
                                     o0=oc * 128, r=xm[kc], n0=n0,
                                     st0=(kc == 0), sp=(kc == DCH - 1):
                                     t.matmul(pb[:, :], lh[:, o0:o0 + 128],
                                              r[:, n0:n0 + 512], start=st0,
                                              stop=sp)), w)
                if kind == "r":
                    ev = c.op("s", P(lambda s, o=ev1, pb=c.pb[k]:
                                     s.activation(o[:, 0:512], pb[:, :],
                                                  AF.Sigmoid)),
                              [("t", td), st_prev, sprev])
                elif kind == "ek":
                    evv = c.op("v", P(lambda v, o=ev1, pb=c.pb[k]:
                                      v.tensor_scalar(o[:, 0:512], pb[:, :],
                                                      10.0, None, ALU.min)),
                               [("t", td), st_prev])
                    ev = c.op("s", P(lambda s, o=ev1, i=ev1:
                                     s.activation(o[:, 0:512], i[:, 0:512],
                                                  AF.Exp)),
                              [("v", evv), sprev])
                else:
                    ev = c.op("s", P(lambda s, o=ev1, pb=c.pb[k]:
                                     s.activation(o[:, 0:512], pb[:, :],
                                                  AF.Identity)),
                              [("t", td), st_prev, sprev])
                sprev = ("s", ev)
                c.pguard[k] = ("s", ev)
                st_prev = c.dma("y", P(lambda y, i=ev1, o=io[skey], oc=oc,
                                       n0=n0:
                                       y.dma_start(out=o.ap()[oc * 128:
                                                              (oc + 1) * 128,
                                                              n0:n0 + 512],
                                                   in_=i[:, 0:512])),
                                (("s", ev),), stream="lasc")
    proj_fence = (("v", c.now("v")), ("t", c.now("t")), ("s", c.now("s")),
                  st_prev)
    prsc.close()
    xmsc.close()
    a2 = Scope(c.nc)
    ztb = [a2.t(f"lazt{i}", [128, 512], F32) for i in range(2)]
    ektb = [a2.t(f"laekt{i}", [128, 512], F32) for i in range(2)]
    vtb = [a2.t(f"lavt{i}", [128, 512], F32) for i in range(2)]
    ekvtb = [a2.t(f"laekvt{i}", [128, 512], F32) for i in range(2)]
    ctb = [a2.t(f"lact{i}", [128, 512], F32) for i in range(2)]
    tdtb = [a2.t(f"latdt{i}", [128, 512], F32) for i in range(2)]
    atb = [a2.t(f"laat{i}", [128, 512], F32) for i in range(2)]
    btb = [a2.t(f"labt{i}", [128, 512], F32) for i in range(2)]
    onesg = a2.t("laones", [128, 512], F32)
    sums = a2.t("lasums", [128, 96], F32)
    c.op("v", P(lambda v, o=onesg: v.memset(o[:], 1.0)), proj_fence)
    rdv = [("v", 0), ("v", 0)]
    rds = [("s", 0), ("s", 0)]
    ldq = {}

    def a2_load(it):
        j, oc = divmod(it, DCH)
        bf = it % 2
        w0 = [rdv[bf], rds[bf]] + (list(proj_fence) if it < 2 else [])
        l1 = c.dma("g", P(lambda g, o=ztb[bf], i=io["scr_z"], oc=oc, j=j:
                          g.dma_start(out=o[:],
                                      in_=i.ap()[oc * 128:(oc + 1) * 128,
                                                 j * 512:(j + 1) * 512])),
                   w0, stream=f"la2z{bf}")
        l2 = c.dma("g", P(lambda g, o=ektb[bf], i=io["scr_ek"], oc=oc, j=j:
                          g.dma_start(out=o[:],
                                      in_=i.ap()[oc * 128:(oc + 1) * 128,
                                                 j * 512:(j + 1) * 512])),
                   w0, stream=f"la2e{bf}")
        l3 = c.dma("g", P(lambda g, o=vtb[bf], i=io["scr_v"], oc=oc, j=j:
                          g.dma_start(out=o[:],
                                      in_=i.ap()[oc * 128:(oc + 1) * 128,
                                                 j * 512:(j + 1) * 512])),
                   w0, stream=f"la2v{bf}")
        ldq[it] = (l1, l2, l3)

    a2_load(0)
    gb1 = [("v", 0), ("v", 0)]
    gs1 = [("v", 0), ("v", 0)]
    gs2 = [("v", 0), ("v", 0)]
    gs3 = [("s", 0), ("s", 0)]
    for it in range(32):
        j, oc = divmod(it, DCH)
        bf = it % 2
        zt, ekt, vt = ztb[bf], ektb[bf], vtb[bf]
        ekvt, ct, tdt, at, bt = (ekvtb[bf], ctb[bf], tdtb[bf], atb[bf],
                                 btb[bf])
        l1, l2, l3 = ldq.pop(it)
        if it + 1 < 32:
            a2_load(it + 1)
        td1 = c.op("s", P(lambda s, o=tdt, i=zt:
                          s.activation(o[:], i[:], AF.Exp)),
                   [l1, gb1[bf]] + ([("s", c.now("s"))] if it < 2 else []))
        ekv1 = c.op("v", P(lambda v, o=ekvt, a=ekt, b=vt:
                           v.tensor_tensor(o[:], a[:], b[:], ALU.mult)),
                    [l2, l3, gb1[bf]] + ([("v", c.now("v"))] if it < 2
                                         else []))
        c1 = c.op("v", P(lambda v, o=ct, d0=onesg, d1=zt:
                         v.tensor_tensor_scan(o[:], d0[:], d1[:], 0.0,
                                              ALU.mult, ALU.add)),
                  [l1, gs3[bf]])
        a1 = c.op("v", P(lambda v, o=at, d0=tdt, d1=ekvt:
                         v.tensor_tensor_scan(o[:], d0[:], d1[:], 0.0,
                                              ALU.mult, ALU.add)),
                  [("s", td1), ("v", ekv1), gs1[bf]])
        b1 = c.op("v", P(lambda v, o=bt, d0=tdt, d1=ekt:
                         v.tensor_tensor_scan(o[:], d0[:], d1[:], 0.0,
                                              ALU.mult, ALU.add)),
                  [("s", td1), l2, gs2[bf]])
        rdv[bf] = ("v", b1)
        rds[bf] = ("s", td1)
        if True:
            s1 = c.op("v", P(lambda v, o=sums, i=at, col=j * 8 + oc:
                             v.tensor_scalar(o[:, col:col + 1],
                                             i[:, 511:512], 1.0, None,
                                             ALU.mult)), (("v", a1),))
            s2 = c.op("v", P(lambda v, o=sums, i=bt, col=32 + j * 8 + oc:
                             v.tensor_scalar(o[:, col:col + 1],
                                             i[:, 511:512], 1.0, None,
                                             ALU.mult)), (("v", b1),))
            s3 = c.op("s", P(lambda s, o=sums, i=ct, col=64 + j * 8 + oc:
                             s.activation(o[:, col:col + 1], i[:, 511:512],
                                          AF.Exp)), [("v", c1)])
            gb1[bf] = ("v", b1)
            gs1[bf] = ("v", s1)
            gs2[bf] = ("v", s2)
            gs3[bf] = ("s", s3)
    sum_done = [("v", c.now("v")), ("s", c.now("s"))]
    sst = c.dma("y", P(lambda y, i=sums, o=io["agsum_in"]:
                       y.dma_start(out=o.ap(), in_=i[:])),
                sum_done, stream="lasum")
    agsum = c.collective("AllGather", ALU.bypass, io["agsum_in"],
                         io["agsum_out"], (sst,))
    sumsb = a2.t("lasumsb", [128, 8 * 96], F32)
    acar = a2.t("laacar", [128, 8], F32)
    bcar = a2.t("labcar", [128, 8], F32)
    mya = a2.t("lamya", [128, 64], F32)
    lsum = c.dma("y", P(lambda y, o=sumsb, i=io["agsum_out"]:
                        y.dma_start(out=o[:].rearrange("p (r q) -> p r q",
                                                       r=8),
                                    in_=i.ap().rearrange("(r p) q -> p r q",
                                                         p=128))),
                 (agsum,), stream="lasum")
    vp = c.op("v", P(lambda v, o=mya: v.memset(o[:], 0.0)), (lsum,))
    for b in range(B):
        vp = c.op("v", P(lambda v, o=acar: v.memset(o[:], 0.0)), (("v", vp),))
        vp = c.op("v", P(lambda v, o=bcar: v.memset(o[:], 0.0)), (("v", vp),))
        for sg in range(NSEG):
            own, slot = owner_of(b, sg)
            base = own * 96 + slot * 8
            for j in (2 * b, 2 * b + 1):
                vp = c.op("v", P(lambda v, o=mya, sel=sel8, a=acar, sg=sg,
                                 j=j:
                                 v.copy_predicated(
                                     o[:, j * 8:(j + 1) * 8],
                                     sel[:, (sg * 4 + j) * 8:
                                         (sg * 4 + j) * 8 + 8],
                                     a[:, 0:8])),
                          [("v", vp), sl8])
                vp = c.op("v", P(lambda v, o=mya, sel=sel8, a=bcar, sg=sg,
                                 j=j:
                                 v.copy_predicated(
                                     o[:, 32 + j * 8:32 + (j + 1) * 8],
                                     sel[:, (sg * 4 + j) * 8:
                                         (sg * 4 + j) * 8 + 8],
                                     a[:, 0:8])), (("v", vp),))
            vp = c.op("v", P(lambda v, o=acar, d=sumsb, base=base,
                             own=own:
                             v.tensor_tensor(o[:, 0:8],
                                             d[:, own * 96 + 64 + (base - own * 96):
                                               own * 96 + 64 + (base - own * 96) + 8],
                                             o[:, 0:8], ALU.mult)),
                      (("v", vp),))
            vp = c.op("v", P(lambda v, o=acar, d=sumsb, base=base:
                             v.tensor_tensor(o[:, 0:8], d[:, base:base + 8],
                                             o[:, 0:8], ALU.add)),
                      (("v", vp),))
            vp = c.op("v", P(lambda v, o=bcar, d=sumsb, base=base,
                             own=own:
                             v.tensor_tensor(o[:, 0:8],
                                             d[:, own * 96 + 64 + (base - own * 96):
                                               own * 96 + 64 + (base - own * 96) + 8],
                                             o[:, 0:8], ALU.mult)),
                      (("v", vp),))
            vp = c.op("v", P(lambda v, o=bcar, d=sumsb, base=base:
                             v.tensor_tensor(o[:, 0:8],
                                             d[:, base - slot * 8 + 32 + slot * 8:
                                               base - slot * 8 + 32 + slot * 8 + 8],
                                             o[:, 0:8], ALU.add)),
                      (("v", vp),))
    carry_done = ("v", c.now("v"))
    rt_b = [a2.t(f"lart{i}", [128, 512], F32) for i in range(2)]
    xintb = [a2.t(f"laxin{i}", [128, 512], F32) for i in range(2)]
    numtb = [a2.t(f"lanum{i}", [128, 512], F32) for i in range(2)]
    strips = a2.t("lastrips", [128, 32 * 65], F32)
    xsth = [("sd_laxs0", 0), ("sd_laxs1", 0)]
    rdv = [carry_done, carry_done]
    rds = [("s", c.now("s")), ("s", c.now("s"))]
    ldq = {}

    def a3_load(it):
        j, oc = divmod(it, DCH)
        bf = it % 2
        w0 = [rdv[bf], rds[bf]] + ([carry_done] if it < 2 else [])
        l1 = c.dma("g", P(lambda g, o=ztb[bf], i=io["scr_z"], oc=oc, j=j:
                          g.dma_start(out=o[:],
                                      in_=i.ap()[oc * 128:(oc + 1) * 128,
                                                 j * 512:(j + 1) * 512])),
                   w0, stream=f"la2z{bf}")
        l2 = c.dma("g", P(lambda g, o=ektb[bf], i=io["scr_ek"], oc=oc, j=j:
                          g.dma_start(out=o[:],
                                      in_=i.ap()[oc * 128:(oc + 1) * 128,
                                                 j * 512:(j + 1) * 512])),
                   w0, stream=f"la2e{bf}")
        l3 = c.dma("g", P(lambda g, o=vtb[bf], i=io["scr_v"], oc=oc, j=j:
                          g.dma_start(out=o[:],
                                      in_=i.ap()[oc * 128:(oc + 1) * 128,
                                                 j * 512:(j + 1) * 512])),
                   w0, stream=f"la2v{bf}")
        l4 = c.dma("g", P(lambda g, o=rt_b[bf], i=io["scr_r"], oc=oc, j=j:
                          g.dma_start(out=o[:],
                                      in_=i.ap()[oc * 128:(oc + 1) * 128,
                                                 j * 512:(j + 1) * 512])),
                   w0, stream=f"la2r{bf}")
        ldq[it] = (l1, l2, l3, l4)

    a3_load(0)
    gb1 = [("v", 0), ("v", 0)]
    gnm = [("v", 0), ("v", 0)]
    gb2c = [("v", 0), ("v", 0)]
    gce = [("s", 0), ("s", 0)]
    gwk = [("v", 0), ("v", 0)]
    gxi = [("v", 0), ("v", 0)]
    glp = [("v", 0), ("v", 0)]
    gst = [("v", 0), ("v", 0)]
    for it in range(32):
        j, oc = divmod(it, DCH)
        jo = j * 8 + oc
        bf = it % 2
        zt, ekt, vt, rt_ = ztb[bf], ektb[bf], vtb[bf], rt_b[bf]
        ekvt, ct, tdt, at, bt = (ekvtb[bf], ctb[bf], tdtb[bf], atb[bf],
                                 btb[bf])
        xint, numt = xintb[bf], numtb[bf]
        l1, l2, l3, l4 = ldq.pop(it)
        if it + 1 < 32:
            a3_load(it + 1)
        if True:
            td1 = c.op("s", P(lambda s, o=tdt, i=zt:
                              s.activation(o[:], i[:], AF.Exp)),
                       [l1, gb1[bf]] + ([("s", c.now("s"))] if it < 2
                                        else []))
            ekv1 = c.op("v", P(lambda v, o=ekvt, a=ekt, b=vt:
                               v.tensor_tensor(o[:], a[:], b[:], ALU.mult)),
                        [l2, l3, gnm[bf]] + ([("v", c.now("v"))] if it < 2
                                             else []))
            c1 = c.op("v", P(lambda v, o=ct, d0=onesg, d1=zt:
                             v.tensor_tensor_scan(o[:], d0[:], d1[:], 0.0,
                                                  ALU.mult, ALU.add)),
                      [l1, gb2c[bf], gce[bf]])
            a1 = c.op("v", P(lambda v, o=at, d0=tdt, d1=ekvt:
                             v.tensor_tensor_scan(o[:], d0[:], d1[:], 0.0,
                                                  ALU.mult, ALU.add)),
                      [("s", td1), ("v", ekv1), gnm[bf]])
            b1 = c.op("v", P(lambda v, o=bt, d0=tdt, d1=ekt:
                             v.tensor_tensor_scan(o[:], d0[:], d1[:], 0.0,
                                                  ALU.mult, ALU.add)),
                      [("s", td1), l2, gwk[bf]])
            ce = c.op("s", P(lambda s, o=ct, i=ct:
                             s.activation(o[:], i[:], AF.Exp)),
                      [("v", c1)])
            a2c = c.op("v", P(lambda v, o=at, ec=ct, my=mya, jo=jo:
                              v.scalar_tensor_tensor(o[:], ec[:],
                                                     my[:, jo:jo + 1],
                                                     o[:], ALU.mult,
                                                     ALU.add)),
                       [("s", ce), ("v", a1)])
            b2c = c.op("v", P(lambda v, o=bt, ec=ct, my=mya, jo=jo:
                              v.scalar_tensor_tensor(o[:], ec[:],
                                                     my[:, 32 + jo:33 + jo],
                                                     o[:], ALU.mult,
                                                     ALU.add)),
                       [("s", ce), ("v", b1)])
            b3 = c.op("v", P(lambda v, o=bt:
                             v.tensor_scalar(o[:], o[:], 1e-6, None,
                                             ALU.add)), (("v", b2c),))
            nm = c.op("v", P(lambda v, o=numt, ekv=ekvt, gd=gdc, a=at,
                             oc=oc:
                             v.scalar_tensor_tensor(o[:], ekv[:],
                                                    gd[:, oc:oc + 1], a[:],
                                                    ALU.mult, ALU.add)),
                      [("v", b3), ("s", gde), gxi[bf]])
            dn = c.op("v", P(lambda v, o=bt, ek=ekt, gd=gdc, oc=oc:
                             v.scalar_tensor_tensor(o[:], ek[:],
                                                    gd[:, oc:oc + 1], o[:],
                                                    ALU.mult, ALU.add)),
                      (("v", nm),))
            rq = c.op("v", P(lambda v, o=bt: v.reciprocal(o[:], o[:])),
                      (("v", dn),))
            wk = c.op("v", P(lambda v, o=numt, b=bt:
                             v.tensor_tensor(o[:], o[:], b[:], ALU.mult)),
                      (("v", rq),))
            xi = c.op("v", P(lambda v, o=xint, r=rt_, w=numt:
                             v.tensor_tensor(o[:], r[:], w[:], ALU.mult)),
                      [("v", wk), l4, glp[bf], xsth[bf]])
            lp = c.op("v", P(lambda v, o=at, d0=sb["p3t"], d1=xint:
                             v.tensor_tensor_scan(o[:], d0[:], d1[:], 0.0,
                                                  ALU.mult, ALU.add)),
                      [("v", xi), gst[bf]])
            stx = c.op("v", P(lambda v, o=strips, i=at, jo=jo:
                              v.tensor_scalar(o[:, jo * 65:(jo + 1) * 65],
                                              i[:, 447:512], 1.0, None,
                                              ALU.mult)), (("v", lp),))
            xsth[bf] = c.dma("y", P(lambda y, i=xint, o=io["scr_xin"],
                                    oc=oc, j=j:
                                    y.dma_start(out=o.ap()[oc * 128:
                                                           (oc + 1) * 128,
                                                           j * 512:
                                                           (j + 1) * 512],
                                                in_=i[:])),
                             [("v", xi)], stream=f"laxs{bf}")
            gb1[bf] = ("v", b1)
            gnm[bf] = ("v", nm)
            gb2c[bf] = ("v", b2c)
            gce[bf] = ("s", ce)
            gwk[bf] = ("v", wk)
            gxi[bf] = ("v", xi)
            glp[bf] = ("v", lp)
            gst[bf] = ("v", stx)
            rdv[bf] = ("v", xi)
            rds[bf] = ("s", td1)
    a3_done = (("v", c.now("v")), ("s", c.now("s")), xsth[0], xsth[1])
    sts = c.dma("y", P(lambda y, i=strips, o=io["agst_in"]:
                       y.dma_start(out=o.ap().rearrange("(so p) cc -> p so cc",
                                                        p=128),
                                   in_=i[:].rearrange("p (so cc) -> p so cc",
                                                      so=32))),
                list(a3_done), stream="last")
    agst = c.collective("AllGather", ALU.bypass, io["agst_in"],
                        io["agst_out"], (sts,))
    a2.close()
    a4 = Scope(c.nc)
    wst = a4.t("lawst", [128, 32 * 65], F32)
    xint2 = a4.t("laxi2", [128, 512], F32)
    l2t = a4.t("lal2", [128, 512], F32)
    chz = a4.t("lachz", [128, 32 * 576], F32)
    spkT = a4.t("laspkT", [128, (576 - WARM) * 32], BF16)
    swarm = a4.t("laswarm", [128, 32], BF16)
    Sst = a4.t("laS", [128, 32], F32)
    mz = c.op("v", P(lambda v, o=wst: v.memset(o[:], 0.0)), list(a3_done))
    stg = None
    for jo in range(32):
        stg = c.dma("g", P(lambda g, o=wst, i=io["agst_out"],
                           ii=stidxt, jo=jo:
                           g.indirect_dma_start(
                               out=o[:, jo * 65:(jo + 1) * 65],
                               out_offset=None,
                               in_=i.ap(),
                               in_offset=IndirectOffsetOnAxis(
                                   ap=ii[:, jo:jo + 1], axis=0),
                               bounds_check=32767, oob_is_err=False)),
                    [agst, ("v", mz), stl], stream="lastg")
    ldx = ("v", 0)
    chains = [(j, oc) for j in range(4) for oc in range(DCH)]
    for ci, (j, oc) in enumerate(chains):
        jo = j * 8 + oc
        lx = c.dma("g", P(lambda g, o=xint2, i=io["scr_xin"], oc=oc,
                          j=j:
                          g.dma_start(out=o[:],
                                      in_=i.ap()[oc * 128:(oc + 1) * 128,
                                                 j * 512:(j + 1) * 512])),
                   [ldx, xsth[0], xsth[1]], stream="lax2")
        l2s = c.op("v", P(lambda v, o=l2t, d0=sb["p3t"], d1=xint2,
                          ini=wst, jo=jo:
                          v.tensor_tensor_scan(o[:], d0[:], d1[:],
                                               ini[:, jo * 65 + 64:
                                                   jo * 65 + 65],
                                               ALU.mult, ALU.add)),
                   [lx, stg, ("v", c.now("v"))])
        zm = c.op("v", P(lambda v, o=chz, i=l2t, ci=ci:
                         v.tensor_scalar(o[:, ci * 576 + 64:
                                           ci * 576 + 576],
                                         i[:], 5.0, -1.0, ALU.mult,
                                         ALU.add)), (("v", l2s),))
        zw = c.op("v", P(lambda v, o=chz, i=wst, jo=jo, ci=ci:
                         v.tensor_scalar(o[:, ci * 576:ci * 576 + 64],
                                         i[:, jo * 65:jo * 65 + 64],
                                         5.0, -1.0, ALU.mult, ALU.add)),
                  (("v", zm),))
        ldx = ("v", zw)
    sz = c.op("v", P(lambda v, o=Sst: v.memset(o[:], 0.0)), (ldx,))
    vp = ("v", sz)
    for t in range(576):
        dst = (spkT, (t - WARM) * 32) if t >= WARM else (swarm, 0)
        s1 = c.op("v", P(lambda v, o=dst[0], c0=dst[1], z=chz, S=Sst, t=t:
                         v.tensor_tensor(o[:, c0:c0 + 32],
                                         S[:],
                                         z[:, t:32 * 576:576],
                                         ALU.is_lt)), (vp,))
        vp = ("v", c.op("v", P(lambda v, o=Sst, sp_=dst[0], c0=dst[1]:
                               v.scalar_tensor_tensor(
                                   o[:], o[:], 0.3,
                                   sp_[:, c0:c0 + 32],
                                   ALU.mult, ALU.add)), (("v", s1),)))
    for ci, (j, oc) in enumerate(chains):
        vp = ("v", c.op("v", P(lambda v, o=sb["x"][oc], i=spkT, j=j,
                               ci=ci:
                               v.tensor_tensor(o[:, j * 512:(j + 1) * 512],
                                               i[:, ci:
                                                 (576 - WARM) * 32:32],
                                               o[:, j * 512:(j + 1) * 512],
                                               ALU.add)), (vp,)))
    fence(c)
    a4.close()
    sc.close()


# version 17 (final)
# speedup vs baseline: 1.0021x; 1.0021x over previous
import numpy as np
import ml_dtypes
import concourse.bass as bass
import concourse.mybir as mybir
from concourse.bass_utils import run_bass_kernel_spmd
from concourse.bass_core_rust import IndirectOffsetOnAxis

F32 = mybir.dt.float32
BF16 = mybir.dt.bfloat16
I32 = mybir.dt.int32
AF = mybir.ActivationFunctionType
ALU = mybir.AluOpType
AXX = mybir.AxisListType.X

NC = 8
B = 2
S = 8192
D = 1024
SEG = 512
NSEG = 16
TOK = 2048
DCH = 8
WARM = 64


def core_insts(i):
    return [(0, i), (0, 15 - i), (1, i), (1, 15 - i)]


def owner_of(b, sg):
    i = min(sg, 15 - sg)
    slot = (0 if sg <= 7 else 1) + 2 * b
    return i, slot


def seg_loc(b, sg):
    i, slot = owner_of(b, sg)
    return i, slot * SEG


ENGS = ("t", "v", "s", "g", "y")


class Ctx:

    def __init__(self, nc, block):
        self.nc = nc
        self.block = block
        self.q = {e: [] for e in ENGS}
        self.sem = {}
        self.n = {}
        for name in ("t", "v", "s", "g", "y"):
            self._mksem(name)
        self.dstreams = []
        self.ncc = 0
        self.pb = [nc.psum_tensor(f"pb{k}", [128, 512], F32).__enter__()
                   for k in range(8)]
        self.pguard = [("v", 0)] * 8

    def _mksem(self, name):
        h = self.nc.semaphore(name).__enter__()
        self.sem[name] = h
        self.n[name] = 0

    def ccsem(self):
        name = f"cc{self.ncc}"
        self.ncc += 1
        self._mksem(name)
        return name

    def collective(self, kind, op, in_t, out_t, waits=()):
        ccs = self.ccsem()
        w = tuple(waits)
        if getattr(self, "last_cc", None) is not None:
            w = w + (self.last_cc,)
        self.op("g", P(lambda g, i=in_t, o=out_t, k=kind, a=op:
                       g.collective_compute(k, a,
                                            replica_groups=[list(range(NC))],
                                            ins=[i.ap().opt()],
                                            outs=[o.ap().opt()])),
                w, sem=ccs, by=1)
        self.last_cc = (ccs, 1)
        return (ccs, 1)

    def op(self, eng, fn, waits=(), sem=None, by=1):
        if sem is None:
            sem = eng
        self.n[sem] += by
        self.q[eng].append((tuple(waits), fn, sem, by))
        return self.n[sem]

    def dma(self, eng, fn, waits=(), stream="misc"):
        nm = "sd_" + stream
        if nm not in self.sem:
            self._mksem(nm)
            self.dstreams.append(nm)
        cnt = self.op(eng, fn, waits, sem=nm, by=16)
        return (nm, cnt)

    def dma_all(self):
        return tuple((nm, self.n[nm]) for nm in self.dstreams
                     if self.n[nm] > 0)

    def now(self, name):
        return self.n[name]

    def allv(self):
        return ("v", self.n["v"])

    def alls(self):
        return ("s", self.n["s"])

    def allt(self):
        return ("t", self.n["t"])

    def finalize(self):
        secmap = {"t": self.block.tensor, "v": self.block.vector,
                  "s": self.block.scalar, "g": self.block.gpsimd,
                  "y": self.block.sync}

        for eng in ENGS:
            ops = self.q[eng]
            if not ops:
                continue

            def run(e, ops=ops):
                last = {}
                for waits, fn, semname, by in ops:
                    for snm, val in waits:
                        if last.get(snm, -1) < val:
                            e.wait_ge(self.sem[snm], val)
                            last[snm] = val
                    fn(e).then_inc(self.sem[semname], by)
            secmap[eng](run)


def P(fn, **kw):
    return lambda e, fn=fn, kw=dict(kw): fn(e, **kw)


class Scope:

    def __init__(self, nc):
        self.nc = nc
        self.cms = []

    def t(self, name, shape, dt):
        cm = self.nc.sbuf_tensor(name, shape, dt)
        h = cm.__enter__()
        self.cms.append(cm)
        return h

    def close(self):
        for cm in reversed(self.cms):
            cm.__exit__(None, None, None)
        self.cms = []


def fence(c):
    tgts = [(nm, c.n[nm]) for nm in ("t", "v", "s", "g", "y") if c.n[nm] > 0]
    tgts += list(c.dma_all())
    for eng in ENGS:
        c.q[eng].append((tuple(tgts), None, None, 0))


_LN_SCRATCH = 0


def alloc_ln_pack(sc, with_ffn=False):
    sb = {}
    global _LN_SCRATCH
    u = _LN_SCRATCH
    _LN_SCRATCH += 1
    sb["sq"] = sc.t(f"sqt{u}", [128, TOK], BF16)
    sb["ssum"] = sc.t(f"ssum{u}", [1, TOK], F32)
    sb["ssq"] = sc.t(f"ssq{u}", [1, TOK], F32)
    sb["sv1"] = sc.t(f"sv1_{u}", [1, TOK], F32)
    sb["sv2"] = sc.t(f"sv2_{u}", [1, TOK], F32)
    sb["mu_bc"] = sc.t(f"mu_bc{u}", [128, TOK], F32)
    sb["sc_bc"] = sc.t(f"sc_bc{u}", [128, TOK], F32)
    sb["tmp"] = sc.t(f"tmpf{u}", [128, TOK], F32)
    return sb


def emit_ln(c, sb, variant):
    eps2 = {"plain": None, "l0a": 1e-6, "l0ffn": 1e-3}[variant]
    ssum, ssq = sb["ssum"], sb["ssq"]
    sv1, sv2 = sb["sv1"], sb["sv2"]
    for ch in range(DCH):
        wv = (("t", c.now("t")),) if ch > 0 else ()
        vc = c.op("v", P(lambda v, o=sb["xn"][ch], i=sb["x"][ch]:
                         v.tensor_scalar(o[:], i[:], 1.0, None, ALU.mult)), wv)
        sq = c.op("v", P(lambda v, o=sb["sq"], i=sb["xn"][ch]:
                         v.tensor_tensor(o[:], i[:], i[:], ALU.mult)),
                  (("v", vc),))
        for piece in range(4):
            w = [("v", vc)]
            if ch == 0:
                w.append(c.pguard[piece])
            c.op("t", P(lambda t, pb=c.pb[piece], lh=sb["ones_bf"],
                        r=sb["xn"][ch], n0=piece * 512, st0=(ch == 0),
                        sp=(ch == DCH - 1):
                        t.matmul(pb[0:1, :], lh[:, 0:1], r[:, n0:n0 + 512],
                                 start=st0, stop=sp)), w)
        for piece in range(4):
            w = [("v", sq)]
            if ch == 0:
                w.append(c.pguard[4 + piece])
            c.op("t", P(lambda t, pb=c.pb[4 + piece], lh=sb["ones_bf"],
                        r=sb["sq"], n0=piece * 512, st0=(ch == 0),
                        sp=(ch == DCH - 1):
                        t.matmul(pb[0:1, :], lh[:, 0:1], r[:, n0:n0 + 512],
                                 start=st0, stop=sp)), w)
    tmm = c.now("t")
    for piece in range(4):
        ev = c.op("v", P(lambda v, o=ssum, pb=c.pb[piece], n0=piece * 512:
                         v.tensor_scalar(o[0:1, n0:n0 + 512], pb[0:1, :],
                                         1.0 / D, None, ALU.mult)),
                  (("t", tmm),))
        c.pguard[piece] = ("v", ev)
        ev = c.op("v", P(lambda v, o=ssq, pb=c.pb[4 + piece], n0=piece * 512:
                         v.tensor_scalar(o[0:1, n0:n0 + 512], pb[0:1, :],
                                         1.0 / D, None, ALU.mult)),
                  (("t", tmm),))
        c.pguard[4 + piece] = ("v", ev)
    def V(fn, *w):
        return c.op("v", fn, tuple(("v", x) if isinstance(x, int) else x
                                   for x in w))

    def SS(fn, *w):
        return c.op("s", fn, tuple(("v", x) if isinstance(x, int) else x
                                   for x in w))
    p = c.now("v")
    p = V(P(lambda v, o=sv1, m=ssum: v.tensor_tensor(
        o[0:1, :], m[0:1, :], m[0:1, :], ALU.mult)), p)
    p = V(P(lambda v, o=sv1, m=ssq: v.tensor_tensor(
        o[0:1, :], m[0:1, :], o[0:1, :], ALU.subtract)), p)
    p = V(P(lambda v, o=sv2, i=sv1: v.tensor_scalar(
        o[0:1, :], i[0:1, :], 1e-5, None, ALU.add)), p)
    if eps2 is None:
        ps = SS(P(lambda s, o=sv2, i=sv2: s.activation(
            o[0:1, :], i[0:1, :], AF.Sqrt)), p)
        p = V(P(lambda v, o=sv2, i=sv2: v.reciprocal(
            o[0:1, :], i[0:1, :])), ("s", ps))
    else:
        p1 = V(P(lambda v, o=ssq, i=sv2: v.reciprocal(
            o[0:1, :], i[0:1, :])), p)
        ps = SS(P(lambda s, o=sv2, i=sv2: s.activation(
            o[0:1, :], i[0:1, :], AF.Sqrt)), p1)
        p = V(P(lambda v, o=sv2, i=sv2: v.reciprocal(
            o[0:1, :], i[0:1, :])), ("s", ps))
        p = V(P(lambda v, o=sv1, a=sv1, b=ssq: v.tensor_tensor(
            o[0:1, :], a[0:1, :], b[0:1, :], ALU.mult)), p)
        p = V(P(lambda v, o=sv1, i=sv1, e=eps2: v.tensor_scalar(
            o[0:1, :], i[0:1, :], e, None, ALU.add)), p)
        ps = SS(P(lambda s, o=sv1, i=sv1: s.activation(
            o[0:1, :], i[0:1, :], AF.Sqrt)), p)
        p = V(P(lambda v, o=sv1, i=sv1: v.reciprocal(
            o[0:1, :], i[0:1, :])), ("s", ps))
        p = V(P(lambda v, o=sv2, a=sv2, b=sv1: v.tensor_tensor(
            o[0:1, :], a[0:1, :], b[0:1, :], ALU.mult)), p)
    scv = p
    for piece in range(4):
        kmu, ksc = piece, 4 + piece
        w = [("v", scv), c.pguard[kmu], c.pguard[ksc]]
        c.op("t", P(lambda t, pb=c.pb[kmu], lh=sb["ones1_f"], r=ssum,
                    n0=piece * 512:
                    t.matmul(pb[:, :], lh[0:1, :], r[0:1, n0:n0 + 512],
                             start=True, stop=True)), w)
        tb = c.op("t", P(lambda t, pb=c.pb[ksc], lh=sb["ones1_f"], r=sv2,
                         n0=piece * 512:
                         t.matmul(pb[:, :], lh[0:1, :], r[0:1, n0:n0 + 512],
                                  start=True, stop=True)), w)
        e1 = c.op("v", P(lambda v, o=sb["mu_bc"], pb=c.pb[kmu], n0=piece * 512:
                         v.tensor_scalar(o[:, n0:n0 + 512], pb[:, :], 1.0,
                                         None, ALU.mult)), (("t", tb),))
        e2 = c.op("v", P(lambda v, o=sb["sc_bc"], pb=c.pb[ksc], n0=piece * 512:
                         v.tensor_scalar(o[:, n0:n0 + 512], pb[:, :], 1.0,
                                         None, ALU.mult)), (("t", tb),))
        c.pguard[kmu] = ("v", e1)
        c.pguard[ksc] = ("v", e2)
    bc_done = c.now("v")
    for ch in range(DCH):
        t1 = c.op("v", P(lambda v, o=sb["tmp"], x=sb["x"][ch], m=sb["mu_bc"]:
                         v.tensor_tensor(o[:], x[:], m[:], ALU.subtract)),
                  (("v", max(bc_done, c.now("v"))),))
        if variant == "plain":
            c.op("v", P(lambda v, o=sb["xn"][ch], a=sb["tmp"], b=sb["sc_bc"]:
                        v.tensor_tensor(o[:], a[:], b[:], ALU.mult)),
                 (("v", t1),))
        elif variant == "l0a":
            c.op("v", P(lambda v, o=sb["xn"][ch], a=sb["tmp"], b=sb["sc_bc"],
                        w=sb["normw"], ch=ch:
                        v.scalar_tensor_tensor(o[:], a[:], w[:, ch:ch + 1],
                                               b[:], ALU.mult, ALU.mult)),
                 (("v", t1),))
        else:
            t2 = c.op("v", P(lambda v, o=sb["tmp"], a=sb["tmp"], b=sb["sc_bc"]:
                             v.tensor_tensor(o[:], a[:], b[:], ALU.mult)),
                      (("v", t1),))
            xnv = c.op("v", P(lambda v, o=sb["xn"][ch], i=sb["tmp"],
                              g=sb["lngam"], bt=sb["lnbet"], ch=ch:
                              v.tensor_scalar(o[:], i[:], g[:, ch:ch + 1],
                                              bt[:, ch:ch + 1], ALU.mult,
                                              ALU.add)), (("v", t2),))
            c.op("v", P(lambda v, o=sb["amax"], i=sb["xn"][ch], ch=ch:
                        v.reduce_max(o[:, ch:ch + 1], i[:], AXX,
                                     apply_absolute_value=True)),
                 (("v", xnv),))
    return c.now("v")


def bcast_col(c, sb, src, dst_col, k, waits):
    tb = c.op("t", P(lambda t, pb=c.pb[k], lh=sb["ones1_f"], r=src:
                     t.matmul(pb[:, 0:1], lh[0:1, :], r, start=True,
                              stop=True)), list(waits) + [c.pguard[k]])
    ev = c.op("v", P(lambda v, o=dst_col, pb=c.pb[k]:
                     v.tensor_scalar(o, pb[:, 0:1], 1.0, None, ALU.mult)),
              (("t", tb),))
    c.pguard[k] = ("v", ev)
    return ev


def emit_proj(c, w_tiles, w_ready, rhs, rhs_ready, dout, tokw, evac_cb,
              wslice=None):
    npieces = (tokw + 511) // 512
    bi = 0
    for oc in range(dout // 128):
        for piece in range(npieces):
            n0 = piece * 512
            nn = min(512, tokw - n0)
            k = bi % 8
            bi += 1
            pbk = c.pb[k]
            td = 0
            for kc in range(DCH):
                w = [w_ready, rhs_ready]
                if kc == 0:
                    w.append(c.pguard[k])
                td = c.op("t", P(lambda t, pb=pbk, lh=w_tiles[kc],
                                 o0=oc * 128, r=rhs[kc], n0=n0, nn=nn,
                                 st0=(kc == 0), sp=(kc == DCH - 1):
                                 t.matmul(pb[:, 0:nn], lh[:, o0:o0 + 128],
                                          r[:, n0:n0 + nn], start=st0,
                                          stop=sp)), w)
            c.pguard[k] = evac_cb(oc, n0, nn, pbk, ("t", td))


def emit_l0_ffn(c, sbp, io):
    sc = Scope(c.nc)
    u = io["uid"]
    sb = dict(sbp)
    sb["xn"] = [sc.t(f"ffxn{u}_{i}", [128, TOK], BF16) for i in range(DCH)]
    sb["amax"] = sc.t(f"amax{u}", [128, 8], F32)
    sb["lngam"] = sc.t(f"lngam{u}", [128, 8], F32)
    sb["lnbet"] = sc.t(f"lnbet{u}", [128, 8], F32)
    lnp = Scope(c.nc)
    sb.update(alloc_ln_pack(lnp))
    gb1 = c.dma("y", P(lambda y, o=sb["lngam"], i=io["lngb"]:
                       y.dma_start(out=o[:], in_=i[:, 0:8])), stream="gb")
    gb2 = c.dma("y", P(lambda y, o=sb["lnbet"], i=io["lngb"]:
                       y.dma_start(out=o[:], in_=i[:, 8:16])), stream="gb")
    c.q["v"].append(((gb2,), None, None, 0))
    ln_done = emit_ln(c, sb, "l0ffn")
    ln_fence = (("v", c.now("v")), ("t", c.now("t")), ("s", c.now("s")))
    lnp.close()
    sb["wf"] = sc.t(f"wf{u}", [128, D], F32)
    sb["wq"] = [sc.t(f"wqz{u}_{i}", [128, D], BF16) for i in range(DCH)]
    sb["wabs"] = sc.t(f"wabs{u}", [128, D], F32)
    sb["wsg"] = sc.t(f"wsg{u}", [128, D], BF16)
    sb["col"] = sc.t(f"colw{u}", [128, 8], F32)
    sca = sb["sca"]
    rs = 0
    for ch in range(DCH):
        ld = c.dma("y", P(lambda y, o=sb["wf"], hp=io["w_hbm"], ch=ch:
                          y.dma_start(out=o[:], in_=hp[ch * 128:(ch + 1) * 128, :])),
                   (("v", rs),) + (ln_fence if ch == 0 else ()), stream="wf")
        rs = c.op("v", P(lambda v, o=sb["col"], i=sb["wf"], ch=ch:
                         v.reduce_sum(o[:, ch:ch + 1], i[:], AXX,
                                      apply_absolute_value=True)),
                  (ld,))
    vsum = c.now("v")
    kB = 0
    tb = c.op("t", P(lambda t, pb=c.pb[kB], lh=sb["onesc_f"], r=sb["col"]:
                     t.matmul(pb[0:1, 0:8], lh[:, 0:1], r[:, 0:8],
                              start=True, stop=True)),
              [("v", vsum), c.pguard[kB]])
    e1 = c.op("v", P(lambda v, o=sca, pb=c.pb[kB]:
                     v.tensor_scalar(o[0:1, 0:8], pb[0:1, 0:8], 1.0, None,
                                     ALU.mult)), (("t", tb),))
    c.pguard[kB] = ("v", e1)
    c.op("v", P(lambda v, o=sca: v.reduce_sum(o[0:1, 8:9], o[0:1, 0:8], AXX)),
         (("v", e1),))
    bet = c.op("v", P(lambda v, o=sca: v.tensor_scalar(
        o[0:1, 9:10], o[0:1, 8:9], 1.0 / (D * D), None, ALU.mult)),
        (("v", c.now("v")),))
    thr = c.op("v", P(lambda v, o=sca: v.tensor_scalar(
        o[0:1, 10:11], o[0:1, 9:10], 1e-5, 0.5, ALU.add, ALU.mult)),
        (("v", bet),))
    thr_bc = bcast_col(c, sb, sca[0:1, 10:11], sb["thr_col"][:, 0:1], 1,
                       [("v", thr)])
    wq_done = 0
    sgd = 0
    for ch in range(DCH):
        ld = c.dma("y", P(lambda y, o=sb["wf"], hp=io["w_hbm"], ch=ch:
                          y.dma_start(out=o[:], in_=hp[ch * 128:(ch + 1) * 128, :])),
                   (("s", sgd), ("v", rs)), stream="wf")
        ab = c.op("s", P(lambda s, o=sb["wabs"], i=sb["wf"]:
                         s.activation(o[:], i[:], AF.Abs)),
                  (ld, ("v", wq_done)))
        cp = c.op("v", P(lambda v, o=sb["wq"][ch], i=sb["wabs"],
                         tc=sb["thr_col"]:
                         v.tensor_scalar(o[:], i[:], tc[:, 0:1], None,
                                         ALU.is_gt)),
                  (("s", ab), ("v", thr_bc)))
        sgd = c.op("s", P(lambda s, o=sb["wsg"], i=sb["wf"]:
                          s.activation(o[:], i[:], AF.Sign)), (("s", ab),))
        wq_done = c.op("v", P(lambda v, o=sb["wq"][ch], b=sb["wsg"]:
                              v.tensor_tensor(o[:], o[:], b[:], ALU.mult)),
                       (("s", sgd), ("v", cp)))
    kT = 2
    t1 = c.op("t", P(lambda t, pb=c.pb[kT], i=sb["amax"], idn=sb["ident_f"]:
                     t.transpose(pb[0:8, 0:128], i[:, 0:8], idn[:])),
              [("v", ln_done), c.pguard[kT]])
    m1 = c.op("v", P(lambda v, o=sb["gcol"], pb=c.pb[kT]:
                     v.reduce_max(o[0:8, 0:1], pb[0:8, 0:128], AXX)),
              (("t", t1),))
    c.pguard[kT] = ("v", m1)
    t2 = c.op("t", P(lambda t, pb=c.pb[kT], i=sb["gcol"], idn=sb["ident_f"]:
                     t.transpose(pb[0:1, 0:8], i[0:8, 0:1], idn[0:8, 0:8])),
              [("v", m1), c.pguard[kT]])
    m2 = c.op("v", P(lambda v, o=sb["grow"], pb=c.pb[kT]:
                     v.reduce_max(o[0:1, 0:1], pb[0:1, 0:8], AXX)),
              (("t", t2),))
    c.pguard[kT] = ("v", m2)
    st_g = c.dma("y", P(lambda y, o=io["gam_in"].ap(), i=sb["grow"]:
                        y.dma_start(out=o, in_=i[0:1, 0:16])), (("v", m2),),
                 stream="gam")
    ccp = c.collective("AllReduce", ALU.max, io["gam_in"], io["gam_out"],
                       (st_g,))
    ldg = c.dma("y", P(lambda y, o=sb["grow"], i=io["gam_out"]:
                       y.dma_start(out=o[0:1, 16:32], in_=i.ap())),
                (ccp,), stream="gam")
    rg = c.op("v", P(lambda v, o=sca, i=sb["grow"]:
                     v.reciprocal(o[0:1, 11:12], i[0:1, 16:17])),
              (ldg,))
    scq = c.op("v", P(lambda v, o=sca: v.tensor_scalar(
        o[0:1, 12:13], o[0:1, 11:12], 128.0, None, ALU.mult)), (("v", rg),))
    g1 = c.op("v", P(lambda v, o=sca, g=sb["grow"]: v.tensor_tensor(
        o[0:1, 13:14], g[0:1, 16:17], o[0:1, 9:10], ALU.mult)), (("v", scq),))
    sco = c.op("v", P(lambda v, o=sca: v.tensor_scalar(
        o[0:1, 14:15], o[0:1, 13:14], 1.0 / 128.0, None, ALU.mult)),
        (("v", g1),))
    b1 = bcast_col(c, sb, sca[0:1, 12:13], sb["thr_col"][:, 1:2], 3,
                   [("v", scq)])
    b2 = bcast_col(c, sb, sca[0:1, 14:15], sb["thr_col"][:, 2:3], 4,
                   [("v", sco)])
    xq_done = 0
    for ch in range(DCH):
        q1 = c.op("v", P(lambda v, o=sb["xn"][ch], tc=sb["thr_col"]:
                         v.tensor_scalar(o[:], o[:], tc[:, 1:2],
                                         128.0 - 1e-5, ALU.mult, ALU.min)),
                  (("v", b1), ("v", ln_done)))
        xq_done = c.op("v", P(lambda v, o=sb["xn"][ch]:
                              v.tensor_scalar(o[:], o[:], -(128.0 - 1e-5),
                                              None, ALU.max)), (("v", q1),))
    def evac(oc, n0, nn, pbk, tg):
        ev = c.op("v", P(lambda v, o=sb["x"][oc], pb=pbk, tc=sb["thr_col"],
                         n0=n0, nn=nn:
                         v.scalar_tensor_tensor(o[:, n0:n0 + nn],
                                                pb[:, 0:nn], tc[:, 2:3],
                                                o[:, n0:n0 + nn],
                                                ALU.mult, ALU.add)),
                  [tg, ("v", b2)])
        return ("v", ev)

    emit_proj(c, sb["wq"], ("v", wq_done), sb["xn"], ("v", xq_done),
              D, TOK, evac)
    fence(c)
    sc.close()


def alloc_common(nc, stack):
    def sb_(name, shape, dt):
        t = nc.sbuf_tensor(name, shape, dt).__enter__()
        stack.append(t)
        return t
    sb = {}
    sb["x"] = [sb_(f"x{i}", [128, TOK], F32) for i in range(DCH)]
    sb["ones_bf"] = sb_("ones_bf", [128, 1], BF16)
    sb["onesc_f"] = sb_("onesc_f", [128, 1], F32)
    sb["ones1_f"] = sb_("ones1_f", [1, 128], F32)
    sb["ident_f"] = sb_("ident_f", [128, 128], F32)
    sb["ident_bf"] = sb_("ident_bf", [128, 128], BF16)
    sb["sca"] = sb_("sca", [1, 32], F32)
    sb["thr_col"] = sb_("thr_col", [128, 4], F32)
    sb["grow"] = sb_("grow", [1, 32], F32)
    sb["gcol"] = sb_("gcol", [8, 1], F32)
    return sb


def emit_init_consts(c, sb, io):
    c.op("v", P(lambda v, o=sb["ones_bf"]: v.memset(o[:], 1.0)))
    c.op("v", P(lambda v, o=sb["onesc_f"]: v.memset(o[:], 1.0)))
    c.op("v", P(lambda v, o=sb["ones1_f"]: v.memset(o[:], 1.0)))
    c.op("v", P(lambda v, o=sb["grow"]: v.memset(o[:], 0.0)))
    c.dma("y", P(lambda y, o=sb["ident_f"], i=io["ident"]:
                 y.dma_start(out=o[:], in_=i)), stream="init")
    c.dma("g", P(lambda g, o=sb["ident_bf"], i=io["ident"]:
                 g.dma_start(out=o[:], in_=i)), stream="initg")


def emit_load_x(c, sb, x_hbm):
    prs = []
    for ch in range(DCH):
        prs.append(c.dma("y", P(lambda y, o=sb["x"][ch], hp=x_hbm, ch=ch:
                   y.dma_start(out=o[:], in_=hp[ch * 128:(ch + 1) * 128, :])),
                   stream="ldx"))
    return tuple(prs)


def emit_store_x(c, sb, x_hbm, waits):
    prs = []
    for ch in range(DCH):
        prs.append(c.dma("y", P(lambda y, i=sb["x"][ch], hp=x_hbm, ch=ch:
                   y.dma_start(out=hp[ch * 128:(ch + 1) * 128, :], in_=i[:])),
                   waits if ch == 0 else (), stream="stx"))
    c.q["y"].append(((prs[-1],), None, None, 0))


def _final_wait_patch(ctx):
    pass


def _finalize(self):
    secmap = {"t": self.block.tensor, "v": self.block.vector,
              "s": self.block.scalar, "g": self.block.gpsimd,
              "y": self.block.sync}
    for eng in ENGS:
        ops = self.q[eng]
        if not ops:
            continue

        def run(e, ops=ops):
            last = {}
            for waits, fn, semname, by in ops:
                for snm, val in waits:
                    if last.get(snm, -1) < val:
                        e.wait_ge(self.sem[snm], val)
                        last[snm] = val
                if fn is not None:
                    fn(e).then_inc(self.sem[semname], by)
        secmap[eng](run)


Ctx.finalize = _finalize


PHASES = ("l0a", "l0f1", "l0b", "l0f2", "l1a", "l1f1", "l1b", "l1f2")


def build_program(phases):
    nc = bass.Bass("TRN2", target_bir_lowering=False, debug=False)
    io = {}
    io["x_in"] = nc.dram_tensor("x_in", [D, TOK], F32, kind="ExternalInput").ap()
    io["x_out"] = nc.dram_tensor("x_out", [D, TOK], F32, kind="ExternalOutput").ap()
    io["ident"] = nc.dram_tensor("ident", [128, 128], F32, kind="ExternalInput").ap()
    needs_ffn = any(p in ("l0f1", "l0f2", "lnonly") for p in phases)
    if any(p in ("l1f1", "l1f2") for p in phases):
        io["w1T_sh"] = nc.dram_tensor("w1T_sh", [D, 512], F32, kind="ExternalInput").ap()
        io["w2T_sh"] = nc.dram_tensor("w2T_sh", [4096, 128], F32, kind="ExternalInput").ap()
        io["b1c"] = nc.dram_tensor("b1c", [128, 32], F32, kind="ExternalInput").ap()
        io["b2c"] = nc.dram_tensor("b2c", [128, 8], F32, kind="ExternalInput").ap()
        io["ag1_in"] = nc.dram_tensor("ag1_in", [D, 512], BF16)
        io["ag1_out"] = nc.dram_tensor("ag1_out", [8 * D, 512], BF16)
        io["ag2_in"] = nc.dram_tensor("ag2_in", [4096, 128], BF16)
        io["ag2_out"] = nc.dram_tensor("ag2_out", [8 * 4096, 128], BF16)
    if "l0b" in phases:
        io["b_wqT"] = nc.dram_tensor("b_wqT", [D, D], F32, kind="ExternalInput").ap()
        io["b_wkT"] = nc.dram_tensor("b_wkT", [D, D], F32, kind="ExternalInput").ap()
        io["b_wvT"] = nc.dram_tensor("b_wvT", [D, D], F32, kind="ExternalInput").ap()
        io["b_wffT"] = nc.dram_tensor("b_wffT", [D, D], F32, kind="ExternalInput").ap()
        io["b_bff"] = nc.dram_tensor("b_bff", [128, 8], F32, kind="ExternalInput").ap()
        io["agk_in"] = nc.dram_tensor("agk_in", [D, TOK], BF16)
        io["agk_out"] = nc.dram_tensor("agk_out", [8 * D, TOK], BF16)
        io["agv_in"] = nc.dram_tensor("agv_in", [TOK, 16 * 65], BF16)
        io["agv_out"] = nc.dram_tensor("agv_out", [8 * TOK, 16 * 65], BF16)
    if "l1b" in phases:
        io["m_wdq"] = nc.dram_tensor("m_wdq", [D, 256], F32, kind="ExternalInput").ap()
        io["m_wdkv"] = nc.dram_tensor("m_wdkv", [D, 256], F32, kind="ExternalInput").ap()
        io["m_wuq"] = nc.dram_tensor("m_wuq", [256, D], F32, kind="ExternalInput").ap()
        io["m_wqr"] = nc.dram_tensor("m_wqr", [256, 512], F32, kind="ExternalInput").ap()
        io["m_wuk"] = nc.dram_tensor("m_wuk", [256, D], F32, kind="ExternalInput").ap()
        io["m_wuv"] = nc.dram_tensor("m_wuv", [256, D], F32, kind="ExternalInput").ap()
        io["m_wkr"] = nc.dram_tensor("m_wkr", [D, 32], F32, kind="ExternalInput").ap()
        io["m_wo"] = nc.dram_tensor("m_wo", [D, D], F32, kind="ExternalInput").ap()
        io["m_cos"] = nc.dram_tensor("m_cos", [128, TOK], F32, kind="ExternalInput").ap()
        io["m_sin"] = nc.dram_tensor("m_sin", [128, TOK], F32, kind="ExternalInput").ap()
        io["mgate"] = nc.dram_tensor("mgate", [128, 128], F32, kind="ExternalInput").ap()
        if "rotT" not in io:
            io["rotT"] = nc.dram_tensor("rotT", [128, 128], F32, kind="ExternalInput").ap()
        if "amask" not in io:
            io["amask"] = nc.dram_tensor("amask", [512, 512], F32, kind="ExternalInput").ap()
        io["agkc_in"] = nc.dram_tensor("agkc_in", [D, TOK], BF16)
        io["agkc_out"] = nc.dram_tensor("agkc_out", [8 * D, TOK], BF16)
        io["agkr_in"] = nc.dram_tensor("agkr_in", [32, TOK], BF16)
        io["agkr_out"] = nc.dram_tensor("agkr_out", [256, TOK], BF16)
        io["agv2_in"] = nc.dram_tensor("agv2_in", [TOK, 16 * 65], BF16)
        io["agv2_out"] = nc.dram_tensor("agv2_out", [8 * TOK, 16 * 65], BF16)
    if "l1a" in phases:
        for nm2, shp in (("la_wrT", [D, D]), ("la_wkT", [D, D]),
                         ("la_wvT", [D, D]), ("la_wtdT", [D, D]),
                         ("la_gd", [128, 8]), ("la_mix", [128, 8]),
                         ("la_sel", [128, 64])):
            io[nm2] = nc.dram_tensor(nm2, shp, F32, kind="ExternalInput").ap()
        io["la_sel8"] = nc.dram_tensor("la_sel8", [128, 512], I32, kind="ExternalInput").ap()
        io["la_bidx"] = nc.dram_tensor("la_bidx", [128, 4], I32, kind="ExternalInput").ap()
        io["la_stidx"] = nc.dram_tensor("la_stidx", [128, 32], I32, kind="ExternalInput").ap()
        for nm2 in ("scr_z", "scr_ek", "scr_v", "scr_r", "scr_xin"):
            io[nm2] = nc.dram_tensor(nm2, [D, TOK], F32)
        io["agbx_in"] = nc.dram_tensor("agbx_in", [512, 8], BF16)
        io["agbx_out"] = nc.dram_tensor("agbx_out", [4096, 8], BF16)
        io["agsum_in"] = nc.dram_tensor("agsum_in", [128, 96], F32)
        io["agsum_out"] = nc.dram_tensor("agsum_out", [1024, 96], F32)
        io["agst_in"] = nc.dram_tensor("agst_in", [4096, 65], F32)
        io["agst_out"] = nc.dram_tensor("agst_out", [32768, 65], F32)
    if "l0a" in phases:
        io["a_wqkv"] = nc.dram_tensor("a_wqkv", [D, 3072], F32, kind="ExternalInput").ap()
        io["a_wout"] = nc.dram_tensor("a_wout", [D, D], F32, kind="ExternalInput").ap()
        io["a_bout"] = nc.dram_tensor("a_bout", [128, 8], F32, kind="ExternalInput").ap()
        io["a_normw"] = nc.dram_tensor("a_normw", [128, 8], F32, kind="ExternalInput").ap()
        io["a_pmkT"] = nc.dram_tensor("a_pmkT", [128, 256], F32, kind="ExternalInput").ap()
        io["a_pmv"] = nc.dram_tensor("a_pmv", [256, 64], F32, kind="ExternalInput").ap()
        io["a_cos"] = nc.dram_tensor("a_cos", [128, TOK], F32, kind="ExternalInput").ap()
        io["a_sin"] = nc.dram_tensor("a_sin", [128, TOK], F32, kind="ExternalInput").ap()
        if "rotT" not in io:
            io["rotT"] = nc.dram_tensor("rotT", [128, 128], F32, kind="ExternalInput").ap()
        if "amask" not in io:
            io["amask"] = nc.dram_tensor("amask", [512, 512], F32, kind="ExternalInput").ap()
    if needs_ffn:
        io["w_hbm"] = nc.dram_tensor("l0f_w", [D, D], F32, kind="ExternalInput").ap()
        io["lngb"] = nc.dram_tensor("l0f_gb", [128, 16], F32, kind="ExternalInput").ap()
        io["gam_in"] = nc.dram_tensor("gam_in", [1, 16], F32)
        io["gam_out"] = nc.dram_tensor("gam_out", [1, 16], F32)
        io["gam_in2"] = nc.dram_tensor("gam_in2", [1, 16], F32)
        io["gam_out2"] = nc.dram_tensor("gam_out2", [1, 16], F32)

    stack = []
    with nc.Block() as block:
        sb = alloc_common(nc, stack)
        c = Ctx(nc, block)
        emit_init_consts(c, sb, io)
        emit_load_x(c, sb, io["x_in"])
        for eng in ("v", "t", "s"):
            c.q[eng].append((c.dma_all(), None, None, 0))
        first_ffn = True
        for ph in phases:
            if ph == "lnonly":
                io2 = dict(io, uid=0)
                sc = Scope(nc)
                sb2 = dict(sb)
                sb2["xn"] = [sc.t(f"lnxn{i}", [128, TOK], BF16)
                             for i in range(DCH)]
                sb2["amax"] = sc.t("amax_l", [128, 8], F32)
                sb2["lngam"] = sc.t("lngam_l", [128, 8], F32)
                sb2["lnbet"] = sc.t("lnbet_l", [128, 8], F32)
                sb2.update(alloc_ln_pack(Scope(nc)))
                c.dma("y", P(lambda y, o=sb2["lngam"], i=io["lngb"]:
                             y.dma_start(out=o[:], in_=i[:, 0:8])), stream="gb")
                gbl = c.dma("y", P(lambda y, o=sb2["lnbet"], i=io["lngb"]:
                                   y.dma_start(out=o[:], in_=i[:, 8:16])),
                            stream="gb")
                c.q["v"].append(((gbl,), None, None, 0))
                nv = emit_ln(c, sb2, "l0ffn")
                for ch in range(DCH):
                    nv = c.op("v", P(lambda v, o=sb2["x"][ch], i=sb2["xn"][ch]:
                                     v.tensor_scalar(o[:], i[:], 1.0, None,
                                                     ALU.mult)), (("v", nv),))
            elif ph == "l0a":
                emit_l0_attn_a(c, sb, io)
            elif ph == "l1a":
                emit_l1_attn_a(c, sb, io)
            elif ph == "l1b":
                emit_l1_attn_b(c, sb, io)
            elif ph == "l0b":
                emit_l0_attn_b(c, sb, io)
            elif ph in ("l1f1", "l1f2"):
                iof = io.setdefault("_l1f_io", dict(io, uid=2))
                first_l1f = "ag_sems" not in iof
                iof2 = dict(iof, uid=2 if first_l1f else 3)
                if not first_l1f:
                    iof2["ag_sems"] = iof["ag_sems"]
                emit_l1_ffn(c, sb, iof2, first=first_l1f)
                if first_l1f:
                    iof["ag_sems"] = iof2["ag_sems"]
            elif ph in ("l0f1", "l0f2"):
                io2 = dict(io, uid=(0 if first_ffn else 1))
                if not first_ffn:
                    io2["gam_in"] = io["gam_in2"]
                    io2["gam_out"] = io["gam_out2"]
                first_ffn = False
                emit_l0_ffn(c, sb, io2)
            else:
                raise NotImplementedError(ph)
        emit_store_x(c, sb, io["x_out"],
                     ((c.allv()), (c.allt()), (c.alls()), ("g", c.now("g")))
                     + c.dma_all())
        c.finalize()
    return nc


def shard_x(x):
    shards = []
    for i in range(NC):
        blocks = [x[b, sg * SEG:(sg + 1) * SEG, :] for (b, sg) in core_insts(i)]
        xc = np.concatenate(blocks, axis=0)
        shards.append(np.ascontiguousarray(xc.T))
    return shards


def unshard_x(shards):
    x = np.zeros((B, S, D), np.float32)
    for i in range(NC):
        xc = shards[i].T
        for j, (b, sg) in enumerate(core_insts(i)):
            x[b, sg * SEG:(sg + 1) * SEG, :] = xc[j * SEG:(j + 1) * SEG]
    return x


_IDENT = np.eye(128, dtype=np.float32)


def _build_rott():
    r = np.zeros((128, 128), np.float32)
    for j in range(64):
        r[2 * j + 1, 2 * j] = -1.0
        r[2 * j, 2 * j + 1] = 1.0
    return r


_ROTT = _build_rott()


def _build_amask():
    kp = np.arange(512)[:, None]
    q = np.arange(512)[None, :]
    return (kp <= q).astype(np.float32)


_AMASK = _build_amask()


def core_positions(i):
    pos = np.zeros(TOK, np.int64)
    for j, (b, sg) in enumerate(core_insts(i)):
        pos[j * SEG:(j + 1) * SEG] = np.arange(sg * SEG, (sg + 1) * SEG)
    return pos


def _rope_tables_core(i, dh):
    pos = core_positions(i).astype(np.float64)
    j = (np.arange(128) % dh) // 2
    inv = 1.0 / 10000.0 ** ((2 * j) / dh)
    ang = inv[:, None] * pos[None, :]
    return (np.cos(ang).astype(np.float32),
            np.sin(ang).astype(np.float32))


def run_phase(phases, shards, inputs, extra=None):
    nc = build_program(tuple(phases))
    in_maps = []
    for i in range(NC):
        m = {"x_in": shards[i], "ident": _IDENT}
        if any(p in ("l0f1", "l0f2", "lnonly") for p in phases):
            m["l0f_w"] = inputs["L0_ffn_weight"]
            gb = np.zeros((128, 16), np.float32)
            gb[:, 0:8] = inputs["L0_ffn_ln_gamma"].reshape(8, 128).T
            gb[:, 8:16] = inputs["L0_ffn_ln_beta"].reshape(8, 128).T
            m["l0f_gb"] = np.ascontiguousarray(gb)
        if any(p in ("l1f1", "l1f2") for p in phases):
            w1t = np.ascontiguousarray(inputs["L1_ffn_w1"].T)
            m["w1T_sh"] = np.ascontiguousarray(w1t[:, i * 512:(i + 1) * 512])
            w2t = np.ascontiguousarray(inputs["L1_ffn_w2"].T)
            m["w2T_sh"] = np.ascontiguousarray(w2t[:, i * 128:(i + 1) * 128])
            m["b1c"] = np.ascontiguousarray(
                inputs["L1_ffn_b1"].reshape(32, 128).T)
            m["b2c"] = np.ascontiguousarray(
                inputs["L1_ffn_b2"].reshape(8, 128).T)
        if "l0b" in phases:
            m["b_wqT"] = np.ascontiguousarray(inputs["L0_attn_b_Wq"].T)
            m["b_wkT"] = np.ascontiguousarray(inputs["L0_attn_b_Wk"].T)
            m["b_wvT"] = np.ascontiguousarray(inputs["L0_attn_b_Wv"].T)
            m["b_wffT"] = np.ascontiguousarray(inputs["L0_attn_b_Wff"].T)
            m["b_bff"] = np.ascontiguousarray(
                inputs["L0_attn_b_bff"].reshape(8, 128).T)
        if "l1a" in phases:
            m["la_wrT"] = np.ascontiguousarray(inputs["L1_attn_a_W_R"].T)
            m["la_wkT"] = np.ascontiguousarray(inputs["L1_attn_a_W_K"].T)
            m["la_wvT"] = np.ascontiguousarray(inputs["L1_attn_a_W_V"].T)
            m["la_wtdT"] = np.ascontiguousarray(inputs["L1_attn_a_W_td"].T)
            m["la_gd"] = np.ascontiguousarray(
                inputs["L1_attn_a_global_decay"].reshape(8, 128).T)
            m["la_mix"] = np.ascontiguousarray(
                inputs["L1_attn_a_mix"].reshape(8, 128).T)
            sel = np.zeros((128, 64), np.float32)
            bidx = np.full((128, 4), 10 ** 7, np.int32)
            stidx = np.full((128, 32), 10 ** 7, np.int32)
            for j, (b, sg) in enumerate(core_insts(i)):
                sel[:, sg * 4 + j] = 1.0
                if sg > 0:
                    po, ps = owner_of(b, sg - 1)
                    bidx[:, j] = po * 512 + ps * 128 + np.arange(128)
                    for oc in range(8):
                        stidx[:, j * 8 + oc] = (po * 4096 + (ps * 8 + oc) * 128
                                                + np.arange(128))
            m["la_sel"] = sel
            m["la_sel8"] = np.ascontiguousarray(
                np.repeat(sel.astype(np.int32), 8, axis=1))
            m["la_bidx"] = bidx
            m["la_stidx"] = stidx
        if "l1b" in phases:
            m["m_wdq"] = inputs["L1_attn_b_W_DQ"]
            m["m_wdkv"] = inputs["L1_attn_b_W_DKV"]
            m["m_wuq"] = inputs["L1_attn_b_W_UQ"]
            m["m_wqr"] = inputs["L1_attn_b_W_QR"]
            m["m_wuk"] = inputs["L1_attn_b_W_UK"]
            m["m_wuv"] = inputs["L1_attn_b_W_UV"]
            m["m_wkr"] = inputs["L1_attn_b_W_KR"]
            m["m_wo"] = inputs["L1_attn_b_W_O"]
            m["m_cos"], m["m_sin"] = _rope_tables_core(i, 32)
            m["rotT"] = _ROTT
            m["amask"] = _AMASK
            mg = np.zeros((128, 128), np.float32)
            for sl in range(2):
                myseg = i if sl == 0 else 15 - i
                for r in range(8):
                    for j in range(8):
                        gseg = r if j < 4 else 15 - r
                        if sl == 0:
                            if j < 4:
                                mg[:, r * 4 + j] = 1.0 if gseg < myseg else 0.0
                        else:
                            mg[:, 64 + r * 8 + j] = 1.0 if gseg < myseg else 0.0
            m["mgate"] = mg
        if "l0a" in phases:
            m["a_wqkv"] = inputs["L0_attn_a_w_qkv"]
            m["a_wout"] = inputs["L0_attn_a_w_out"]
            m["a_bout"] = np.ascontiguousarray(
                inputs["L0_attn_a_b_out"].reshape(8, 128).T)
            m["a_normw"] = np.ascontiguousarray(
                inputs["L0_attn_a_norm_w"].reshape(8, 128).T)
            pmkt = np.concatenate([inputs["L0_attn_a_pm_k"][h].T
                                   for h in range(16)], axis=1)
            m["a_pmkT"] = np.ascontiguousarray(
                np.concatenate([pmkt, pmkt], axis=0))
            m["a_pmv"] = np.ascontiguousarray(
                inputs["L0_attn_a_pm_v"].reshape(256, 64))
            m["a_cos"], m["a_sin"] = _rope_tables_core(i, 64)
            m["rotT"] = _ROTT
            m["amask"] = _AMASK
        if extra:
            m.update(extra[i] if isinstance(extra, list) else extra)
        in_maps.append(m)
    import os
    if os.environ.get("KSIM"):
        import concourse.bass_interp as bass_interp
        sim = bass_interp.MultiCoreSim(nc, NC, require_finite=False,
                                       require_nnan=False,
                                       ignore_data_errors=True)
        for i in range(NC):
            for k, v in in_maps[i].items():
                sim.cores[i].tensor(k)[:] = v
        sim.simulate()
        return [np.array(sim.cores[i].mem_tensor("x_out")) for i in range(NC)]
    res = run_bass_kernel_spmd(nc, in_maps, core_ids=list(range(NC)))
    return [res.results[i]["x_out"] for i in range(NC)]


ALL_PHASES = ("l0a", "l0f1", "l0b", "l0f2", "l1a", "l1f1", "l1b", "l1f2")


def kernel(**inputs):
    import os
    x = np.asarray(inputs["L0_attn_a_seq"], np.float32)
    shards = shard_x(x)
    if os.environ.get("KUNFUSED"):
        for ph in ALL_PHASES:
            shards = run_phase([ph], shards, inputs)
    else:
        shards = run_phase(list(ALL_PHASES), shards, inputs)
    return unshard_x(shards)


def emit_l0_attn_a(c, sbp, io):
    sc = Scope(c.nc)
    sb = dict(sbp)
    sb["xn"] = [sc.t(f"axn{i}", [128, TOK], BF16) for i in range(DCH)]
    sb["normw"] = sc.t("normw", [128, 8], F32)
    lnp = Scope(c.nc)
    sb.update(alloc_ln_pack(lnp))
    nw = c.dma("y", P(lambda y, o=sb["normw"], i=io["a_normw"]:
                      y.dma_start(out=o[:], in_=i)), stream="gb")
    c.q["v"].append(((nw,), None, None, 0))
    ln_done = emit_ln(c, sb, "l0a")
    ln_fence = (("v", c.now("v")), ("t", c.now("t")), ("s", c.now("s")))
    lnp.close()

    wqkv = [sc.t(f"awq{i}", [128, 3072], BF16) for i in range(DCH)]
    wo2 = [sc.t(f"awo{i}", [128, D], BF16) for i in range(2)]
    bout = sc.t("about", [128, 8], F32)
    cosx = sc.t("acos", [128, TOK], BF16)
    sinx = sc.t("asin", [128, TOK], BF16)
    rt = sc.t("art", [128, 128], BF16)
    pmk = sc.t("apmk", [128, 256], BF16)
    pmv = sc.t("apmv", [16, 16 * 65], BF16)
    masks = sc.t("amask_sb", [128, 4 * 512], BF16)
    for ch in range(DCH):
        c.dma("g", P(lambda g, o=wqkv[ch], i=io["a_wqkv"], ch=ch:
                     g.dma_start(out=o[:], in_=i[ch * 128:(ch + 1) * 128, :])),
              ln_fence if ch == 0 else (), stream="wa")
    c.dma("y", P(lambda y, o=bout, i=io["a_bout"]:
                 y.dma_start(out=o[:], in_=i)), ln_fence, stream="wa2")
    c.dma("g", P(lambda g, o=cosx, i=io["a_cos"]:
                 g.dma_start(out=o[:], in_=i)), stream="wa")
    c.dma("g", P(lambda g, o=sinx, i=io["a_sin"]:
                 g.dma_start(out=o[:], in_=i)), stream="wa")
    c.dma("g", P(lambda g, o=rt, i=io["rotT"]:
                 g.dma_start(out=o[:], in_=i)), stream="wa")
    c.dma("g", P(lambda g, o=pmk, i=io["a_pmkT"]:
                 g.dma_start(out=o[:], in_=i)), stream="wa")
    pm_set = c.op("v", P(lambda v, o=pmv: v.memset(o[:], 1.0)))
    for h in range(16):
        c.dma("g", P(lambda g, o=pmv, i=io["a_pmv"], h=h:
                     g.dma_start(out=o[:, h * 65:h * 65 + 64],
                                 in_=i[h * 16:(h + 1) * 16, :])),
              (("v", pm_set),), stream="wa")
    c.dma("g", P(lambda g, o=masks, i=io["amask"]:
                 g.dma_start(out=o[:].rearrange("p (k q) -> p k q", k=4),
                             in_=i.rearrange("(k p) q -> p k q", p=128))),
          stream="wa")
    wready = c.dma_all()

    qs = [sc.t(f"aqs{i}", [128, SEG], BF16) for i in range(8)]
    ks = [sc.t(f"aks{i}", [128, SEG], BF16) for i in range(8)]
    vfm = sc.t("avfm", [128, SEG], BF16)
    vaug = [sc.t(f"avg{i}", [128, 16 * 65], BF16) for i in range(4)]
    rtmp1 = sc.t("artmp1", [128, SEG], F32)
    pts = [sc.t(f"apts{i}", [128, SEG], BF16) for i in range(4)]
    oun = sc.t("aoun", [128, SEG], F32)
    rtmp2 = oun
    rcp = sc.t("arcp", [1, SEG], F32)
    attn = [sc.t(f"aat{i}", [128, SEG], BF16) for i in range(8)]
    vg_set = c.op("v", P(lambda v: None) if False else
                  P(lambda v, o=vaug[0]: v.memset(o[:], 1.0)))
    for tb in range(1, 4):
        vg_set = c.op("v", P(lambda v, o=vaug[tb]: v.memset(o[:], 1.0)))

    bki = [0]

    def bank():
        k = bki[0] % 8
        bki[0] += 1
        return k

    wo_prev = [("t", 0), ("t", 0)]
    for sg in range(4):
        s0 = sg * SEG
        segw = [("v", c.now("v")), ("t", c.now("t")), ("s", c.now("s"))]
        for oc in range(24):
            k = bank()
            pbk = c.pb[k]
            td = 0
            for kc in range(DCH):
                w = [("v", ln_done)] + list(wready)
                if kc == 0:
                    w.append(c.pguard[k])
                    w += segw
                td = c.op("t", P(lambda t, pb=pbk, lh=wqkv[kc], o0=oc * 128,
                                 r=sb["xn"][kc], s0=s0, st0=(kc == 0),
                                 sp=(kc == DCH - 1):
                                 t.matmul(pb[:, :], lh[:, o0:o0 + 128],
                                          r[:, s0:s0 + SEG], start=st0,
                                          stop=sp)), w)
            if oc < 16:
                dst = qs[oc] if oc < 8 else ks[oc - 8]
                ev = c.op("s", P(lambda s, o=dst, pb=pbk:
                                 s.activation(o[:], pb[:, :], AF.Identity)),
                          (("t", td),))
                k2 = bank()
                pb2 = c.pb[k2]
                tr = c.op("t", P(lambda t, pb=pb2, lh=rt, r=dst:
                                 t.matmul(pb[:, :], lh[:, :], r[:, :],
                                          start=True, stop=True)),
                          [("s", ev), c.pguard[k2]])
                v1 = c.op("v", P(lambda v, o=rtmp1, a=dst, cx=cosx, s0=s0:
                                 v.tensor_tensor(o[:], a[:],
                                                 cx[:, s0:s0 + SEG],
                                                 ALU.mult)),
                          [("s", ev), ("v", c.now("v"))])
                v2 = c.op("v", P(lambda v, o=rtmp2, pb=pb2, sx=sinx, s0=s0:
                                 v.tensor_tensor(o[:], pb[:, :],
                                                 sx[:, s0:s0 + SEG],
                                                 ALU.mult)), (("t", tr),))
                v3 = c.op("v", P(lambda v, o=dst, a=rtmp1, b=rtmp2:
                                 v.tensor_tensor(o[:], a[:], b[:], ALU.add)),
                          (("v", v1), ("v", v2)))
                c.pguard[k] = ("v", v1)
                c.pguard[k2] = ("v", v2)
            else:
                voc = oc - 16
                ev = c.op("s", P(lambda s, o=vfm, pb=pbk:
                                 s.activation(o[:], pb[:, :], AF.Identity)),
                          [("t", td), ("t", c.now("t"))])
                c.pguard[k] = ("s", ev)
                for tb in range(4):
                    k3 = bank()
                    pb3 = c.pb[k3]
                    tt = c.op("t", P(lambda t, pb=pb3, i=vfm,
                                     idn=sb["ident_bf"], tb=tb:
                                     t.transpose(pb[:, :].bitcast(BF16)[:, 0:128],
                                                 i[:, tb * 128:(tb + 1) * 128],
                                                 idn[:])),
                              [("s", ev), c.pguard[k3]])
                    hA, hB = 2 * voc, 2 * voc + 1
                    e1 = c.op("v", P(lambda v, o=vaug[tb], pb=pb3, hA=hA:
                                     v.tensor_scalar(o[:, hA * 65:hA * 65 + 64],
                                                     pb[:, :].bitcast(BF16)[:, 0:64],
                                                     1.0, None, ALU.mult)),
                              [("t", tt), ("v", vg_set)])
                    e2 = c.op("v", P(lambda v, o=vaug[tb], pb=pb3, hB=hB:
                                     v.tensor_scalar(o[:, hB * 65:hB * 65 + 64],
                                                     pb[:, :].bitcast(BF16)[:, 64:128],
                                                     1.0, None, ALU.mult)),
                              (("t", tt),))
                    c.pguard[k3] = ("v", e2)
        proj_done = [("v", c.now("v")), ("s", c.now("s"))]
        state = {"mu_prev": ("v", 0), "tb2_prev": ("t", 0)}
        ko_of = {}

        def emit_norm(h, av_cnt):
            oc_h, row = h // 2, (h % 2) * 64
            ko = ko_of.pop(h)
            pbo = c.pb[ko]
            ev = c.op("s", P(lambda s, o=oun, pb=pbo:
                             s.activation(o[0:65, :],
                                          pb[0:65, :], AF.Identity)),
                      (("t", av_cnt), state["mu_prev"]))
            c.pguard[ko] = ("s", ev)
            rc = c.op("v", P(lambda v, o=rcp, i=oun:
                             v.reciprocal(o[0:1, :], i[64:65, :])),
                      (("s", ev), state["tb2_prev"]))
            kb = bank()
            while kb in ko_of.values():
                kb = bank()
            pbb = c.pb[kb]
            tb2 = c.op("t", P(lambda t, pb=pbb, lh=sb["ones1_f"],
                              r=rcp:
                              t.matmul(pb[0:64, :], lh[0:1, 0:64],
                                       r[0:1, :],
                                       start=True, stop=True)),
                       [("v", rc), c.pguard[kb]])
            mu = c.op("v", P(lambda v, o=attn[oc_h], row=row, a=oun,
                             pb=pbb:
                             v.tensor_tensor(o[row:row + 64, :],
                                             a[0:64, :],
                                             pb[0:64, :], ALU.mult)),
                      (("t", tb2),))
            c.pguard[kb] = ("v", mu)
            state["mu_prev"] = ("v", mu)
            state["tb2_prev"] = ("t", tb2)

        av_init = c.now("t")
        avh = {}
        DEFER = 2
        pendq = []
        steps = [(h, kc) for h in range(16) for kc in range(5)]
        for si, (h, kc) in enumerate(steps):
            oc_h, row = h // 2, (h % 2) * 64
            if kc == 0:
                ko_of[h] = bank()
                while len([1 for k2 in ko_of.values()
                           if k2 == ko_of[h]]) > 1:
                    ko_of[h] = bank()
            ko = ko_of[h]
            ksc = bank()
            while ksc in ko_of.values():
                ksc = bank()
            pbs = c.pb[ksc]
            pbo = c.pb[ko]
            if kc == 0:
                tmm = c.op("t", P(lambda t, pb=pbs, lh=pmk, h=h,
                                  q=qs[oc_h], row=row:
                                  t.matmul(pb[0:16, :],
                                           lh[row:row + 64,
                                              h * 16:(h + 1) * 16],
                                           q[row:row + 64, :],
                                           start=True, stop=True)),
                           proj_done + [c.pguard[ksc]])
                npart = 16
            else:
                tmm = c.op("t", P(lambda t, pb=pbs, kk=ks[oc_h], row=row,
                                  kc=kc, q=qs[oc_h]:
                                  t.matmul(pb[:, :],
                                           kk[row:row + 64,
                                              (kc - 1) * 128:kc * 128],
                                           q[row:row + 64, :],
                                           start=True, stop=True)),
                           proj_done + [c.pguard[ksc]])
                npart = 128
            ptb = si % 4
            ex = c.op("s", P(lambda s, o=pts[ptb], pb=pbs, np_=npart:
                             s.activation(o[0:np_, :], pb[0:np_, :],
                                          AF.Exp, scale=0.125)),
                      [("t", tmm), ("t", avh.get(si - 4, av_init))])
            c.pguard[ksc] = ("s", ex)
            pdep = ("s", ex)
            if kc > 0:
                mk = c.op("v", P(lambda v, o=pts[ptb], m=masks, kc=kc:
                                 v.tensor_tensor(o[:], o[:],
                                                 m[:, (kc - 1) * 512:
                                                   kc * 512],
                                                 ALU.mult)), (("s", ex),))
                pdep = ("v", mk)
            pendq.append((ptb, pdep, h, kc, si))
            if len(pendq) > DEFER:
                p_ptb, p_dep, p_h, p_kc, p_si = pendq.pop(0)
                w = [p_dep]
                if p_kc == 0:
                    w.append(c.pguard[ko_of[p_h]])
                lhs = (P(lambda t, pb=c.pb[ko_of[p_h]], lh=pmv, h=p_h,
                         r=pts[p_ptb]:
                         t.matmul(pb[0:65, :],
                                  lh[0:16, h * 65:h * 65 + 65],
                                  r[0:16, :], start=True, stop=False))
                       if p_kc == 0 else
                       P(lambda t, pb=c.pb[ko_of[p_h]], lh=vaug[p_kc - 1],
                         h=p_h, r=pts[p_ptb], sp=(p_kc == 4):
                         t.matmul(pb[0:65, :],
                                  lh[:, h * 65:h * 65 + 65],
                                  r[:, :], start=False, stop=sp)))
                avh[p_si] = c.op("t", lhs, w)
                if p_kc == 4:
                    emit_norm(p_h, avh[p_si])
        for p_ptb, p_dep, p_h, p_kc, p_si in pendq:
            w = [p_dep]
            if p_kc == 0:
                w.append(c.pguard[ko_of[p_h]])
            lhs = (P(lambda t, pb=c.pb[ko_of[p_h]], lh=pmv, h=p_h,
                     r=pts[p_ptb]:
                     t.matmul(pb[0:65, :],
                              lh[0:16, h * 65:h * 65 + 65],
                              r[0:16, :], start=True, stop=False))
                   if p_kc == 0 else
                   P(lambda t, pb=c.pb[ko_of[p_h]], lh=vaug[p_kc - 1],
                     h=p_h, r=pts[p_ptb], sp=(p_kc == 4):
                     t.matmul(pb[0:65, :],
                              lh[:, h * 65:h * 65 + 65],
                              r[:, :], start=False, stop=sp)))
            avh[p_si] = c.op("t", lhs, w)
            if p_kc == 4:
                emit_norm(p_h, avh[p_si])
        pendq = []
        attn_done = ("v", c.now("v"))
        td = 0
        for kc in range(DCH):
            wb = kc % 2
            ld = c.dma("g", P(lambda g, o=wo2[wb], i=io["a_wout"], kc=kc:
                              g.dma_start(out=o[:],
                                          in_=i[kc * 128:(kc + 1) * 128, :])),
                       (wo_prev[wb],), stream=f"wo{wb}")
            for oc in range(DCH):
                w = [attn_done, ld]
                if kc == 0:
                    w.append(c.pguard[oc])
                td = c.op("t", P(lambda t, pb=c.pb[oc], lh=wo2[wb],
                                 o0=oc * 128, r=attn[kc], st0=(kc == 0),
                                 sp=(kc == DCH - 1):
                                 t.matmul(pb[:, :], lh[:, o0:o0 + 128],
                                          r[:, :], start=st0, stop=sp)), w)
            wo_prev[wb] = ("t", td)
        for oc in range(DCH):
            ev = c.op("v", P(lambda v, o=sb["x"][oc], pb=c.pb[oc], bc=bout,
                             oc=oc, s0=s0:
                             v.scalar_tensor_tensor(o[:, s0:s0 + SEG],
                                                    pb[:, :],
                                                    bc[:, oc:oc + 1],
                                                    o[:, s0:s0 + SEG],
                                                    ALU.add, ALU.add)),
                      (("t", td),))
            c.pguard[oc] = ("v", ev)
        bki[0] = 0
    fence(c)
    sc.close()


def emit_l1_ffn(c, sbp, io, first):
    sc = Scope(c.nc)
    sb = dict(sbp)
    sb["xn"] = [sc.t(f"fxn{io['uid']}_{i}", [128, TOK], BF16)
                for i in range(DCH)]
    lnp = Scope(c.nc)
    sb.update(alloc_ln_pack(lnp))
    ln_done = emit_ln(c, sb, "plain")
    ln_fence = (("v", c.now("v")), ("t", c.now("t")), ("s", c.now("s")))
    lnp.close()
    u = io["uid"]

    if first:
        bsc = Scope(c.nc)
        wsl = bsc.t(f"bwsl{u}", [128, 512], F32)
        sgn = bsc.t(f"bsgn{u}", [128, 512], BF16)
        ab = bsc.t(f"bab{u}", [128, 512], BF16)
        arow = bsc.t(f"barow{u}", [1, 512], F32)
        bwsh = bsc.t(f"bwsh{u}", [128, 512], BF16)
        sb["tmpbc"] = bsc.t(f"btmpbc{u}", [128, 640], F32)

        def binarize(w_sh_hbm, nch, ncols, alpha_mm_bank):
            for chx in range(nch):
                ld = c.dma("y", P(lambda y, o=wsl, i=w_sh_hbm, chx=chx,
                                  nc_=ncols:
                                  y.dma_start(out=o[:, 0:nc_],
                                              in_=i[chx * 128:(chx + 1) * 128, :])),
                           (("s", c.now("s")),)
                           + (ln_fence if chx == 0 else ()), stream=f"bz{u}")
                aa = c.op("s", P(lambda s, o=ab, i=wsl, nc_=ncols:
                                 s.activation(o[:, 0:nc_], i[:, 0:nc_],
                                              AF.Abs)), (ld, ("t", c.now("t"))))
                w = [("s", aa)]
                if chx == 0:
                    w.append(c.pguard[alpha_mm_bank])
                c.op("t", P(lambda t, pb=c.pb[alpha_mm_bank], lh=sb["ones_bf"],
                            r=ab, st0=(chx == 0), sp=(chx == nch - 1),
                            nc_=ncols:
                            t.matmul(pb[0:1, 0:nc_], lh[:, 0:1], r[:, 0:nc_],
                                     start=st0, stop=sp)), w)

        binarize(io["w1T_sh"], 8, 512, 0)
        ev = c.op("v", P(lambda v, o=arow, pb=c.pb[0]:
                         v.tensor_scalar(o[0:1, 0:512], pb[0:1, 0:512],
                                         1.0 / D, None, ALU.mult)),
                  (("t", c.now("t")),))
        c.pguard[0] = ("v", ev)
        a1bc = c.op("t", P(lambda t, pb=c.pb[1], lh=sb["ones1_f"], r=arow:
                           t.matmul(pb[:, :], lh[0:1, :], r[0:1, 0:512],
                                    start=True, stop=True)),
                    [("v", ev), c.pguard[1]])
        a1e = c.op("v", P(lambda v, o=sb["tmpbc"], pb=c.pb[1]:
                          v.tensor_scalar(o[:, 0:512], pb[:, 0:512], 1.0,
                                          None, ALU.mult)), (("t", a1bc),))
        c.pguard[1] = ("v", a1e)
        pst = ("v", 0)
        for chx in range(8):
            ld = c.dma("y", P(lambda y, o=wsl, i=io["w1T_sh"], chx=chx:
                              y.dma_start(out=o[:],
                                          in_=i[chx * 128:(chx + 1) * 128, :])),
                       (("s", c.now("s")), pst), stream=f"bz{u}")
            sg = c.op("s", P(lambda s, o=sgn, i=wsl:
                             s.activation(o[:], i[:], AF.Sign)), (ld,))
            ml = c.op("v", P(lambda v, o=bwsh, a=sgn, b=sb["tmpbc"]:
                             v.tensor_tensor(o[:], a[:], b[:, 0:512],
                                             ALU.mult)),
                      (("s", sg), ("v", a1e), pst))
            pst = c.dma("y", P(lambda y, i=bwsh, o=io["ag1_in"], chx=chx:
                               y.dma_start(out=o.ap()[chx * 128:(chx + 1) * 128, :],
                                           in_=i[:])), (("v", ml),),
                        stream=f"bzs{u}")
        ag1s = c.collective("AllGather", ALU.bypass, io["ag1_in"],
                            io["ag1_out"], (pst,))
        binarize(io["w2T_sh"], 32, 128, 2)
        ev2 = c.op("v", P(lambda v, o=arow, pb=c.pb[2]:
                          v.tensor_scalar(o[0:1, 0:128], pb[0:1, 0:128],
                                          1.0 / 4096.0, None, ALU.mult)),
                   (("t", c.now("t")),))
        c.pguard[2] = ("v", ev2)
        a2bc = c.op("t", P(lambda t, pb=c.pb[3], lh=sb["ones1_f"], r=arow:
                           t.matmul(pb[:, 0:128], lh[0:1, :], r[0:1, 0:128],
                                    start=True, stop=True)),
                    [("v", ev2), c.pguard[3]])
        a2e = c.op("v", P(lambda v, o=sb["tmpbc"], pb=c.pb[3]:
                          v.tensor_scalar(o[:, 512:640], pb[:, 0:128], 1.0,
                                          None, ALU.mult)), (("t", a2bc),))
        c.pguard[3] = ("v", a2e)
        pst2 = pst
        for chx in range(32):
            ld = c.dma("y", P(lambda y, o=wsl, i=io["w2T_sh"], chx=chx:
                              y.dma_start(out=o[:, 0:128],
                                          in_=i[chx * 128:(chx + 1) * 128, :])),
                       (("s", c.now("s")), pst2), stream=f"bz{u}")
            sg = c.op("s", P(lambda s, o=sgn, i=wsl:
                             s.activation(o[:, 0:128], i[:, 0:128], AF.Sign)),
                      (ld,))
            ml = c.op("v", P(lambda v, o=bwsh, a=sgn, b=sb["tmpbc"]:
                             v.tensor_tensor(o[:, 0:128], a[:, 0:128],
                                             b[:, 512:640], ALU.mult)),
                      (("s", sg), ("v", a2e), pst2))
            pst2 = c.dma("y", P(lambda y, i=bwsh, o=io["ag2_in"], chx=chx:
                                y.dma_start(out=o.ap()[chx * 128:(chx + 1) * 128, :],
                                            in_=i[:, 0:128])), (("v", ml),),
                         stream=f"bzs{u}")
        ag2s = c.collective("AllGather", ALU.bypass, io["ag2_in"],
                            io["ag2_out"], (pst2,))
        io["bz_fence"] = (pst2, ("s", c.now("s")), ("t", c.now("t")),
                          ("v", c.now("v")))
        bsc.close()
        io["ag_sems"] = (ag1s, ag2s)
    ag1s, ag2s = io["ag_sems"]
    bz_fence = io.get("bz_fence", ())

    bw1 = [sc.t(f"fbw1_{u}_{i}", [128, 4096], BF16) for i in range(DCH)]
    b1c = sc.t(f"fb1c{u}", [128, 32], F32)
    b2c = sc.t(f"fb2c{u}", [128, 8], F32)
    hbf = [sc.t(f"fh{u}_{i}", [128, 512], BF16) for i in range(32)]
    bw2 = [sc.t(f"fbw2_{u}_{i}", [128, 1024], BF16) for i in range(2)]
    ag1r = io["ag1_out"].ap().rearrange("(r k p) h -> k p r h", r=8, k=8,
                                        p=128)
    ag2r = io["ag2_out"].ap().rearrange("(r h p) q -> h p r q", r=8, h=32,
                                        p=128)
    for kc in range(DCH):
        c.dma("y", P(lambda y, o=bw1[kc], i=ag1r, kc=kc:
                     y.dma_start(out=o[:].rearrange("p (r h) -> p r h", r=8),
                                 in_=i[kc])),
              (ag1s,) + (tuple(bz_fence) + ln_fence if kc == 0 else ()),
              stream=f"fw{u}")
    c.dma("y", P(lambda y, o=b1c, i=io["b1c"]: y.dma_start(out=o[:], in_=i)),
          stream=f"fw{u}")
    c.dma("y", P(lambda y, o=b2c, i=io["b2c"]: y.dma_start(out=o[:], in_=i)),
          stream=f"fw{u}")
    wready = c.dma_all()

    bw2_prev = [("t", 0), ("t", 0)]
    for q in range(4):
        q0 = q * 512
        qw = [("v", c.now("v")), ("s", c.now("s")), ("t", c.now("t"))]
        for hc in range(32):
            k = hc % 8
            pbk = c.pb[k]
            td = 0
            for kc in range(DCH):
                w = [("v", ln_done)] + list(wready)
                if kc == 0:
                    w.append(c.pguard[k])
                    w += qw
                td = c.op("t", P(lambda t, pb=pbk, lh=bw1[kc], hc=hc,
                                 r=sb["xn"][kc], q0=q0, st0=(kc == 0),
                                 sp=(kc == DCH - 1):
                                 t.matmul(pb[:, :], lh[:, hc * 128:(hc + 1) * 128],
                                          r[:, q0:q0 + 512], start=st0,
                                          stop=sp)), w)
            ge = c.op("s", P(lambda s, o=hbf[hc], pb=pbk, b=b1c, hc=hc:
                             s.activation(o[:], pb[:, :], AF.Gelu,
                                          bias=b[:, hc:hc + 1])),
                      (("t", td),))
            c.pguard[k] = ("s", ge)
        h_done = ("s", c.now("s"))
        td = 0
        for hc in range(32):
            wb = hc % 2
            ld = c.dma("g", P(lambda g, o=bw2[wb], i=ag2r, hc=hc:
                              g.dma_start(out=o[:].rearrange(
                                  "p (r q) -> p r q", r=8), in_=i[hc])),
                       (bw2_prev[wb], ag2s)
                       + (tuple(bz_fence) + ln_fence if hc < 2 else ()),
                       stream=f"f2w{u}{wb}")
            for oc in range(DCH):
                w = [h_done, ld]
                if hc == 0:
                    w.append(c.pguard[oc])
                td = c.op("t", P(lambda t, pb=c.pb[oc], lh=bw2[wb],
                                 o0=oc * 128, r=hbf[hc], st0=(hc == 0),
                                 sp=(hc == 31):
                                 t.matmul(pb[:, :], lh[:, o0:o0 + 128],
                                          r[:, :], start=st0, stop=sp)), w)
            bw2_prev[wb] = ("t", td)
        for oc in range(DCH):
            ev = c.op("v", P(lambda v, o=sb["x"][oc], pb=c.pb[oc], bc=b2c,
                             oc=oc, q0=q0:
                             v.scalar_tensor_tensor(o[:, q0:q0 + 512],
                                                    pb[:, :],
                                                    bc[:, oc:oc + 1],
                                                    o[:, q0:q0 + 512],
                                                    ALU.add, ALU.add)),
                      (("t", td),))
            c.pguard[oc] = ("v", ev)
    fence(c)
    sc.close()


def emit_l0_attn_b(c, sbp, io):
    sc = Scope(c.nc)
    sb = dict(sbp)
    sb["xn"] = [sc.t(f"bxn{i}", [128, TOK], BF16) for i in range(DCH)]
    lnp = Scope(c.nc)
    sb.update(alloc_ln_pack(lnp))
    ln_done = emit_ln(c, sb, "plain")
    ln_fence = (("v", c.now("v")), ("t", c.now("t")), ("s", c.now("s")))
    lnp.close()

    qf = [sc.t(f"bqf{i}", [128, TOK], BF16) for i in range(DCH)]
    bffc = sc.t("bbffc", [128, 8], F32)
    psc = Scope(c.nc)
    bffl = None
    wres = [psc.t(f"bw{i}", [128, D], BF16) for i in range(DCH)]
    kstage = psc.t("bkst", [128, TOK], BF16)
    vaug = psc.t("bvaug", [128, 16 * 65], BF16)
    bffl = c.dma("y", P(lambda y, o=bffc, i=io["b_bff"]:
                        y.dma_start(out=o[:], in_=i)), ln_fence, stream="bb")

    bki = [0]

    def bank():
        k = bki[0] % 8
        bki[0] += 1
        return k

    def load_w(w_hbm, prev_done):
        lds = []
        for kc in range(DCH):
            lds.append(c.dma("g", P(lambda g, o=wres[kc], i=w_hbm, kc=kc:
                             g.dma_start(out=o[:],
                                         in_=i[kc * 128:(kc + 1) * 128, :])),
                             (prev_done if kc == 0 else ()), stream="bwl"))
        return lds[-1]

    ldq = load_w(io["b_wqT"], ln_fence)
    for oc in range(DCH):
        for piece in range(4):
            n0 = piece * 512
            k = bank()
            td = 0
            for kc in range(DCH):
                w = [("v", ln_done), ldq]
                if kc == 0:
                    w.append(c.pguard[k])
                td = c.op("t", P(lambda t, pb=c.pb[k], lh=wres[kc],
                                 o0=oc * 128, r=sb["xn"][kc], n0=n0,
                                 st0=(kc == 0), sp=(kc == DCH - 1):
                                 t.matmul(pb[:, :], lh[:, o0:o0 + 128],
                                          r[:, n0:n0 + 512], start=st0,
                                          stop=sp)), w)
            ev = c.op("v", P(lambda v, o=qf[oc], pb=c.pb[k], n0=n0:
                             v.tensor_scalar(o[:, n0:n0 + 512], pb[:, :],
                                             1.0, None, ALU.mult)),
                      (("t", td),))
            c.pguard[k] = ("v", ev)
    q_done = ("v", c.now("v"))
    ldk = load_w(io["b_wkT"], (("t", c.now("t")),))
    kst_prev = ("sd_bkg", 0)
    for oc in range(DCH):
        evs = []
        for piece in range(4):
            n0 = piece * 512
            k = bank()
            td = 0
            for kc in range(DCH):
                w = [("v", ln_done), ldk]
                if kc == 0:
                    w.append(c.pguard[k])
                td = c.op("t", P(lambda t, pb=c.pb[k], lh=wres[kc],
                                 o0=oc * 128, r=sb["xn"][kc], n0=n0,
                                 st0=(kc == 0), sp=(kc == DCH - 1):
                                 t.matmul(pb[:, :], lh[:, o0:o0 + 128],
                                          r[:, n0:n0 + 512], start=st0,
                                          stop=sp)), w)
            ev = c.op("v", P(lambda v, o=kstage, pb=c.pb[k], n0=n0:
                             v.tensor_scalar(o[:, n0:n0 + 512], pb[:, :],
                                             1.0, None, ALU.mult)),
                      (("t", td), kst_prev))
            evs.append(ev)
            c.pguard[k] = ("v", ev)
        kst_prev = c.dma("y", P(lambda y, i=kstage, o=io["agk_in"], oc=oc:
                                y.dma_start(out=o.ap()[oc * 128:(oc + 1) * 128, :],
                                            in_=i[:])),
                         (("v", evs[-1]),), stream="bkg")
    ldv = load_w(io["b_wvT"], (("t", c.now("t")),))
    vg_set = c.op("v", P(lambda v, o=vaug: v.memset(o[:], 1.0)),
                  (("v", c.now("v")),))
    vst_prev = ("sd_bvg", 0)
    for tc in range(16):
        bs = [bank(), bank()]
        tds = []
        for half in range(2):
            td = 0
            for kc in range(DCH):
                w = [("v", ln_done), ldv]
                if kc == 0:
                    w.append(c.pguard[bs[half]])
                td = c.op("t", P(lambda t, pb=c.pb[bs[half]], lh=sb["xn"][kc],
                                 tc=tc, r=wres[kc], half=half,
                                 st0=(kc == 0), sp=(kc == DCH - 1):
                                 t.matmul(pb[:, :],
                                          lh[:, tc * 128:(tc + 1) * 128],
                                          r[:, half * 512:half * 512 + 512],
                                          start=st0, stop=sp)), w)
            tds.append(td)
        evl = 0
        for h in range(16):
            half, col = h // 8, (h % 8) * 64
            evl = c.op("v", P(lambda v, o=vaug, pb=c.pb[bs[half]], h=h,
                              col=col:
                              v.tensor_scalar(o[:, h * 65:h * 65 + 64],
                                              pb[:, col:col + 64], 1.0,
                                              None, ALU.mult)),
                       [("t", tds[half]), ("v", vg_set), vst_prev])
        c.pguard[bs[0]] = ("v", evl)
        c.pguard[bs[1]] = ("v", evl)
        vst_prev = c.dma("y", P(lambda y, i=vaug, o=io["agv_in"], tc=tc:
                                y.dma_start(out=o.ap()[tc * 128:(tc + 1) * 128, :],
                                            in_=i[:])),
                         (("v", evl),), stream="bvg")
    agk = c.collective("AllGather", ALU.bypass, io["agk_in"], io["agk_out"],
                       (kst_prev,))
    agv = c.collective("AllGather", ALU.bypass, io["agv_in"], io["agv_out"],
                       (vst_prev,))
    proj_fence = (("v", c.now("v")), ("t", c.now("t")), ("s", c.now("s")),
                  kst_prev, vst_prev)
    psc.close()
    attn = sb["xn"]
    Kt = [sc.t(f"bK{i}", [128, S], BF16) for i in range(2)]
    Vt = [sc.t(f"bV{i}", [128, 64 * 65], BF16) for i in range(2)]
    qh = [sc.t(f"bqh{i}", [128, 1024], BF16) for i in range(2)]
    zms = None
    for tl in (Kt[0], Kt[1], qh[0], qh[1]):
        zms = c.op("v", P(lambda v, o=tl: v.memset(o[64:128, :], 0.0)),
                   () if zms is None else (("v", zms),))
    zms = ("v", zms)
    pts = [sc.t(f"bpts{i}", [128, 512], BF16) for i in range(6)]
    oun = sc.t("boun", [128, 1024], F32)
    rcp = sc.t("brcp", [1, 1024], F32)
    agvr = io["agv_out"].ap().rearrange("(r b j p) cc -> r b p j cc", r=8,
                                        b=2, j=8, p=128)
    kv_free = [("t", 0), ("t", 0)]
    mu_prev = ("v", 0)
    ev_prev = ("s", 0)
    bhs = [(b, h) for b in range(B) for h in range(16)]
    kvld = {}

    def load_kv_l0b(idx):
        b, h = bhs[idx]
        pr = idx % 2
        lk = lv = None
        for r in range(NC):
            lk = c.dma("y", P(lambda y, o=Kt[pr], i=io["agk_out"], r=r, h=h,
                              b=b:
                              y.dma_start(out=o[0:64,
                                               r * 1024:(r + 1) * 1024],
                                          in_=i.ap()[r * 1024 + (h // 2) * 128
                                                     + (h % 2) * 64:
                                                     r * 1024 + (h // 2) * 128
                                                     + (h % 2) * 64 + 64,
                                                     b * 1024:(b + 1) * 1024])),
                       [agk, kv_free[pr]] + (list(proj_fence) if idx == 0
                                             else []), stream=f"bkl{pr}")
            lv = c.dma("y", P(lambda y, o=Vt[pr], i=agvr, r=r, h=h, b=b:
                              y.dma_start(out=o[:, r * 520:(r + 1) * 520]
                                          .rearrange("p (j cc) -> p j cc",
                                                     j=8),
                                          in_=i[r, b][:, :,
                                                      h * 65:(h + 1) * 65])),
                       [agv, kv_free[pr]], stream=f"bvl{pr}")
        qc = c.op("v", P(lambda v, o=qh[pr], i=qf[h // 2], row=(h % 2) * 64,
                         b=b:
                         v.tensor_scalar(o[0:64, :],
                                         i[row:row + 64,
                                           b * 1024:(b + 1) * 1024],
                                         1.0, None, ALU.mult)),
                  [q_done, ("v", c.now("v")), kv_free[pr]])
        kvld[idx] = (lk, lv, ("v", qc))

    load_kv_l0b(0)
    for idx, (b, h) in enumerate(bhs):
        oc_h, row = h // 2, (h % 2) * 64
        pr = idx % 2
        lk, lv, qc = kvld[idx]
        if idx + 1 < len(bhs):
            load_kv_l0b(idx + 1)
        ko = [bank(), bank()]
        av_init = c.now("t")
        avh = {}
        steps = [(kc, piece) for piece in range(2) for kc in range(64)]
        DEFER = 4
        pendq = []
        for si, (kc, piece) in enumerate(steps):
            ksb = bank()
            while ksb in ko:
                ksb = bank()
            tmm = c.op("t", P(lambda t, pb=c.pb[ksb], K=Kt[pr], kc=kc,
                              q=qh[pr], n0=piece * 512:
                              t.matmul(pb[:, :],
                                       K[:, kc * 128:(kc + 1) * 128],
                                       q[:, n0:n0 + 512],
                                       start=True, stop=True)),
                       [lk, lv, qc, zms, c.pguard[ksb]])
            ptb = si % 6
            ex = c.op("s", P(lambda s, o=pts[ptb], pb=c.pb[ksb]:
                             s.activation(o[:], pb[:, :], AF.Exp,
                                          scale=0.03125)),
                      [("t", tmm), ("t", avh.get(si - 6, av_init))])
            c.pguard[ksb] = ("s", ex)
            pendq.append((ptb, ex, piece, kc, si))
            if len(pendq) > DEFER:
                p_ptb, p_ex, p_piece, p_kc, p_si = pendq.pop(0)
                w = [("s", p_ex)]
                if p_kc == 0:
                    w.append(c.pguard[ko[p_piece]])
                avh[p_si] = c.op("t", P(lambda t, pb=c.pb[ko[p_piece]],
                                        V=Vt[pr],
                                        kc=p_kc, r=pts[p_ptb],
                                        st0=(p_kc == 0),
                                        sp=(p_kc == 63):
                                        t.matmul(pb[0:65, :],
                                                 V[:, kc * 65:(kc + 1) * 65],
                                                 r[:, :], start=st0,
                                                 stop=sp)), w)
        for p_ptb, p_ex, p_piece, p_kc, p_si in pendq:
            av_last = c.op("t", P(lambda t, pb=c.pb[ko[p_piece]], V=Vt[pr],
                                  kc=p_kc, r=pts[p_ptb], st0=(p_kc == 0),
                                  sp=(p_kc == 63):
                                  t.matmul(pb[0:65, :],
                                           V[:, kc * 65:(kc + 1) * 65],
                                           r[:, :], start=st0, stop=sp)),
                           (("s", p_ex),))
        pendq = []
        kv_free[pr] = ("t", av_last)
        for piece in range(2):
            ev = c.op("v", P(lambda v, o=oun, pb=c.pb[ko[piece]],
                             n0=piece * 512:
                             v.tensor_scalar(o[0:65, n0:n0 + 512],
                                             pb[0:65, :], 1.0, None,
                                             ALU.mult)),
                      [("t", av_last), mu_prev])
            c.pguard[ko[piece]] = ("v", ev)
            ev_prev = ("v", ev)
        rc = c.op("v", P(lambda v, o=rcp, i=oun:
                         v.reciprocal(o[0:1, :], i[64:65, :])),
                  [ev_prev, ("v", c.now("v")), ("t", c.now("t"))])
        for piece in range(2):
            kb = bank()
            while kb in ko:
                kb = bank()
            tb2 = c.op("t", P(lambda t, pb=c.pb[kb], lh=sb["ones1_f"],
                              r=rcp, n0=piece * 512:
                              t.matmul(pb[0:64, :], lh[0:1, 0:64],
                                       r[0:1, n0:n0 + 512], start=True,
                                       stop=True)),
                       [("v", rc), c.pguard[kb]])
            mu = c.op("v", P(lambda v, o=attn[oc_h], row=row, a=oun,
                             pb=c.pb[kb], b=b, n0=piece * 512:
                             v.tensor_tensor(o[row:row + 64,
                                               b * 1024 + n0:
                                               b * 1024 + n0 + 512],
                                             a[0:64, n0:n0 + 512],
                                             pb[0:64, :], ALU.mult)),
                      (("t", tb2),))
            c.pguard[kb] = ("v", mu)
            mu_prev = ("v", mu)
    attn_done = ("v", c.now("v"))
    wo2 = [sc.t(f"bwo{i}", [128, D], BF16) for i in range(2)]
    wo_prev = [("t", 0), ("t", 0)]
    for piece in range(4):
        n0 = piece * 512
        td = 0
        for kc in range(DCH):
            wb = kc % 2
            ld = c.dma("g", P(lambda g, o=wo2[wb], i=io["b_wffT"], kc=kc:
                              g.dma_start(out=o[:],
                                          in_=i[kc * 128:(kc + 1) * 128, :])),
                       (wo_prev[wb],), stream=f"bwo{wb}")
            for oc in range(DCH):
                w = [attn_done, ld]
                if kc == 0:
                    w.append(c.pguard[oc])
                td = c.op("t", P(lambda t, pb=c.pb[oc], lh=wo2[wb],
                                 o0=oc * 128, r=attn[kc], n0=n0,
                                 st0=(kc == 0), sp=(kc == DCH - 1):
                                 t.matmul(pb[:, :], lh[:, o0:o0 + 128],
                                          r[:, n0:n0 + 512], start=st0,
                                          stop=sp)), w)
            wo_prev[wb] = ("t", td)
        for oc in range(DCH):
            ev = c.op("v", P(lambda v, o=sb["x"][oc], pb=c.pb[oc], bc=bffc,
                             oc=oc, n0=n0:
                             v.scalar_tensor_tensor(o[:, n0:n0 + 512],
                                                    pb[:, :],
                                                    bc[:, oc:oc + 1],
                                                    o[:, n0:n0 + 512],
                                                    ALU.add, ALU.add)),
                      (("t", td), bffl))
            c.pguard[oc] = ("v", ev)
    fence(c)
    sc.close()


def emit_l1_attn_b(c, sbp, io):
    sc = Scope(c.nc)
    sb = dict(sbp)
    masks2 = sc.t("mmask", [128, 4 * 512], BF16)
    sb["xn"] = [sc.t(f"mxn{i}", [128, TOK], BF16) for i in range(DCH)]
    lnp = Scope(c.nc)
    sb.update(alloc_ln_pack(lnp))
    ln_done = emit_ln(c, sb, "plain")
    ln_fence = (("v", c.now("v")), ("t", c.now("t")), ("s", c.now("s")))
    lnp.close()
    qcf = [sc.t(f"mqc{i}", [128, TOK], BF16) for i in range(DCH)]
    qrf = [sc.t(f"mqr{i}", [128, TOK], BF16) for i in range(4)]

    psc = Scope(c.nc)
    cos2 = psc.t("mcos", [128, TOK], BF16)
    sin2 = psc.t("msin", [128, TOK], BF16)
    rt2 = psc.t("mrt", [128, 128], BF16)
    cqf = [psc.t(f"mcq{i}", [128, TOK], BF16) for i in range(2)]
    ckf = [psc.t(f"mck{i}", [128, TOK], BF16) for i in range(2)]
    wdq = [psc.t(f"mwdq{i}", [128, 256], BF16) for i in range(DCH)]
    wdkv = [psc.t(f"mwdkv{i}", [128, 256], BF16) for i in range(DCH)]
    wu = [psc.t(f"mwu{i}", [128, D], BF16) for i in range(2)]
    wkr = [psc.t(f"mwkr{i}", [128, 32], BF16) for i in range(DCH)]
    kstage = psc.t("mkst", [128, TOK], BF16)
    krst = psc.t("mkrst", [32, TOK], BF16)
    vaug = psc.t("mvaug", [128, 16 * 65], BF16)
    rtmp1 = psc.t("mrt1", [128, 512], F32)
    rtmp2 = psc.t("mrt2", [128, 512], F32)
    c.dma("g", P(lambda g, o=cos2, i=io["m_cos"]:
                 g.dma_start(out=o[:], in_=i)), ln_fence, stream="mw")
    c.dma("g", P(lambda g, o=sin2, i=io["m_sin"]:
                 g.dma_start(out=o[:], in_=i)), stream="mw")
    c.dma("g", P(lambda g, o=rt2, i=io["rotT"]:
                 g.dma_start(out=o[:], in_=i)), stream="mw")
    c.dma("g", P(lambda g, o=masks2, i=io["amask"]:
                 g.dma_start(out=o[:].rearrange("p (k q) -> p k q", k=4),
                             in_=i.rearrange("(k p) q -> p k q", p=128))),
          stream="mw")
    for ch in range(DCH):
        c.dma("g", P(lambda g, o=wdq[ch], i=io["m_wdq"], ch=ch:
                     g.dma_start(out=o[:], in_=i[ch * 128:(ch + 1) * 128, :])),
              ln_fence if ch == 0 else (), stream="mw")
        c.dma("g", P(lambda g, o=wdkv[ch], i=io["m_wdkv"], ch=ch:
                     g.dma_start(out=o[:], in_=i[ch * 128:(ch + 1) * 128, :])),
              stream="mw")
        c.dma("g", P(lambda g, o=wkr[ch], i=io["m_wkr"], ch=ch:
                     g.dma_start(out=o[:], in_=i[ch * 128:(ch + 1) * 128, :])),
              stream="mw")
    wready = c.dma_all()

    bki = [0]

    def bank():
        k = bki[0] % 8
        bki[0] += 1
        return k

    for dst, wt in ((cqf, wdq), (ckf, wdkv)):
        for oc in range(2):
            for piece in range(4):
                n0 = piece * 512
                k = bank()
                td = 0
                for kc in range(DCH):
                    w = [("v", ln_done)] + list(wready)
                    if kc == 0:
                        w.append(c.pguard[k])
                    td = c.op("t", P(lambda t, pb=c.pb[k], lh=wt[kc],
                                     o0=oc * 128, r=sb["xn"][kc], n0=n0,
                                     st0=(kc == 0), sp=(kc == DCH - 1):
                                     t.matmul(pb[:, :], lh[:, o0:o0 + 128],
                                              r[:, n0:n0 + 512], start=st0,
                                              stop=sp)), w)
                ev = c.op("s", P(lambda s, o=dst[oc], pb=c.pb[k], n0=n0:
                                 s.activation(o[:, n0:n0 + 512], pb[:, :],
                                              AF.Identity)), (("t", td),))
                c.pguard[k] = ("s", ev)
    cq_done = ("s", c.now("s"))
    for piece in range(4):
        n0 = piece * 512
        k = bank()
        td = 0
        for kc in range(DCH):
            w = [("v", ln_done)] + list(wready)
            if kc == 0:
                w.append(c.pguard[k])
            td = c.op("t", P(lambda t, pb=c.pb[k], lh=wkr[kc],
                             r=sb["xn"][kc], n0=n0, st0=(kc == 0),
                             sp=(kc == DCH - 1):
                             t.matmul(pb[0:32, :], lh[:, :],
                                      r[:, n0:n0 + 512], start=st0,
                                      stop=sp)), w)
        ev = c.op("s", P(lambda s, o=krst, pb=c.pb[k], n0=n0:
                         s.activation(o[0:32, n0:n0 + 512], pb[0:32, :],
                                      AF.Identity)), (("t", td),))
        c.pguard[k] = ("s", ev)
    kr_ev = ("s", c.now("s"))
    rot_last = ("v", c.now("v"))
    for piece in range(4):
        n0 = piece * 512
        k2 = bank()
        trk = c.op("t", P(lambda t, pb=c.pb[k2], lh=rt2, r=krst, n0=n0:
                          t.matmul(pb[0:32, 0:512], lh[0:32, 0:32],
                                   r[0:32, n0:n0 + 512], start=True,
                                   stop=True)), [kr_ev, c.pguard[k2]])
        v1 = c.op("v", P(lambda v, o=rtmp1, a=krst, cx=cos2, n0=n0:
                         v.tensor_tensor(o[0:32, 0:512],
                                         a[0:32, n0:n0 + 512],
                                         cx[0:32, n0:n0 + 512], ALU.mult)),
                  [kr_ev, rot_last])
        v2 = c.op("v", P(lambda v, o=rtmp2, pb=c.pb[k2], sx=sin2, n0=n0:
                         v.tensor_tensor(o[0:32, 0:512], pb[0:32, 0:512],
                                         sx[0:32, n0:n0 + 512], ALU.mult)),
                  (("t", trk), rot_last))
        rot_last = ("v", c.op("v", P(lambda v, o=krst, a=rtmp1, b=rtmp2,
                                     n0=n0:
                                     v.tensor_tensor(o[0:32, n0:n0 + 512],
                                                     a[0:32, 0:512],
                                                     b[0:32, 0:512],
                                                     ALU.add)),
                              (("v", v1), ("v", v2))))
        c.pguard[k2] = ("v", v2)
    krg = c.dma("y", P(lambda y, i=krst, o=io["agkr_in"]:
                       y.dma_start(out=o.ap(), in_=i[0:32, :])),
                (rot_last,), stream="mkr")
    wu_prev = [("t", 0)]

    def load_wu(w_hbm, cols):
        lds = []
        for kc in range(2):
            lds.append(c.dma("g", P(lambda g, o=wu[kc], i=w_hbm, kc=kc,
                                    cc=cols:
                                    g.dma_start(out=o[:, 0:cc],
                                                in_=i[kc * 128:(kc + 1) * 128, :])),
                             (wu_prev[0],), stream="mwu"))
        return lds[-1]

    def proj256(src, dout, evac_cb, ld):
        for oc in range(dout // 128):
            for piece in range(4):
                n0 = piece * 512
                k = bank()
                td = 0
                for kc in range(2):
                    w = [cq_done, ld]
                    if kc == 0:
                        w.append(c.pguard[k])
                    td = c.op("t", P(lambda t, pb=c.pb[k], lh=wu[kc],
                                     o0=oc * 128, r=src[kc], n0=n0,
                                     st0=(kc == 0), sp=(kc == 1):
                                     t.matmul(pb[:, :], lh[:, o0:o0 + 128],
                                              r[:, n0:n0 + 512],
                                              start=st0, stop=sp)), w)
                c.pguard[k] = evac_cb(oc, n0, k, ("t", td))
        wu_prev[0] = ("t", c.now("t"))

    ld = load_wu(io["m_wuq"], D)
    proj256(cqf, D, lambda oc, n0, k, tg: ("s", c.op(
        "s", P(lambda s, o=qcf[oc], pb=c.pb[k], n0=n0:
               s.activation(o[:, n0:n0 + 512], pb[:, :], AF.Identity)),
        (tg,))), ld)
    ld = load_wu(io["m_wqr"], 512)
    proj256(cqf, 512, lambda oc, n0, k, tg: ("s", c.op(
        "s", P(lambda s, o=qrf[oc], pb=c.pb[k], n0=n0:
               s.activation(o[:, n0:n0 + 512], pb[:, :], AF.Identity)),
        (tg,))), ld)
    qr_ev = ("s", c.now("s"))
    rp_prev = ("v", c.now("v"))
    for oc in range(4):
        for piece in range(4):
            n0 = piece * 512
            k2 = bank()
            trq = c.op("t", P(lambda t, pb=c.pb[k2], lh=rt2, r=qrf[oc],
                              n0=n0:
                              t.matmul(pb[:, :], lh[:, :],
                                       r[:, n0:n0 + 512], start=True,
                                       stop=True)), [qr_ev, c.pguard[k2]])
            v1 = c.op("v", P(lambda v, o=rtmp1, a=qrf[oc], cx=cos2, n0=n0:
                             v.tensor_tensor(o[:, 0:512],
                                             a[:, n0:n0 + 512],
                                             cx[:, n0:n0 + 512], ALU.mult)),
                      [qr_ev, rp_prev])
            v2 = c.op("v", P(lambda v, o=rtmp2, pb=c.pb[k2], sx=sin2, n0=n0:
                             v.tensor_tensor(o[:, 0:512], pb[:, :],
                                             sx[:, n0:n0 + 512], ALU.mult)),
                      (("t", trq), rp_prev))
            ve = c.op("v", P(lambda v, o=qrf[oc], a=rtmp1, b=rtmp2, n0=n0:
                             v.tensor_tensor(o[:, n0:n0 + 512],
                                             a[:, 0:512],
                                             b[:, 0:512], ALU.add)),
                      (("v", v1), ("v", v2)))
            c.pguard[k2] = ("v", v2)
            rp_prev = ("v", ve)
    ld = load_wu(io["m_wuk"], D)
    kst_prev = [("sd_mkg", 0)]

    def kc_evac(oc, n0, k, tg):
        ev = c.op("s", P(lambda s, o=kstage, pb=c.pb[k], n0=n0:
                         s.activation(o[:, n0:n0 + 512], pb[:, :],
                                      AF.Identity)), (tg, kst_prev[0]))
        if n0 == 3 * 512:
            kst_prev[0] = c.dma("y", P(lambda y, i=kstage, o=io["agkc_in"],
                                       oc=oc:
                                       y.dma_start(out=o.ap()[oc * 128:
                                                              (oc + 1) * 128, :],
                                                   in_=i[:])),
                                (("s", ev),), stream="mkg")
        return ("s", ev)
    proj256(ckf, D, kc_evac, ld)
    ld = load_wu(io["m_wuv"], D)
    vg_set = c.op("v", P(lambda v, o=vaug: v.memset(o[:], 1.0)),
                  (("v", c.now("v")),))
    vst_prev = ("sd_mvg", 0)
    for tc in range(16):
        bs = [bank(), bank()]
        tds = []
        for half in range(2):
            td = 0
            for kc in range(2):
                w = [cq_done, ld]
                if kc == 0:
                    w.append(c.pguard[bs[half]])
                td = c.op("t", P(lambda t, pb=c.pb[bs[half]], lh=ckf[kc],
                                 tc=tc, r=wu[kc], half=half, st0=(kc == 0),
                                 sp=(kc == 1):
                                 t.matmul(pb[:, :],
                                          lh[:, tc * 128:(tc + 1) * 128],
                                          r[:, half * 512:half * 512 + 512],
                                          start=st0, stop=sp)), w)
            tds.append(td)
        evl = 0
        for h in range(16):
            half, col = h // 8, (h % 8) * 64
            evl = c.op("v", P(lambda v, o=vaug, pb=c.pb[bs[half]], h=h,
                              col=col:
                              v.tensor_scalar(o[:, h * 65:h * 65 + 64],
                                              pb[:, col:col + 64], 1.0,
                                              None, ALU.mult)),
                       [("t", tds[half]), ("v", vg_set), vst_prev])
        c.pguard[bs[0]] = ("v", evl)
        c.pguard[bs[1]] = ("v", evl)
        vst_prev = c.dma("y", P(lambda y, i=vaug, o=io["agv2_in"], tc=tc:
                                y.dma_start(out=o.ap()[tc * 128:(tc + 1) * 128, :],
                                            in_=i[:])),
                         (("v", evl),), stream="mvg")
    agkc = c.collective("AllGather", ALU.bypass, io["agkc_in"],
                        io["agkc_out"], (kst_prev[0],))
    agkr = c.collective("AllGather", ALU.bypass, io["agkr_in"],
                        io["agkr_out"], (krg,))
    agv2 = c.collective("AllGather", ALU.bypass, io["agv2_in"],
                        io["agv2_out"], (vst_prev,))
    proj_fence = (("v", c.now("v")), ("t", c.now("t")), ("s", c.now("s")),
                  kst_prev[0], vst_prev, krg)
    psc.close()
    attn = sb["xn"]
    Kt = sc.t("mK", [96, S], BF16)
    Kd = [sc.t(f"mKd{i}", [96, 512], BF16) for i in range(1)]
    Vt = [sc.t(f"mV{i}", [128, 64 * 65], BF16) for i in range(2)]
    Vd = [sc.t(f"mVd{i}", [128, 4 * 65], BF16) for i in range(1)]
    qh = [sc.t(f"mqh{i}", [96, 1024], BF16) for i in range(2)]
    pts = [sc.t(f"mpts{i}", [128, 512], BF16) for i in range(6)]
    oun = sc.t("moun", [128, 512], F32)
    rcp = sc.t("mrcp", [1, 512], F32)
    gates = sc.t("mgates", [128, 128], F32)
    gl = c.dma("y", P(lambda y, o=gates, i=io["mgate"]:
                      y.dma_start(out=o[:], in_=i)), proj_fence, stream="mgl")
    agvr = io["agv2_out"].ap().rearrange("(r b j p) cc -> r b p j cc", r=8,
                                         b=2, j=8, p=128)
    agvloc = io["agv2_in"].ap().rearrange("(sl j p) cc -> sl p j cc", sl=4,
                                          j=4, p=128)
    SC96 = 1.0 / float(np.sqrt(96.0))
    CLIP_HI = float(np.exp(np.float64(80.0) * np.float64(np.float32(SC96))))
    CLIP_LO = float(np.exp(np.float64(-80.0) * np.float64(np.float32(SC96))))
    kv_free = [("t", 0), ("t", 0)]
    d_free = [("t", 0)] * 1
    mu_prev = ("v", 0)
    bhs = [(b, h) for b in range(B) for h in range(16)]
    kvld = {}

    def load_k_l1b(idx, kwait):
        b, h = bhs[idx]
        lks = {}
        for r in range(NC):
            c.dma("y", P(lambda y, o=Kt, i=io["agkc_out"], r=r, h=h,
                         b=b:
                         y.dma_start(out=o[0:64,
                                          r * 1024:(r + 1) * 1024],
                                     in_=i.ap()[r * 1024 + (h // 2) * 128
                                                + (h % 2) * 64:
                                                r * 1024 + (h // 2) * 128
                                                + (h % 2) * 64 + 64,
                                                b * 1024:(b + 1) * 1024])),
                  [agkc, kwait] + (list(proj_fence) if idx == 0
                                   else []), stream=f"mkl{r}")
            lks[r] = c.dma("y", P(lambda y, o=Kt, i=io["agkr_out"], r=r, b=b:
                                  y.dma_start(out=o[64:96,
                                                   r * 1024:(r + 1) * 1024],
                                              in_=i.ap()[r * 32:r * 32 + 32,
                                                         b * 1024:(b + 1)
                                                         * 1024])),
                           [agkr], stream=f"mkl{r}")
        kvld_k[idx] = lks

    def load_kv_l1b(idx):
        b, h = bhs[idx]
        pr = idx % 2
        lv = None
        for r in range(NC):
            lv = c.dma("y", P(lambda y, o=Vt[pr], i=agvr, r=r, h=h, b=b:
                              y.dma_start(out=o[:, r * 520:(r + 1) * 520]
                                          .rearrange("p (j cc) -> p j cc",
                                                     j=8),
                                          in_=i[r, b][:, :,
                                                      h * 65:(h + 1) * 65])),
                       [agv2, kv_free[pr]] + (list(proj_fence) if idx == 0
                                              else []), stream=f"mvl{pr}")
        qc1 = c.op("v", P(lambda v, o=qh[pr], i=qcf[h // 2],
                          row=(h % 2) * 64, b=b:
                          v.tensor_scalar(o[0:64, :],
                                          i[row:row + 64,
                                            b * 1024:(b + 1) * 1024],
                                          1.0, None, ALU.mult)),
                   [("v", c.now("v")), kv_free[pr]])
        qc2 = c.op("v", P(lambda v, o=qh[pr], i=qrf[h // 4],
                          qr=(h % 4) * 32, b=b:
                          v.tensor_scalar(o[64:96, :],
                                          i[qr:qr + 32,
                                            b * 1024:(b + 1) * 1024],
                                          1.0, None, ALU.mult)),
                   (("v", qc1),))
        kvld[idx] = (lv, ("v", qc2))
    kvld_k = {}

    def load_diag(idx2, b, h, sl):
        slot = 2 * b + sl
        dp = 0
        lkd = c.dma("y", P(lambda y, o=Kd[dp], i=io["agkc_in"], h=h,
                           slot=slot:
                           y.dma_start(out=o[0:64, :],
                                       in_=i.ap()[(h // 2) * 128
                                                  + (h % 2) * 64:
                                                  (h // 2) * 128
                                                  + (h % 2) * 64 + 64,
                                                  slot * 512:
                                                  slot * 512 + 512])),
                    [d_free[dp]], stream=f"mdl{dp}")
        lkd = c.dma("y", P(lambda y, o=Kd[dp], i=io["agkr_in"], slot=slot:
                           y.dma_start(out=o[64:96, :],
                                       in_=i.ap()[0:32,
                                                  slot * 512:
                                                  slot * 512 + 512])),
                    (), stream=f"mdl{dp}")
        lvd = c.dma("y", P(lambda y, o=Vd[dp], i=agvloc, h=h, slot=slot:
                           y.dma_start(out=o[:].rearrange(
                               "p (j cc) -> p j cc", j=4),
                               in_=i[slot][:, :, h * 65:(h + 1) * 65])),
                    (), stream=f"mdl{dp}")
        return dp, lvd

    load_kv_l1b(0)
    load_k_l1b(0, ("t", 0))
    dql = {}
    idx2 = 0
    for b0_, h0_ in bhs[:1]:
        pass
    dql[(0, 0)] = load_diag(0, bhs[0][0], bhs[0][1], 0)
    nd = 1
    for idx, (b, h) in enumerate(bhs):
        oc_h, row = h // 2, (h % 2) * 64
        pr = idx % 2
        lv, qc2 = kvld[idx]
        lks = kvld_k[idx]
        if idx + 1 < len(bhs):
            load_kv_l1b(idx + 1)
        for sl in range(2):
            n0q = sl * 512
            nchunk = 32 if sl == 0 else 64
            ko = bank()
            pbo = c.pb[ko]
            dp, lvd = dql[(idx, sl)]
            av_init = c.now("t")
            avh = {}
            DEFER = 5
            pendq = []
            for kc in range(nchunk + 4):
                diag = kc >= nchunk
                ksb = bank()
                while ksb == ko:
                    ksb = bank()
                if not diag:
                    if sl == 0:
                        kcol = (kc // 4) * 1024 + (kc % 4) * 128
                        vcol = ((kc // 4) * 8 + (kc % 4)) * 65
                    else:
                        kcol = kc * 128
                        vcol = kc * 65
                    tmm = c.op("t", P(lambda t, pb=c.pb[ksb], K=Kt,
                                      kcol=kcol, q=qh[pr], n0=n0q:
                                      t.matmul(pb[:, :],
                                               K[:, kcol:kcol + 128],
                                               q[:, n0:n0 + 512],
                                               start=True, stop=True)),
                               [lks[kcol // 1024], lv, qc2,
                                c.pguard[ksb]])
                else:
                    j = kc - nchunk
                    vcol = None
                    tmm = c.op("t", P(lambda t, pb=c.pb[ksb], K=Kd[dp], j=j,
                                      q=qh[pr], n0=n0q:
                                      t.matmul(pb[:, :],
                                               K[:, j * 128:(j + 1) * 128],
                                               q[:, n0:n0 + 512],
                                               start=True, stop=True)),
                               [lvd, qc2, c.pguard[ksb]])
                ptb = kc % 6
                ex = c.op("s", P(lambda s, o=pts[ptb], pb=c.pb[ksb]:
                                 s.activation(o[:], pb[:, :], AF.Exp,
                                              scale=SC96)),
                          [("t", tmm), ("t", avh.get(kc - 6, av_init))])
                cl = c.op("v", P(lambda v, o=pts[ptb]:
                                 v.tensor_scalar(o[:], o[:], CLIP_HI,
                                                 CLIP_LO, ALU.min,
                                                 ALU.max)), (("s", ex),))
                if not diag:
                    gt = c.op("v", P(lambda v, o=pts[ptb], g=gates, sl=sl,
                                     kc=kc:
                                     v.tensor_scalar(o[:], o[:],
                                                     g[:, sl * 64 + kc:
                                                       sl * 64 + kc + 1],
                                                     None, ALU.mult)),
                              (("v", cl), gl))
                else:
                    j = kc - nchunk
                    gt = c.op("v", P(lambda v, o=pts[ptb], m=masks2, j=j:
                                     v.tensor_tensor(o[:], o[:],
                                                     m[:, j * 512:
                                                       (j + 1) * 512],
                                                     ALU.mult)),
                              (("v", cl),))
                c.pguard[ksb] = ("v", gt)
                pendq.append((ptb, gt, kc, diag, vcol))
                if len(pendq) > DEFER:
                    p_ptb, p_gt, p_kc, p_diag, p_vcol = pendq.pop(0)
                    w = [("v", p_gt)]
                    if p_kc == 0:
                        w.append(c.pguard[ko])
                    Vsrc = Vt[pr] if not p_diag else Vd[dp]
                    vc2 = p_vcol if not p_diag else (p_kc - nchunk) * 65
                    avh[p_kc] = c.op("t", P(lambda t, pb=pbo, V=Vsrc, vc=vc2,
                                            r=pts[p_ptb], st0=(p_kc == 0):
                                            t.matmul(pb[0:65, :],
                                                     V[:, vc:vc + 65],
                                                     r[:, :], start=st0,
                                                     stop=False)),
                                     w + ([lvd] if p_diag else [lv]))
            for fi, (p_ptb, p_gt, p_kc, p_diag, p_vcol) in enumerate(pendq):
                Vsrc = Vt[pr] if not p_diag else Vd[dp]
                vc2 = p_vcol if not p_diag else (p_kc - nchunk) * 65
                av_last = c.op("t", P(lambda t, pb=pbo, V=Vsrc, vc=vc2,
                                      r=pts[p_ptb], st0=(p_kc == 0),
                                      sp=(fi == len(pendq) - 1):
                                      t.matmul(pb[0:65, :], V[:, vc:vc + 65],
                                               r[:, :], start=st0,
                                               stop=sp)),
                               [("v", p_gt)] + ([lvd] if p_diag else [lv]))
            pendq = []
            d_free[dp] = ("t", av_last)
            nxt = (idx, 1) if sl == 0 else (idx + 1, 0)
            if nxt[0] < len(bhs):
                dql[nxt] = load_diag(nd, bhs[nxt[0]][0], bhs[nxt[0]][1],
                                     nxt[1])
                nd += 1
            ev = c.op("s", P(lambda s, o=oun, pb=pbo:
                             s.activation(o[0:65, :],
                                          pb[0:65, :], AF.Identity)),
                      [("t", av_last), mu_prev])
            c.pguard[ko] = ("s", ev)
            rc = c.op("v", P(lambda v, o=rcp, i=oun:
                             v.reciprocal(o[0:1, :], i[64:65, :])),
                      [("s", ev), ("v", c.now("v")), ("t", c.now("t"))])
            kb = bank()
            while kb == ko:
                kb = bank()
            tb2 = c.op("t", P(lambda t, pb=c.pb[kb], lh=sb["ones1_f"],
                              r=rcp:
                              t.matmul(pb[0:64, :], lh[0:1, 0:64],
                                       r[0:1, :], start=True,
                                       stop=True)),
                       [("v", rc), c.pguard[kb]])
            mu = c.op("v", P(lambda v, o=attn[oc_h], row=row, a=oun,
                             pb=c.pb[kb], b=b, n0=n0q:
                             v.tensor_tensor(o[row:row + 64,
                                               b * 1024 + n0:
                                               b * 1024 + n0 + 512],
                                             a[0:64, :],
                                             pb[0:64, :], ALU.mult)),
                      (("t", tb2),))
            c.pguard[kb] = ("v", mu)
            mu_prev = ("v", mu)
        kv_free[pr] = ("t", c.now("t"))
        if idx + 1 < len(bhs):
            load_k_l1b(idx + 1, ("t", c.now("t")))
    attn_done = ("v", c.now("v"))
    wo2 = [sc.t(f"mwo{i}", [128, D], BF16) for i in range(2)]
    wo_prev = [("t", 0), ("t", 0)]
    for piece in range(4):
        n0 = piece * 512
        td = 0
        for kc in range(DCH):
            wb = kc % 2
            ld = c.dma("g", P(lambda g, o=wo2[wb], i=io["m_wo"], kc=kc:
                              g.dma_start(out=o[:],
                                          in_=i[kc * 128:(kc + 1) * 128, :])),
                       (wo_prev[wb],), stream=f"mwo{wb}")
            for oc in range(DCH):
                w = [attn_done, ld]
                if kc == 0:
                    w.append(c.pguard[oc])
                td = c.op("t", P(lambda t, pb=c.pb[oc], lh=wo2[wb],
                                 o0=oc * 128, r=attn[kc], n0=n0,
                                 st0=(kc == 0), sp=(kc == DCH - 1):
                                 t.matmul(pb[:, :], lh[:, o0:o0 + 128],
                                          r[:, n0:n0 + 512], start=st0,
                                          stop=sp)), w)
            wo_prev[wb] = ("t", td)
        for oc in range(DCH):
            ev = c.op("v", P(lambda v, o=sb["x"][oc], pb=c.pb[oc], n0=n0:
                             v.tensor_tensor(o[:, n0:n0 + 512], pb[:, :],
                                             o[:, n0:n0 + 512], ALU.add)),
                      (("t", td),))
            c.pguard[oc] = ("v", ev)
    fence(c)
    sc.close()


def emit_l1_attn_a(c, sbp, io):
    sc = Scope(c.nc)
    sb = dict(sbp)
    gdc = sc.t("agdc", [128, 8], F32)
    mixc = sc.t("amixc", [128, 8], F32)
    omixc = sc.t("aomixc", [128, 8], F32)
    selt = sc.t("aselt", [128, 64], F32)
    sel8 = sc.t("asel8", [128, 512], I32)
    bidxt = sc.t("abidxt", [128, 4], I32)
    stidxt = sc.t("astidxt", [128, 32], I32)
    bxn = sc.t("abxn", [128, 32], BF16)
    sb["p3t"] = sc.t("lap3t", [128, 512], F32)
    sb["tmpx"] = sc.t("latmpx", [128, 512], F32)
    xmsc = Scope(c.nc)
    xm = [xmsc.t(f"axm{i}", [128, TOK], BF16) for i in range(DCH)]
    xnsc = Scope(c.nc)
    sb["xn"] = [xnsc.t(f"laxn{i}", [128, TOK], BF16) for i in range(DCH)]
    lnp = Scope(c.nc)
    sb.update(alloc_ln_pack(lnp))
    ln_done = emit_ln(c, sb, "plain")
    ln_fence = (("v", c.now("v")), ("t", c.now("t")), ("s", c.now("s")))
    lnp.close()
    c.op("v", P(lambda v, o=sb["p3t"]: v.memset(o[:], 0.3)))
    gml = c.dma("y", P(lambda y, o=gdc, i=io["la_gd"]:
                       y.dma_start(out=o[:], in_=i)), ln_fence, stream="lac")
    c.dma("y", P(lambda y, o=mixc, i=io["la_mix"]:
                 y.dma_start(out=o[:], in_=i)), stream="lac")
    sl_l = c.dma("y", P(lambda y, o=selt, i=io["la_sel"]:
                        y.dma_start(out=o[:], in_=i)), stream="lac")
    sl8 = c.dma("y", P(lambda y, o=sel8, i=io["la_sel8"]:
                       y.dma_start(out=o[:], in_=i)), stream="lac")
    bxl = c.dma("y", P(lambda y, o=bidxt, i=io["la_bidx"]:
                       y.dma_start(out=o[:], in_=i)), stream="laidx")
    stl = c.dma("y", P(lambda y, o=stidxt, i=io["la_stidx"]:
                       y.dma_start(out=o[:], in_=i)), stream="laidx")
    gde = c.op("s", P(lambda s, o=gdc, i=gdc:
                      s.activation(o[:], i[:], AF.Exp)), (sl8,))
    omx = c.op("v", P(lambda v, o=omixc, i=mixc:
                      v.tensor_scalar(o[:], i[:], -1.0, 1.0, ALU.mult,
                                      ALU.add)), (sl8,))
    bst = None
    for j in range(4):
        for ch in range(DCH):
            bst = c.dma("y", P(lambda y, i=sb["xn"][ch], o=io["agbx_in"],
                               j=j, ch=ch:
                               y.dma_start(out=o.ap()[j * 128:(j + 1) * 128,
                                                      ch:ch + 1],
                                           in_=i[:, j * 512 + 511:j * 512 + 512],
                                           allow_slow_non_contiguous=True)),
                        (("v", ln_done),) if (j, ch) == (0, 0) else (),
                        stream="labx")
    agbx = c.collective("AllGather", ALU.bypass, io["agbx_in"],
                        io["agbx_out"], (bst,))
    bxm = c.op("v", P(lambda v, o=bxn: v.memset(o[:], 0.0)))
    bxg = None
    for j in range(4):
        bxg = c.dma("g", P(lambda g, o=bxn, i=io["agbx_out"], ii=bidxt,
                           j=j:
                           g.indirect_dma_start(
                               out=o[:, j * 8:(j + 1) * 8], out_offset=None,
                               in_=i.ap(),
                               in_offset=IndirectOffsetOnAxis(
                                   ap=ii[:, j:j + 1], axis=0),
                               bounds_check=4095, oob_is_err=False)),
                    [agbx, ("v", bxm), stl], stream="labxg")
    xm_done = 0
    for ch in range(DCH):
        for j in range(4):
            t1 = c.op("v", P(lambda v, o=sb["tmpx"], i=sb["xn"][ch], mx=mixc,
                             ch=ch, j=j:
                             v.tensor_scalar(o[:, 0:512],
                                             i[:, j * 512:(j + 1) * 512],
                                             mx[:, ch:ch + 1], None,
                                             ALU.mult)),
                      [("v", ln_done), ("v", omx), ("v", xm_done)])
            x2 = c.op("v", P(lambda v, o=xm[ch], i=sb["xn"][ch], om=omixc,
                             t=sb["tmpx"], ch=ch, j=j:
                             v.scalar_tensor_tensor(
                                 o[:, j * 512 + 1:(j + 1) * 512],
                                 i[:, j * 512:(j + 1) * 512 - 1],
                                 om[:, ch:ch + 1], t[:, 1:512],
                                 ALU.mult, ALU.add)), (("v", t1),))
            xm_done = c.op("v", P(lambda v, o=xm[ch], bx=bxn, om=omixc,
                                  t=sb["tmpx"], ch=ch, j=j:
                                  v.scalar_tensor_tensor(
                                      o[:, j * 512:j * 512 + 1],
                                      bx[:, j * 8 + ch:j * 8 + ch + 1],
                                      om[:, ch:ch + 1], t[:, 0:1],
                                      ALU.mult, ALU.add)),
                           (("v", x2), bxg))
    xm_fence = (("v", c.now("v")),)
    xnsc.close()
    prsc = Scope(c.nc)
    wres = [prsc.t(f"law{i}", [128, D], BF16) for i in range(DCH)]
    ev1 = prsc.t("laev1", [128, 512], F32)
    mats = [("la_wtdT", "scr_z", "z"), ("la_wkT", "scr_ek", "ek"),
            ("la_wvT", "scr_v", "v"), ("la_wrT", "scr_r", "r")]
    bki = [0]

    def bank():
        k = bki[0] % 8
        bki[0] += 1
        return k

    st_prev = ("v", 0)
    for wkey, skey, kind in mats:
        lds = []
        for kc in range(DCH):
            lds.append(c.dma("g", P(lambda g, o=wres[kc], i=io[wkey], kc=kc:
                             g.dma_start(out=o[:],
                                         in_=i[kc * 128:(kc + 1) * 128, :])),
                             ((("t", c.now("t")),) + xm_fence) if kc == 0
                             else (), stream="law"))
        ld = lds[-1]
        sprev = ("s", 0)
        for oc in range(DCH):
            for piece in range(4):
                n0 = piece * 512
                k = bank()
                td = 0
                for kc in range(DCH):
                    w = [ld]
                    if kc == 0:
                        w.append(c.pguard[k])
                    td = c.op("t", P(lambda t, pb=c.pb[k], lh=wres[kc],
                                     o0=oc * 128, r=xm[kc], n0=n0,
                                     st0=(kc == 0), sp=(kc == DCH - 1):
                                     t.matmul(pb[:, :], lh[:, o0:o0 + 128],
                                              r[:, n0:n0 + 512], start=st0,
                                              stop=sp)), w)
                if kind == "r":
                    ev = c.op("s", P(lambda s, o=ev1, pb=c.pb[k]:
                                     s.activation(o[:, 0:512], pb[:, :],
                                                  AF.Sigmoid)),
                              [("t", td), st_prev, sprev])
                elif kind == "ek":
                    evv = c.op("v", P(lambda v, o=ev1, pb=c.pb[k]:
                                      v.tensor_scalar(o[:, 0:512], pb[:, :],
                                                      10.0, None, ALU.min)),
                               [("t", td), st_prev])
                    ev = c.op("s", P(lambda s, o=ev1, i=ev1:
                                     s.activation(o[:, 0:512], i[:, 0:512],
                                                  AF.Exp)),
                              [("v", evv), sprev])
                else:
                    ev = c.op("s", P(lambda s, o=ev1, pb=c.pb[k]:
                                     s.activation(o[:, 0:512], pb[:, :],
                                                  AF.Identity)),
                              [("t", td), st_prev, sprev])
                sprev = ("s", ev)
                c.pguard[k] = ("s", ev)
                st_prev = c.dma("y", P(lambda y, i=ev1, o=io[skey], oc=oc,
                                       n0=n0:
                                       y.dma_start(out=o.ap()[oc * 128:
                                                              (oc + 1) * 128,
                                                              n0:n0 + 512],
                                                   in_=i[:, 0:512])),
                                (("s", ev),), stream="lasc")
    proj_fence = (("v", c.now("v")), ("t", c.now("t")), ("s", c.now("s")),
                  st_prev)
    prsc.close()
    xmsc.close()
    a2 = Scope(c.nc)
    ztb = [a2.t(f"lazt{i}", [128, 512], F32) for i in range(2)]
    ektb = [a2.t(f"laekt{i}", [128, 512], F32) for i in range(2)]
    vtb = [a2.t(f"lavt{i}", [128, 512], F32) for i in range(2)]
    ekvtb = [a2.t(f"laekvt{i}", [128, 512], F32) for i in range(2)]
    ctb = [a2.t(f"lact{i}", [128, 512], F32) for i in range(2)]
    tdtb = [a2.t(f"latdt{i}", [128, 512], F32) for i in range(2)]
    atb = [a2.t(f"laat{i}", [128, 512], F32) for i in range(2)]
    btb = [a2.t(f"labt{i}", [128, 512], F32) for i in range(2)]
    onesg = a2.t("laones", [128, 512], F32)
    sums = a2.t("lasums", [128, 96], F32)
    c.op("v", P(lambda v, o=onesg: v.memset(o[:], 1.0)), proj_fence)
    rdv = [("v", 0), ("v", 0)]
    rds = [("s", 0), ("s", 0)]
    ldq = {}

    def a2_load(it):
        j, oc = divmod(it, DCH)
        bf = it % 2
        w0 = [rdv[bf], rds[bf]] + (list(proj_fence) if it < 2 else [])
        l1 = c.dma("g", P(lambda g, o=ztb[bf], i=io["scr_z"], oc=oc, j=j:
                          g.dma_start(out=o[:],
                                      in_=i.ap()[oc * 128:(oc + 1) * 128,
                                                 j * 512:(j + 1) * 512])),
                   w0, stream=f"la2z{bf}")
        l2 = c.dma("g", P(lambda g, o=ektb[bf], i=io["scr_ek"], oc=oc, j=j:
                          g.dma_start(out=o[:],
                                      in_=i.ap()[oc * 128:(oc + 1) * 128,
                                                 j * 512:(j + 1) * 512])),
                   w0, stream=f"la2e{bf}")
        l3 = c.dma("g", P(lambda g, o=vtb[bf], i=io["scr_v"], oc=oc, j=j:
                          g.dma_start(out=o[:],
                                      in_=i.ap()[oc * 128:(oc + 1) * 128,
                                                 j * 512:(j + 1) * 512])),
                   w0, stream=f"la2v{bf}")
        ldq[it] = (l1, l2, l3)

    a2_load(0)
    gb1 = [("v", 0), ("v", 0)]
    gs1 = [("v", 0), ("v", 0)]
    gs2 = [("v", 0), ("v", 0)]
    gs3 = [("s", 0), ("s", 0)]
    for it in range(32):
        j, oc = divmod(it, DCH)
        bf = it % 2
        zt, ekt, vt = ztb[bf], ektb[bf], vtb[bf]
        ekvt, ct, tdt, at, bt = (ekvtb[bf], ctb[bf], tdtb[bf], atb[bf],
                                 btb[bf])
        l1, l2, l3 = ldq.pop(it)
        if it + 1 < 32:
            a2_load(it + 1)
        td1 = c.op("s", P(lambda s, o=tdt, i=zt:
                          s.activation(o[:], i[:], AF.Exp)),
                   [l1, gb1[bf]] + ([("s", c.now("s"))] if it < 2 else []))
        ekv1 = c.op("v", P(lambda v, o=ekvt, a=ekt, b=vt:
                           v.tensor_tensor(o[:], a[:], b[:], ALU.mult)),
                    [l2, l3, gb1[bf]] + ([("v", c.now("v"))] if it < 2
                                         else []))
        c1 = c.op("v", P(lambda v, o=ct, d0=onesg, d1=zt:
                         v.tensor_tensor_scan(o[:], d0[:], d1[:], 0.0,
                                              ALU.mult, ALU.add)),
                  [l1, gs3[bf]])
        a1 = c.op("v", P(lambda v, o=at, d0=tdt, d1=ekvt:
                         v.tensor_tensor_scan(o[:], d0[:], d1[:], 0.0,
                                              ALU.mult, ALU.add)),
                  [("s", td1), ("v", ekv1), gs1[bf]])
        b1 = c.op("v", P(lambda v, o=bt, d0=tdt, d1=ekt:
                         v.tensor_tensor_scan(o[:], d0[:], d1[:], 0.0,
                                              ALU.mult, ALU.add)),
                  [("s", td1), l2, gs2[bf]])
        rdv[bf] = ("v", b1)
        rds[bf] = ("s", td1)
        if True:
            s1 = c.op("v", P(lambda v, o=sums, i=at, col=j * 8 + oc:
                             v.tensor_scalar(o[:, col:col + 1],
                                             i[:, 511:512], 1.0, None,
                                             ALU.mult)), (("v", a1),))
            s2 = c.op("v", P(lambda v, o=sums, i=bt, col=32 + j * 8 + oc:
                             v.tensor_scalar(o[:, col:col + 1],
                                             i[:, 511:512], 1.0, None,
                                             ALU.mult)), (("v", b1),))
            s3 = c.op("s", P(lambda s, o=sums, i=ct, col=64 + j * 8 + oc:
                             s.activation(o[:, col:col + 1], i[:, 511:512],
                                          AF.Exp)), [("v", c1)])
            gb1[bf] = ("v", b1)
            gs1[bf] = ("v", s1)
            gs2[bf] = ("v", s2)
            gs3[bf] = ("s", s3)
    sum_done = [("v", c.now("v")), ("s", c.now("s"))]
    sst = c.dma("y", P(lambda y, i=sums, o=io["agsum_in"]:
                       y.dma_start(out=o.ap(), in_=i[:])),
                sum_done, stream="lasum")
    agsum = c.collective("AllGather", ALU.bypass, io["agsum_in"],
                         io["agsum_out"], (sst,))
    sumsb = a2.t("lasumsb", [128, 8 * 96], F32)
    acar = a2.t("laacar", [128, 8], F32)
    bcar = a2.t("labcar", [128, 8], F32)
    mya = a2.t("lamya", [128, 64], F32)
    lsum = c.dma("y", P(lambda y, o=sumsb, i=io["agsum_out"]:
                        y.dma_start(out=o[:].rearrange("p (r q) -> p r q",
                                                       r=8),
                                    in_=i.ap().rearrange("(r p) q -> p r q",
                                                         p=128))),
                 (agsum,), stream="lasum")
    vp = c.op("v", P(lambda v, o=mya: v.memset(o[:], 0.0)), (lsum,))
    for b in range(B):
        vp = c.op("v", P(lambda v, o=acar: v.memset(o[:], 0.0)), (("v", vp),))
        vp = c.op("v", P(lambda v, o=bcar: v.memset(o[:], 0.0)), (("v", vp),))
        for sg in range(NSEG):
            own, slot = owner_of(b, sg)
            base = own * 96 + slot * 8
            for j in (2 * b, 2 * b + 1):
                vp = c.op("v", P(lambda v, o=mya, sel=sel8, a=acar, sg=sg,
                                 j=j:
                                 v.copy_predicated(
                                     o[:, j * 8:(j + 1) * 8],
                                     sel[:, (sg * 4 + j) * 8:
                                         (sg * 4 + j) * 8 + 8],
                                     a[:, 0:8])),
                          [("v", vp), sl8])
                vp = c.op("v", P(lambda v, o=mya, sel=sel8, a=bcar, sg=sg,
                                 j=j:
                                 v.copy_predicated(
                                     o[:, 32 + j * 8:32 + (j + 1) * 8],
                                     sel[:, (sg * 4 + j) * 8:
                                         (sg * 4 + j) * 8 + 8],
                                     a[:, 0:8])), (("v", vp),))
            vp = c.op("v", P(lambda v, o=acar, d=sumsb, base=base,
                             own=own:
                             v.tensor_tensor(o[:, 0:8],
                                             d[:, own * 96 + 64 + (base - own * 96):
                                               own * 96 + 64 + (base - own * 96) + 8],
                                             o[:, 0:8], ALU.mult)),
                      (("v", vp),))
            vp = c.op("v", P(lambda v, o=acar, d=sumsb, base=base:
                             v.tensor_tensor(o[:, 0:8], d[:, base:base + 8],
                                             o[:, 0:8], ALU.add)),
                      (("v", vp),))
            vp = c.op("v", P(lambda v, o=bcar, d=sumsb, base=base,
                             own=own:
                             v.tensor_tensor(o[:, 0:8],
                                             d[:, own * 96 + 64 + (base - own * 96):
                                               own * 96 + 64 + (base - own * 96) + 8],
                                             o[:, 0:8], ALU.mult)),
                      (("v", vp),))
            vp = c.op("v", P(lambda v, o=bcar, d=sumsb, base=base:
                             v.tensor_tensor(o[:, 0:8],
                                             d[:, base - slot * 8 + 32 + slot * 8:
                                               base - slot * 8 + 32 + slot * 8 + 8],
                                             o[:, 0:8], ALU.add)),
                      (("v", vp),))
    carry_done = ("v", c.now("v"))
    rt_b = [a2.t(f"lart{i}", [128, 512], F32) for i in range(2)]
    xintb = [a2.t(f"laxin{i}", [128, 512], F32) for i in range(2)]
    numtb = [a2.t(f"lanum{i}", [128, 512], F32) for i in range(2)]
    strips = a2.t("lastrips", [128, 32 * 65], F32)
    xsth = [("sd_laxs0", 0), ("sd_laxs1", 0)]
    rdv = [carry_done, carry_done]
    rds = [("s", c.now("s")), ("s", c.now("s"))]
    ldq = {}

    def a3_load(it):
        j, oc = divmod(it, DCH)
        bf = it % 2
        w0 = [rdv[bf], rds[bf]] + ([carry_done] if it < 2 else [])
        l1 = c.dma("g", P(lambda g, o=ztb[bf], i=io["scr_z"], oc=oc, j=j:
                          g.dma_start(out=o[:],
                                      in_=i.ap()[oc * 128:(oc + 1) * 128,
                                                 j * 512:(j + 1) * 512])),
                   w0, stream=f"la2z{bf}")
        l2 = c.dma("g", P(lambda g, o=ektb[bf], i=io["scr_ek"], oc=oc, j=j:
                          g.dma_start(out=o[:],
                                      in_=i.ap()[oc * 128:(oc + 1) * 128,
                                                 j * 512:(j + 1) * 512])),
                   w0, stream=f"la2e{bf}")
        l3 = c.dma("g", P(lambda g, o=vtb[bf], i=io["scr_v"], oc=oc, j=j:
                          g.dma_start(out=o[:],
                                      in_=i.ap()[oc * 128:(oc + 1) * 128,
                                                 j * 512:(j + 1) * 512])),
                   w0, stream=f"la2v{bf}")
        l4 = c.dma("g", P(lambda g, o=rt_b[bf], i=io["scr_r"], oc=oc, j=j:
                          g.dma_start(out=o[:],
                                      in_=i.ap()[oc * 128:(oc + 1) * 128,
                                                 j * 512:(j + 1) * 512])),
                   w0, stream=f"la2r{bf}")
        ldq[it] = (l1, l2, l3, l4)

    a3_load(0)
    gb1 = [("v", 0), ("v", 0)]
    gnm = [("v", 0), ("v", 0)]
    gb2c = [("v", 0), ("v", 0)]
    gce = [("s", 0), ("s", 0)]
    gwk = [("v", 0), ("v", 0)]
    gxi = [("v", 0), ("v", 0)]
    glp = [("v", 0), ("v", 0)]
    gst = [("v", 0), ("v", 0)]
    for it in range(32):
        j, oc = divmod(it, DCH)
        jo = j * 8 + oc
        bf = it % 2
        zt, ekt, vt, rt_ = ztb[bf], ektb[bf], vtb[bf], rt_b[bf]
        ekvt, ct, tdt, at, bt = (ekvtb[bf], ctb[bf], tdtb[bf], atb[bf],
                                 btb[bf])
        xint, numt = xintb[bf], numtb[bf]
        l1, l2, l3, l4 = ldq.pop(it)
        if it + 1 < 32:
            a3_load(it + 1)
        if True:
            td1 = c.op("s", P(lambda s, o=tdt, i=zt:
                              s.activation(o[:], i[:], AF.Exp)),
                       [l1, gb1[bf]] + ([("s", c.now("s"))] if it < 2
                                        else []))
            ekv1 = c.op("v", P(lambda v, o=ekvt, a=ekt, b=vt:
                               v.tensor_tensor(o[:], a[:], b[:], ALU.mult)),
                        [l2, l3, gnm[bf]] + ([("v", c.now("v"))] if it < 2
                                             else []))
            c1 = c.op("v", P(lambda v, o=ct, d0=onesg, d1=zt:
                             v.tensor_tensor_scan(o[:], d0[:], d1[:], 0.0,
                                                  ALU.mult, ALU.add)),
                      [l1, gb2c[bf], gce[bf]])
            a1 = c.op("v", P(lambda v, o=at, d0=tdt, d1=ekvt:
                             v.tensor_tensor_scan(o[:], d0[:], d1[:], 0.0,
                                                  ALU.mult, ALU.add)),
                      [("s", td1), ("v", ekv1), gnm[bf]])
            b1 = c.op("v", P(lambda v, o=bt, d0=tdt, d1=ekt:
                             v.tensor_tensor_scan(o[:], d0[:], d1[:], 0.0,
                                                  ALU.mult, ALU.add)),
                      [("s", td1), l2, gwk[bf]])
            ce = c.op("s", P(lambda s, o=ct, i=ct:
                             s.activation(o[:], i[:], AF.Exp)),
                      [("v", c1)])
            a2c = c.op("v", P(lambda v, o=at, ec=ct, my=mya, jo=jo:
                              v.scalar_tensor_tensor(o[:], ec[:],
                                                     my[:, jo:jo + 1],
                                                     o[:], ALU.mult,
                                                     ALU.add)),
                       [("s", ce), ("v", a1)])
            b2c = c.op("v", P(lambda v, o=bt, ec=ct, my=mya, jo=jo:
                              v.scalar_tensor_tensor(o[:], ec[:],
                                                     my[:, 32 + jo:33 + jo],
                                                     o[:], ALU.mult,
                                                     ALU.add)),
                       [("s", ce), ("v", b1)])
            b3 = c.op("v", P(lambda v, o=bt:
                             v.tensor_scalar(o[:], o[:], 1e-6, None,
                                             ALU.add)), (("v", b2c),))
            nm = c.op("v", P(lambda v, o=numt, ekv=ekvt, gd=gdc, a=at,
                             oc=oc:
                             v.scalar_tensor_tensor(o[:], ekv[:],
                                                    gd[:, oc:oc + 1], a[:],
                                                    ALU.mult, ALU.add)),
                      [("v", b3), ("s", gde), gxi[bf]])
            dn = c.op("v", P(lambda v, o=bt, ek=ekt, gd=gdc, oc=oc:
                             v.scalar_tensor_tensor(o[:], ek[:],
                                                    gd[:, oc:oc + 1], o[:],
                                                    ALU.mult, ALU.add)),
                      (("v", nm),))
            rq = c.op("v", P(lambda v, o=bt: v.reciprocal(o[:], o[:])),
                      (("v", dn),))
            wk = c.op("v", P(lambda v, o=numt, b=bt:
                             v.tensor_tensor(o[:], o[:], b[:], ALU.mult)),
                      (("v", rq),))
            xi = c.op("v", P(lambda v, o=xint, r=rt_, w=numt:
                             v.tensor_tensor(o[:], r[:], w[:], ALU.mult)),
                      [("v", wk), l4, glp[bf], xsth[bf]])
            lp = c.op("v", P(lambda v, o=at, d0=sb["p3t"], d1=xint:
                             v.tensor_tensor_scan(o[:], d0[:], d1[:], 0.0,
                                                  ALU.mult, ALU.add)),
                      [("v", xi), gst[bf]])
            stx = c.op("v", P(lambda v, o=strips, i=at, jo=jo:
                              v.tensor_scalar(o[:, jo * 65:(jo + 1) * 65],
                                              i[:, 447:512], 1.0, None,
                                              ALU.mult)), (("v", lp),))
            xsth[bf] = c.dma("y", P(lambda y, i=xint, o=io["scr_xin"],
                                    oc=oc, j=j:
                                    y.dma_start(out=o.ap()[oc * 128:
                                                           (oc + 1) * 128,
                                                           j * 512:
                                                           (j + 1) * 512],
                                                in_=i[:])),
                             [("v", xi)], stream=f"laxs{bf}")
            gb1[bf] = ("v", b1)
            gnm[bf] = ("v", nm)
            gb2c[bf] = ("v", b2c)
            gce[bf] = ("s", ce)
            gwk[bf] = ("v", wk)
            gxi[bf] = ("v", xi)
            glp[bf] = ("v", lp)
            gst[bf] = ("v", stx)
            rdv[bf] = ("v", xi)
            rds[bf] = ("s", td1)
    a3_done = (("v", c.now("v")), ("s", c.now("s")), xsth[0], xsth[1])
    sts = c.dma("y", P(lambda y, i=strips, o=io["agst_in"]:
                       y.dma_start(out=o.ap().rearrange("(so p) cc -> p so cc",
                                                        p=128),
                                   in_=i[:].rearrange("p (so cc) -> p so cc",
                                                      so=32))),
                list(a3_done), stream="last")
    agst = c.collective("AllGather", ALU.bypass, io["agst_in"],
                        io["agst_out"], (sts,))
    a2.close()
    a4 = Scope(c.nc)
    wst = a4.t("lawst", [128, 32 * 65], F32)
    xint2 = a4.t("laxi2", [128, 512], F32)
    l2t = a4.t("lal2", [128, 512], F32)
    chz = a4.t("lachz", [128, 32 * 576], F32)
    spkT = a4.t("laspkT", [128, (576 - WARM) * 32], BF16)
    swarm = a4.t("laswarm", [128, 32], BF16)
    Sst = a4.t("laS", [128, 32], F32)
    mz = c.op("v", P(lambda v, o=wst: v.memset(o[:], 0.0)), list(a3_done))
    stg = None
    for jo in range(32):
        stg = c.dma("g", P(lambda g, o=wst, i=io["agst_out"],
                           ii=stidxt, jo=jo:
                           g.indirect_dma_start(
                               out=o[:, jo * 65:(jo + 1) * 65],
                               out_offset=None,
                               in_=i.ap(),
                               in_offset=IndirectOffsetOnAxis(
                                   ap=ii[:, jo:jo + 1], axis=0),
                               bounds_check=32767, oob_is_err=False)),
                    [agst, ("v", mz), stl], stream="lastg")
    ldx = ("v", 0)
    chains = [(j, oc) for j in range(4) for oc in range(DCH)]
    for ci, (j, oc) in enumerate(chains):
        jo = j * 8 + oc
        lx = c.dma("g", P(lambda g, o=xint2, i=io["scr_xin"], oc=oc,
                          j=j:
                          g.dma_start(out=o[:],
                                      in_=i.ap()[oc * 128:(oc + 1) * 128,
                                                 j * 512:(j + 1) * 512])),
                   [ldx, xsth[0], xsth[1]], stream="lax2")
        l2s = c.op("v", P(lambda v, o=l2t, d0=sb["p3t"], d1=xint2,
                          ini=wst, jo=jo:
                          v.tensor_tensor_scan(o[:], d0[:], d1[:],
                                               ini[:, jo * 65 + 64:
                                                   jo * 65 + 65],
                                               ALU.mult, ALU.add)),
                   [lx, stg, ("v", c.now("v"))])
        zm = c.op("v", P(lambda v, o=chz, i=l2t, ci=ci:
                         v.tensor_scalar(o[:, ci * 576 + 64:
                                           ci * 576 + 576],
                                         i[:], 5.0, -1.0, ALU.mult,
                                         ALU.add)), (("v", l2s),))
        zw = c.op("v", P(lambda v, o=chz, i=wst, jo=jo, ci=ci:
                         v.tensor_scalar(o[:, ci * 576:ci * 576 + 64],
                                         i[:, jo * 65:jo * 65 + 64],
                                         5.0, -1.0, ALU.mult, ALU.add)),
                  (("v", zm),))
        ldx = ("v", zw)
    sz = c.op("v", P(lambda v, o=Sst: v.memset(o[:], 0.0)), (ldx,))
    vp = ("v", sz)
    for t in range(576):
        dst = (spkT, (t - WARM) * 32) if t >= WARM else (swarm, 0)
        s1 = c.op("v", P(lambda v, o=dst[0], c0=dst[1], z=chz, S=Sst, t=t:
                         v.tensor_tensor(o[:, c0:c0 + 32],
                                         S[:],
                                         z[:, t:32 * 576:576],
                                         ALU.is_lt)), (vp,))
        vp = ("v", c.op("v", P(lambda v, o=Sst, sp_=dst[0], c0=dst[1]:
                               v.scalar_tensor_tensor(
                                   o[:], o[:], 0.3,
                                   sp_[:, c0:c0 + 32],
                                   ALU.mult, ALU.add)), (("v", s1),)))
    for ci, (j, oc) in enumerate(chains):
        vp = ("v", c.op("v", P(lambda v, o=sb["x"][oc], i=spkT, j=j,
                               ci=ci:
                               v.tensor_tensor(o[:, j * 512:(j + 1) * 512],
                                               i[:, ci:
                                                 (576 - WARM) * 32:32],
                                               o[:, j * 512:(j + 1) * 512],
                                               ALU.add)), (vp,)))
    fence(c)
    a4.close()
    sc.close()
